# version 59
# baseline (speedup 1.0000x reference)
import contextlib
import numpy as np
import ml_dtypes

import concourse.bass as bass
import concourse.mybir as mybir
from concourse.bass_utils import run_bass_kernel_spmd

F32 = mybir.dt.float32
BF16 = mybir.dt.bfloat16
U8 = mybir.dt.uint8
AF = mybir.ActivationFunctionType
ALU = mybir.AluOpType
AXX = mybir.AxisListType.X

B, N, D, H, DH = 32, 2048, 1024, 16, 64
NCORES = 8
NSEQ = B // NCORES
NT = N // 128
NBLK = N // 256
NB4 = N // 512
NCH = D // 128
EPS = 1e-5
SCALE = DH ** -0.5

bfarr = lambda a: np.ascontiguousarray(np.asarray(a, dtype=ml_dtypes.bfloat16))


def _consts():
    h = np.arange(1, H + 1, dtype=np.float64)
    slopes = 1.0 / (2.0 ** 8) ** (h / H)
    j = np.arange(N, dtype=np.float64)
    u = np.exp(-slopes[None, :] * j[:, None])
    u_tab = np.zeros((128, NT * H), dtype=np.float32)
    for t in range(NT):
        u_tab[:, t * H:(t + 1) * H] = u[t * 128:(t + 1) * 128, :]
    mask = (np.arange(128)[None, :] >= np.arange(128)[:, None]).astype(np.float32)
    ident = np.eye(128, dtype=np.float32)
    sel = np.zeros((512, 256), dtype=np.float32)
    for lev in range(8):
        n = 1 << lev
        span = 256 // (2 * n)
        for i in range(n):
            r = n - 1 + i
            lo = i * 2 * span
            sel[r, lo:lo + span] = -1.0
            sel[256 + r, lo + span:lo + 2 * span] = -1.0
    sel_pk = np.zeros((128, 4 * 256), dtype=np.float32)
    for c in range(4):
        sel_pk[:, 256 * c:256 * (c + 1)] = sel[128 * c:128 * (c + 1), :]
    ones2 = np.ones((2, 128), dtype=np.float32)
    epsone = np.zeros((128, 2), dtype=np.float32)
    epsone[:, 0] = EPS
    epsone[:, 1] = 1.0
    return dict(u=u_tab, mask=mask, ident=ident, sel=sel_pk, ones2=ones2, epsone=epsone)


class Arena:
    def __init__(self, nc, ctx, name, size_bytes):
        self.nc = nc
        self.h = ctx.enter_context(nc.sbuf_tensor(name, [128, size_bytes], U8))
        self.base = nc.lookup_mloc(self.h).addr
        self.size = size_bytes

    def t(self, name, off, shape, dtype):
        per_part = int(np.prod(shape[1:])) * mybir.dt.size(dtype)
        assert off + per_part <= self.size, f"{name}: {off}+{per_part} > {self.size}"
        hnd = self.nc.virtual_sbuf_tensor(name, list(shape), dtype)
        self.nc._tensor(hnd.name, hnd.shape, hnd.dtype, type="SB")
        mloc = self.nc.lookup_mloc(hnd)
        mloc.allocated = not self.nc.target_bir_lowering
        mloc.addr = self.base + off
        return hnd


class PsumViews:
    def __init__(self, nc, ctx):
        self.nc = nc
        self.h = ctx.enter_context(nc.psum_tensor("psb", [128, 4096], F32))
        self.base = nc.lookup_mloc(self.h).addr

    def t(self, name, byte_off, shape, dtype):
        hnd = self.nc.virtual_psum_tensor(name, list(shape), dtype)
        self.nc._tensor(hnd.name, hnd.shape, hnd.dtype, type="PSUM")
        mloc = self.nc.lookup_mloc(hnd)
        mloc.allocated = not self.nc.target_bir_lowering
        mloc.addr = self.base + byte_off
        return hnd


class Sem:
    __slots__ = ("h", "v", "name")

    def __init__(self, h, name=""):
        self.h = h
        self.v = 0
        self.name = name


PRUNE_SAME = [True]
SAME_ENG_SEMS = {
    "vector": {"cG1", "cG2", "cG3", "cG4", "cV2", "cV4", "cV6", "cV8",
               "cX1", "cOF", "cNRM", "cQBE", "cVSt", "cVC"},
    "scalar": {"cS1", "cS2", "cS3", "cS4", "cS5", "cS6", "cS7", "cS8",
               "cE", "cLN", "cXN2", "cHL", "cSC"},
    "tensor": {"cGP", "cS", "cAV"},
    "gpsimd": {"cM"},
}


SEM_NAMES = [
    "dXA0", "dXA1", "dXA2", "dXA3", "dXB0", "dXB1", "dXB2", "dXB3", "dW", "dW2", "dW3", "dC", "dY0", "dY1",
    "cG1", "cS1", "cV2", "cS2", "cLN",
    "cGP", "cGE0", "cGE1",
    "cS", "cE", "cM", "cAV", "cNRM",
    "cG2", "cS3", "cV4", "cS4", "cX1",
    "cG3", "cS5", "cV6", "cS6", "cXN2",
    "cTPf", "cTEf",
    "cZL", "cHL",
    "cG4", "cS7", "cV8", "cS8", "cOF",
    "cVC", "cSC", "cGC", "cQBE", "cTPo", "cTEo", "cVSt", "cVG", "cRCP", "cTEfV", "cTPg",
    "dTo0", "dTo1", "dTf0", "dTf1", "dTa0", "dTa1", "dTa2", "dTa3",
    "dWk", "dWv", "dWq",
    "sMT", "sMV", "sMS", "sInit",
]

LAYOUT = {}
_off = 0


def _lay(name, nbytes, align=32):
    global _off
    _off = (_off + align - 1) // align * align
    LAYOUT[name] = _off
    _off += nbytes


_lay("OT", 32768)
_lay("XIN", 8192)
_lay("XN", 8192)
_lay("UC", 1024)
_lay("MSK", 256)
_lay("IDN", 256)
_lay("SEL", 4096)
_lay("EPS1", 32)
_lay("BOHL", 2048)
_lay("WOBHL", 2048)
_lay("ONES2", 256)
_lay("VTMP", 2048)
_lay("LNT", 512)
_lay("ONAT", 2048)
_lay("PTM", 2048)
_lay("RCP", 64)
_UB = (_off + 31) // 32 * 32
LAYOUT["A_W0"] = _UB + 0
LAYOUT["A_W1"] = _UB + 16384
LAYOUT["A_W2"] = _UB + 32768
LAYOUT["XNT"] = _UB + 49152
LAYOUT["KG"] = _UB + 81920
LAYOUT["VG"] = _UB + 98304
LAYOUT["QB"] = _UB + 115200
LAYOUT["PT"] = _UB + 123392
_BB = _UB + 32768
_bc = _BB + 16384
def _layB(name, nbytes):
    global _bc
    LAYOUT[name] = _bc
    _bc += nbytes
_layB("RTRT", 4096)
_layB("LEAF", 4096)
_layB("XN2T", 16384)
_layB("QT", 8192)
_layB("SPE", 4096)
_layB("SPL", 8192)
_layB("LPT", 2048)
_layB("RETT", 8192)
_layB("OUT2", 8192)
_layB("OUTF", 8192)
_layB("OATT", 8192)
_layB("X1", 16384)
LAYOUT["XINA"] = LAYOUT["XIN"]
ARENA_SZ = max(LAYOUT["X1"] + 16384, LAYOUT["PT"] + 8192)
assert ARENA_SZ <= 212800, ARENA_SZ


def build_nc(nseq=NSEQ):
    nc = bass.Bass("TRN2", target_bir_lowering=False)
    C = _consts()

    dram = {}
    dram["x"] = nc.dram_tensor("x", [nseq, N, D], F32, kind="ExternalInput").ap()
    for nm, shp in (("wq", [D, D]), ("wk", [D, D]), ("wv", [D, D]), ("wo", [D, D]),
                    ("bo", [1, D]), ("wqfT", [D, D]), ("rtrT", [D, 256]),
                    ("leaf", [256, D]), ("woutT", [D, D]), ("wob", [1, D])):
        dram[nm] = nc.dram_tensor(nm, shp, F32, kind="ExternalInput").ap()
    dram["y"] = nc.dram_tensor("y", [nseq, N, D], F32, kind="ExternalOutput").ap()
    dram["u_c"] = nc.inline_tensor(C["u"], "u_c").ap()
    dram["mask_c"] = nc.inline_tensor(bfarr(C["mask"]), "mask_c").ap()
    dram["id_c"] = nc.inline_tensor(bfarr(C["ident"]), "id_c").ap()
    dram["sel_c"] = nc.inline_tensor(C["sel"], "sel_c").ap()
    dram["ones2_c"] = nc.inline_tensor(bfarr(C["ones2"]), "ones2_c").ap()
    dram["epsone_c"] = nc.inline_tensor(C["epsone"], "epsone_c").ap()

    ctx = contextlib.ExitStack()
    with ctx:
        ar = Arena(nc, ctx, "arena", ARENA_SZ)
        L = LAYOUT
        T = {}
        T["OT"] = ar.t("OT", L["OT"], [128, NCH * N], BF16)
        
        T["XN"] = ar.t("XN", L["XN"], [128, 4, 1024], BF16)
        T["UC"] = ar.t("UC", L["UC"], [128, 256], F32)
        T["MSK"] = ar.t("MSK", L["MSK"], [128, 128], BF16)
        T["IDN"] = ar.t("IDN", L["IDN"], [128, 128], BF16)
        T["SEL"] = ar.t("SEL", L["SEL"], [128, 1024], F32)
        T["BSBO"] = ar.t("BSBO", L["XIN"], [1, 1024], F32)
        T["BSWOB"] = ar.t("BSWOB", L["XIN"] + 4096, [1, 1024], F32)
        T["LOT"] = ar.t("LOT", L["XN"], [1, 2048], BF16)
        T["SQSB"] = T["LOT"]
        T["EPS1"] = ar.t("EPS1", L["EPS1"], [128, 2], F32)
        T["BOHL"] = ar.t("BOHL", L["BOHL"], [2, 1024], BF16)
        T["WOBHL"] = ar.t("WOBHL", L["WOBHL"], [2, 1024], BF16)
        T["ONES2"] = ar.t("ONES2", L["ONES2"], [2, 128], BF16)

        T["VTMP"] = ar.t("VTMP", L["VTMP"], [128, 1024], BF16)
        T["LNT"] = ar.t("LNT", L["LNT"], [128, 128], F32)
        T["XINA"] = ar.t("XINA", L["XINA"], [128, 4, 1024], BF16)
        T["ONAT"] = ar.t("ONAT", L["ONAT"], [128, 8, 128], BF16)
        T["PTM"] = ar.t("PTM", L["PTM"], [128, 4, 256], BF16)
        T["RCP"] = ar.t("RCP", L["RCP"], [128, 16], F32)
        T["A_W0"] = ar.t("A_W0", L["A_W0"], [128, NCH * 1024], BF16)
        T["A_W1"] = ar.t("A_W1", L["A_W1"], [128, NCH * 1024], BF16)
        T["A_W2"] = ar.t("A_W2", L["A_W2"], [128, NCH * 1024], BF16)
        T["XNT"] = ar.t("XNT", L["XNT"], [128, NCH * N], BF16)
        T["KG"] = ar.t("KG", L["KG"], [128, 4 * N], BF16)
        T["VG"] = ar.t("VG", L["VG"], [128, NT * 528], BF16)
        T["QB"] = ar.t("QB", L["QB"], [128, 2, 2048], BF16)
        T["PT"] = ar.t("PT", L["PT"], [128, 4, 1024], BF16)
        T["SQD"] = ar.t("SQD", L["ONAT"], [128, 1024], F32)
        T["WOUTT"] = ar.t("WOUTT", L["A_W0"], [128, NCH * 1024], BF16)
        T["WO_SB"] = ar.t("WO_SB", L["A_W2"], [128, NCH * 1024], BF16)
        T["RTRT"] = ar.t("RTRT", L["RTRT"], [128, NCH * 256], BF16)
        T["LEAF"] = ar.t("LEAF", L["LEAF"], [128, 2 * 1024], BF16)
        T["XN2T"] = ar.t("XN2T", L["XN2T"], [128, 2, NCH * 512], BF16)
        T["QT"] = ar.t("QT", L["QT"], [128, NCH * 512], BF16)
        T["SPE"] = ar.t("SPE", L["SPE"], [128, 2, 512], F32)
        T["SPL"] = ar.t("SPL", L["SPL"], [128, 4, 512], F32)
        T["LPT"] = ar.t("LPT", L["LPT"], [128, 2 * 512], BF16)
        T["RETT"] = ar.t("RETT", L["RETT"], [128, NCH * 512], BF16)
        T["OUT2"] = ar.t("OUT2", L["OUT2"], [128, 4, 1024], BF16)
        T["OUTF"] = ar.t("OUTF", L["OUTF"], [128, 2, 1024], F32)
        T["OATT"] = ar.t("OATT", L["OATT"], [128, 4, 1024], BF16)
        T["X1"] = ar.t("X1", L["X1"], [128, 8, 1024], BF16)

        pv = PsumViews(nc, ctx)
        T["pS"] = [pv.t(f"pS{i}", 4096 * i, [128, 2, 512], F32) for i in range(2)]
        T["pAV"] = [pv.t(f"pAV{i}", 8192 + 2048 * i, [128, 264], F32) for i in range(2)]
        T["pG"] = [pv.t(f"pG{i}", 12288 + 2048 * i, [128, 512], F32) for i in range(2)]
        T["pGb"] = [pv.t(f"pGb{i}", 12288 + 2048 * i, [128, 128], BF16) for i in range(2)]
        T["pTb"] = [pv.t(f"pTb{i}", 8192 + 2048 * i, [128, 128], BF16) for i in range(2)]
        T["pTa"] = [pv.t(f"pTa{i}", 2048 * i, [128, 128], BF16) for i in range(4)]
        T["pS2"] = [pv.t(f"pS2{i}", 4096 * i, [128, 2, 512], F32) for i in range(2)]
        T["pS4"] = pv.t("pS4", 0, [128, 4, 512], F32)
        T["pF"] = [pv.t(f"pF{i}", 2048 * i, [128, 512], F32) for i in range(4)]
        T["pBV"] = [pv.t(f"pBV{i}", 8192 + 2048 * i, [128, 512], F32) for i in range(2)]
        T["pF2"] = [pv.t(f"pF2{i}", 4096 * i, [128, 1024], F32) for i in range(2)]
        T["pAV2"] = [pv.t(f"pAV2{i}", 8192 + 4096 * i, [128, 2, 512], F32) for i in range(2)]
        T["pQB"] = [pv.t(f"pQB{i}", 4096 + 2048 * i, [128, 512], F32) for i in range(2)]
        T["pTo"] = [pv.t(f"pTo{i}", 8192 + 2048 * i, [128, 128], BF16) for i in range(4)]
        T["SQS"] = pv.t("SQSp", 8192, [128, 1024], F32)

        SM = {n: Sem(ctx.enter_context(nc.semaphore(n)), n) for n in SEM_NAMES}

        blk = ctx.enter_context(nc.Block())
        for role, starter in (("sync", blk.sync), ("gpsimd", blk.gpsimd),
                              ("vector", blk.vector), ("scalar", blk.scalar),
                              ("tensor", blk.tensor)):
            for s in SM.values():
                s.v = 0
            starter(lambda e, r=role: emit(r, e, T, dram, SM, nseq))

    return nc


def emit(role, E, T, dram, S, nseq):
    OT, XN, UC, MSK, IDN, SEL = (T["OT"], T["XN"], T["UC"],
                                 T["MSK"], T["IDN"], T["SEL"])
    BSBO, BSWOB, LOT = T["BSBO"], T["BSWOB"], T["LOT"]
    EPS1, BOHL, WOBHL, ONES2 = T["EPS1"], T["BOHL"], T["WOBHL"], T["ONES2"]
    SQD = T["SQD"]
    SQS, VTMP, LNT, ONAT, PTM, RCP = (T["SQS"], T["VTMP"], T["LNT"],
                                      T["ONAT"], T["PTM"], T["RCP"])
    XINA = T["XINA"]
    A_W0, A_W1, A_W2, XNT, KG, VG, QB, PT = (T["A_W0"], T["A_W1"], T["A_W2"],
                                             T["XNT"], T["KG"], T["VG"],
                                             T["QB"], T["PT"])
    WOUTT, RTRT, LEAF, XN2T, QT = T["WOUTT"], T["RTRT"], T["LEAF"], T["XN2T"], T["QT"]
    WO_SB = T["WO_SB"]
    SPE, SPL, LPT, RETT = T["SPE"], T["SPL"], T["LPT"], T["RETT"]
    OUT2, OUTF, OATT, X1 = T["OUT2"], T["OUTF"], T["OATT"], T["X1"]
    pS, pAV, pG, pGb, pTb, pTa = T["pS"], T["pAV"], T["pG"], T["pGb"], T["pTb"], T["pTa"]
    pS2, pAV2, pQB, pTo = T["pS2"], T["pAV2"], T["pQB"], T["pTo"]
    pS4, pF, pF2 = T["pS4"], T["pF"], T["pF2"]
    pBV = T["pBV"]
    xd, yd = dram["x"], dram["y"]

    def I(name, f):
        return f() if role == name else None

    def inc(ins, sem, k=1):
        if ins is not None:
            ins.then_inc(sem.h, k)
        sem.v += k

    _whw = {}

    def W(name, sem, val):
        if role == name and val > 0:
            if _whw.get(id(sem), -1) >= val:
                return
            _whw[id(sem)] = val
            E.wait_ge(sem.h, val)

    genk = [0]

    def gen_acquire():
        k = genk[0]
        genk[0] += 1
        bank = k % 2
        W("tensor", S["cGE0"] if bank == 0 else S["cGE1"], k // 2)
        return k, bank

    def gen_release(ins, bank):
        inc(ins, S["cGE0"] if bank == 0 else S["cGE1"])

    def ge_rec():
        return (S["cGE0"].v, S["cGE1"].v)

    def wait_ge_rec(name, rec):
        W(name, S["cGE0"], rec[0])
        W(name, S["cGE1"], rec[1])

    bvk = [0]
    BVS = ("cVG", "cTPg")

    def bv_acquire():
        k = bvk[0]
        bvk[0] += 1
        bank = k % 2
        W("tensor", S[BVS[bank]], k // 2)
        return bank

    def bv_release(ins, bank):
        inc(ins, S[BVS[bank]])

    f4k = [0]
    F4S = ("cTPf", "cTEf", "cTPo", "cTEo")
    f4_extra = [None] * 4

    def f4_acquire():
        k = f4k[0]
        f4k[0] += 1
        bank = k % 4
        W("tensor", S[F4S[bank]], k // 4)
        if f4_extra[bank] is not None:
            W("tensor", S[f4_extra[bank][0]], f4_extra[bank][1])
            f4_extra[bank] = None
        return bank

    def f4_release(ins, bank):
        inc(ins, S[F4S[bank]])

    def f4_rec():
        return tuple(S[nm].v for nm in F4S)

    def f4_wait(name, rec):
        for nm, vv in zip(F4S, rec):
            W(name, S[nm], vv)

    def f4_pairs(rec):
        return [(S[nm], vv) for nm, vv in zip(F4S, rec)]

    def lnt(col):
        return LNT[:, col:col + 1]

    epsap = EPS1[:, 0:1]
    oneap = EPS1[:, 1:2]

    def vchain(ins):
        inc(ins, S["cVC"])

    def vwait():
        W("vector", S["cVC"], S["cVC"].v)

    def schain(ins):
        inc(ins, S["cSC"])

    def swait():
        W("scalar", S["cSC"], S["cSC"].v)

    def ln_sums(src_ap, r, colbase, sems, extra_waits, extra_waits_s=None, dump=None):
        cR, cSq = sems[0], sems[1]
        for sm, val in extra_waits:
            W("vector", sm, val)
        W("vector", sems[3], sems[3].v - 3)
        i1 = I("vector", lambda: E.reduce_sum(out=lnt(colbase + 0 + r), in_=src_ap, axis=AXX))
        inc(i1, cR)
        for sm, val in (extra_waits_s if extra_waits_s is not None else extra_waits):
            W("scalar", sm, val)
        W("scalar", S["cE"], S["cE"].v)
        W("scalar", S["cS1"], S["cS1"].v)
        W("scalar", S["cS3"], S["cS3"].v)
        W("scalar", S["cS5"], S["cS5"].v)
        W("scalar", S["cS7"], S["cS7"].v)
        swait()
        dmp = SQD if dump is None else dump
        i1 = I("scalar", lambda: E.activation(out=dmp[:], in_=src_ap,
                                              func=AF.Square, accum_out=lnt(colbase + 4 + r)))
        inc(i1, cSq)

    def ln_sums_psum(pair_ap, r, colbase, sems, mm_val, vring):
        cR, cSq = sems[0], sems[1]
        W("vector", S["cGP"], mm_val)
        for sm, val in vring:
            W("vector", sm, val)
        W("vector", sems[3], sems[3].v - 3)
        i1 = I("vector", lambda: E.reduce_sum(out=lnt(colbase + 0 + r), in_=pair_ap, axis=AXX))
        inc(i1, cR)
        W("scalar", S["cGP"], mm_val)
        W("scalar", S["cE"], S["cE"].v)
        W("scalar", S["cS1"], S["cS1"].v)
        W("scalar", S["cS3"], S["cS3"].v)
        W("scalar", S["cS5"], S["cS5"].v)
        W("scalar", S["cS7"], S["cS7"].v)
        swait()
        i1 = I("scalar", lambda: E.activation(out=SQD[:], in_=pair_ap,
                                              func=AF.Square, accum_out=lnt(colbase + 4 + r)))
        inc(i1, cSq)

    def ln_finish(r, colbase, sems, negmu=False, war=None):
        cR, cSq, cV, cRs = sems
        W("vector", cSq, cSq.v)
        W("vector", cR, cR.v)
        if war is not None:
            W("vector", S[war[0]], war[1])
        vwait()
        i1 = I("vector", lambda: E.tensor_scalar(out=lnt(colbase + 8 + r), in0=lnt(colbase + 0 + r),
                                                 scalar1=(-1.0 if negmu else 1.0) / D,
                                                 scalar2=None, op0=ALU.mult))
        vchain(i1)
        i2 = I("vector", lambda: E.tensor_scalar(out=lnt(colbase + 20 + r), in0=lnt(colbase + 0 + r),
                                                 scalar1=-1.0 / (D * D), scalar2=lnt(colbase + 0 + r),
                                                 op0=ALU.mult, op1=ALU.mult))
        vchain(i2)
        vwait()
        i3 = I("vector", lambda: E.scalar_tensor_tensor(out=lnt(colbase + 12 + r), in0=lnt(colbase + 4 + r),
                                                        scalar=1.0 / D, in1=lnt(colbase + 20 + r),
                                                        op0=ALU.mult, op1=ALU.add))
        inc(i3, cV)
        W("scalar", cV, cV.v)
        i4 = I("scalar", lambda: E.activation(out=lnt(colbase + 16 + r), in_=lnt(colbase + 12 + r),
                                              func=AF.Ln, bias=epsap))
        schain(i4)
        swait()
        i5 = I("scalar", lambda: E.activation(out=lnt(colbase + 24 + r), in_=lnt(colbase + 16 + r),
                                              func=AF.Exp, scale=-0.5))
        inc(i5, cRs)
        if negmu:
            W("scalar", cRs, cRs.v)
            i6 = I("scalar", lambda: E.activation(out=lnt(colbase + 20 + r), in_=lnt(colbase + 8 + r),
                                                  func=AF.Identity, scale=lnt(colbase + 24 + r),
                                                  bias=0.0))
            schain(i6)

    for dst, src in ((UC, "u_c"), (MSK, "mask_c"), (IDN, "id_c"), (SEL, "sel_c"),
                     (ONES2, "ones2_c"), (EPS1, "epsone_c")):
        i1 = I("sync", lambda dst=dst, src=src: E.dma_start(out=dst[:], in_=dram[src]))
        inc(i1, S["dC"], 16)
    i1 = I("sync", lambda: E.dma_start(out=BSBO[:], in_=dram["bo"]))
    inc(i1, S["dC"], 16)
    i1 = I("sync", lambda: E.dma_start(out=BSWOB[:], in_=dram["wob"]))
    inc(i1, S["dC"], 16)
    W("scalar", S["dC"], 16 * 8)
    i1 = I("scalar", lambda: E.copy(out=BOHL[0:1, :], in_=BSBO[:]))
    i2 = I("scalar", lambda: E.copy(out=WOBHL[0:1, :], in_=BSWOB[:]))
    inc(i2, S["sInit"])
    W("vector", S["sInit"], 1)
    W("vector", S["dC"], 16 * 8)
    i1 = I("vector", lambda: E.tensor_sub(out=LOT[0:1, 0:1024], in0=BSBO[:], in1=BOHL[0:1, :]))
    i2 = I("vector", lambda: E.tensor_sub(out=LOT[0:1, 1024:2048], in0=BSWOB[:], in1=WOBHL[0:1, :]))
    inc(i2, S["sInit"])
    W("sync", S["sInit"], 2)
    i1 = I("sync", lambda: E.dma_start(out=BOHL[1:2, :], in_=LOT[0:1, 0:1024]))
    inc(i1, S["dC"], 16)
    i1 = I("sync", lambda: E.dma_start(out=WOBHL[1:2, :], in_=LOT[0:1, 1024:2048]))
    inc(i1, S["dC"], 16)
    W("gpsimd", S["dC"], 16 * 8)

    def emit_kv(grp, nb, tef_need, dw_need, f4=False, k_gen2=False):
        def acq():
            if f4:
                return f4_acquire()
            return gen_acquire()[1]

        def rel(ins, bank):
            if f4:
                f4_release(ins, bank)
            else:
                gen_release(ins, bank)

        def pb(bank):
            return pF[bank] if f4 else pG[bank]
        for ti, t in enumerate(range(4 * nb, 4 * nb + 4)):
            bank = acq()
            W("tensor", S[tef_need[ti][0]], tef_need[ti][1])
            W("tensor", S[dw_need[1][0]], dw_need[1][1])
            i1 = None
            for c in range(NCH):
                i1 = I("tensor", lambda c=c, t=t, grp=grp, bank=bank: E.matmul(
                    pb(bank)[:],
                    XNT[:, 4096 * (t // 4) + 512 * c + 128 * (t % 4):4096 * (t // 4) + 512 * c + 128 * (t % 4) + 128],
                    A_W2[:, 1024 * c + 512 * grp:1024 * c + 512 * grp + 512],
                    start=(c == 0), stop=(c == NCH - 1)))
            inc(i1, S["cGP"])
            W("vector", S["cGP"], S["cGP"].v)
            for hh in range(8):
                I("vector", lambda t=t, hh=hh, grp=grp, bank=bank: E.tensor_scalar(
                    out=VG[:, 528 * t + 66 * hh:528 * t + 66 * hh + 64],
                    in0=pb(bank)[:, 64 * hh:64 * hh + 64],
                    scalar1=UC[:, H * t + 8 * grp + hh:H * t + 8 * grp + hh + 1],
                    scalar2=None, op0=ALU.mult))
            i2 = I("vector", lambda t=t, grp=grp: E.tensor_copy(
                out=VG[:, 528 * t + 64:528 * t + 527:66],
                in_=UC[:, H * t + 8 * grp:H * t + 8 * grp + 8]))
            rel(i2, bank)
        for c2 in range(4):
            use_gen = k_gen2 and c2 < 2
            use_bv = k_gen2 and c2 >= 2
            if use_gen:
                k, bank = gen_acquire()
                dst = pG[bank]
            elif use_bv:
                bank = bv_acquire()
                dst = pBV[bank]
            else:
                bank = acq()
                dst = pb(bank)
            if c2 == 0:
                W("tensor", S[dw_need[0][0]], dw_need[0][1])
            i1 = None
            for c in range(NCH):
                i1 = I("tensor", lambda c=c, c2=c2, nb=nb, grp=grp, dst=dst: E.matmul(
                    dst[:],
                    A_W1[:, 1024 * c + 128 * (grp * 4 + c2):1024 * c + 128 * (grp * 4 + c2) + 128],
                    XNT[:, 4096 * nb + 512 * c:4096 * nb + 512 * c + 512],
                    start=(c == 0), stop=(c == NCH - 1)))
            inc(i1, S["cGP"])
            W("vector", S["cGP"], S["cGP"].v)
            i2 = I("vector", lambda c2=c2, nb=nb, dst=dst: E.tensor_copy(
                out=KG[:, N * c2 + 512 * nb:N * c2 + 512 * (nb + 1)], in_=dst[:]))
            if use_gen:
                gen_release(i2, bank)
            elif use_bv:
                bv_release(i2, bank)
            else:
                rel(i2, bank)

    qbe_rec = {0: 0, 1: 0}

    def emit_qblk(grp, b, qb_rdy_blk, in_attention, tef_need=None, dw_need=0, f4=False):
        for c2 in range(4):
            if in_attention:
                bank = c2 % 2
                W("tensor", S["cE"], S["cE"].v)
                W("tensor", S["cQBE"], qbe_rec[bank])
                dst = pQB[bank]
            else:
                bank = f4_acquire() if f4 else gen_acquire()[1]
                if tef_need:
                    for nm, vv in tef_need:
                        W("tensor", S[nm], vv)
                if dw_need:
                    W("tensor", S[dw_need[0]], dw_need[1])
                dst = pF[bank] if f4 else pG[bank]
            i1 = None
            for c in range(NCH):
                i1 = I("tensor", lambda c=c, c2=c2, b=b, grp=grp, dst=dst: E.matmul(
                    dst[:],
                    A_W0[:, 1024 * c + 128 * (grp * 4 + c2):1024 * c + 128 * (grp * 4 + c2) + 128],
                    XNT[:, 4096 * b + 512 * c:4096 * b + 512 * c + 512],
                    start=(c == 0), stop=(c == NCH - 1)))
            inc(i1, S["cGP"])
            W("vector", S["cGP"], S["cGP"].v)
            i2 = I("vector", lambda c2=c2, b=b, dst=dst: E.tensor_copy(
                out=QB[:, b % 2, 512 * c2:512 * (c2 + 1)], in_=dst[:]))
            if in_attention:
                inc(i2, S["cQBE"])
                qbe_rec[bank] = S["cQBE"].v
                qb_rdy_blk[(grp, b, c2)] = [("cQBE", S["cQBE"].v)]
            else:
                if f4:
                    f4_release(i2, bank)
                    qb_rdy_blk[(grp, b, c2)] = [(nm, vv) for nm, vv in zip(F4S, f4_rec())]
                else:
                    gen_release(i2, bank)
                    qb_rdy_blk[(grp, b, c2)] = [("cGE0", S["cGE0"].v), ("cGE1", S["cGE1"].v)]

    ot_blk = [0]
    ot_prev = [None]
    xn2t_prev = [0]
    PREV = {}
    PRO = [None]

    def phi1a_prologue(s2, x1b_gate, dtf1_gate):
        base_LN, base_S1 = S["cLN"].v, S["cS1"].v
        base_V2, base_S2 = S["cV2"].v, S["cS2"].v
        base_G1 = S["cG1"].v
        base_dXA = (S["dXA0"].v, S["dXA1"].v, S["dXA2"].v, S["dXA3"].v)
        dta_tile = {}

        def p1_dma(g4):
            for tt in range(4):
                t = 4 * g4 + tt
                r = t % 4
                W("gpsimd", S["cLN"], base_LN + t - 3)
                W("gpsimd", S["cS1"], base_S1 + t - 3)
                W("gpsimd", S["cG1"], base_G1 + t - 3)
                if s2 == 0 and t == 0:
                    W("gpsimd", S["sInit"], 2)
                if x1b_gate is not None and t < 4:
                    W("gpsimd", S["cX1"], x1b_gate + 13 + r)
                i1 = I("gpsimd", lambda t=t, r=r: E.dma_start(
                    out=XINA[:, r, :], in_=xd[s2, 128 * t:128 * (t + 1), :]))
                inc(i1, S[f"dXA{r}"], 16)

        def p1_sums(g4):
            for tt in range(4):
                t = 4 * g4 + tt
                r = t % 4
                if t == 0:
                    W("scalar", S["cAV"], S["cAV"].v)
                    W("scalar", S["cM"], S["cM"].v)
                    W("scalar", S["cNRM"], S["cNRM"].v)
                    W("scalar", S["dTo0"], S["dTo0"].v)
                    W("scalar", S["dTo1"], S["dTo1"].v)
                ln_sums(XINA[:, r, :], r, 0,
                        (S["cG1"], S["cS1"], S["cV2"], S["cS2"]),
                        [(S[f"dXA{r}"], base_dXA[r] + 16 * (t // 4 + 1)),
                         (S["cV2"], base_V2 + t - 3)])

        def p1_fin(g4):
            for tt in range(4):
                t = 4 * g4 + tt
                r = t % 4
                ln_finish(r, 0, (S["cG1"], S["cS1"], S["cV2"], S["cS2"]),
                          negmu=True, war=("cLN", base_LN + t - 3))
                if s2 == 0 and t == 0:
                    W("scalar", S["dC"], 16 * 10)
                if dtf1_gate and t == 0:
                    W("scalar", S["dTf1"], dtf1_gate)
                if t >= 4:
                    W("scalar", S[dta_tile[t - 4][0]], dta_tile[t - 4][1])
                swait()
                i1 = I("scalar", lambda r=r: E.activation(
                    out=XN[:, r, :], in_=XINA[:, r, :], func=AF.Identity,
                    scale=lnt(24 + r), bias=lnt(20 + r)))
                inc(i1, S["cLN"])

        def p1_xnt(g4, ret3_gate):
            for tt in range(4):
                t = 4 * g4 + tt
                r = t % 4
                atd = f"dTa{t % 4}"
                W("sync", S["cLN"], base_LN + t + 1)
                if s2 == 0 and t == 0:
                    W("sync", S["dC"], 16 * 10)
                if ret3_gate and tt == 0:
                    W("sync", S["cGP"], ret3_gate)
                i1 = I("sync", lambda r=r, t=t: E.dma_start_transpose(
                    XNT[:].rearrange("p (nb c u) -> p nb c u", c=8, u=512)[
                        :, t // 4, :, 128 * (t % 4):128 * (t % 4) + 128],
                    XN[:, r, :].rearrange("p (c f) -> p c f", f=128)))
                inc(i1, S[atd], 16)
                dta_tile[t] = (atd, S[atd].v)

        return {"dma": p1_dma, "sums": p1_sums, "fin": p1_fin, "xnt": p1_xnt,
                "dta_tile": dta_tile}

    for s in range(nseq):
        for wdst, wsrc, gp, sem in ((A_W2, "wv", "wo3", "dWv"),
                                    (A_W1, "wk", "qt3", "dWk"),
                                    (A_W0, "wq", "o23", "dWq")):
            if s > 0:
                W("gpsimd", S["cGP"], PREV[gp])
            i1 = I("gpsimd", lambda wdst=wdst, wsrc=wsrc: E.dma_start(
                out=wdst[:].rearrange("p (c d) -> p c d", d=1024),
                in_=dram[wsrc].rearrange("(c p) d -> p c d", p=128)))
            inc(i1, S[sem], 16)
        dW_qkv = (("dWk", S["dWk"].v), ("dWv", S["dWv"].v))
        dW_q = ("dWq", S["dWq"].v)

        s_end_rec = [None, None]
        ot_rec = {}
        av_end_rec = [None, None]
        kv_rdy = [None, None]
        qblk_g1_gp = [0]
        qb_rdy_blk = {}
        otw_war = [0]

        kv_parts = {}
        if PRO[0] is not None:
            st = PRO[0]
            PRO[0] = None
        else:
            st = phi1a_prologue(s, None, 0)
            st["dma"](0)
            st["sums"](0)
            st["fin"](0)
            st["xnt"](0, None)
        st["dma"](1)
        st["sums"](1)
        dta_tile = st["dta_tile"]
        p1_dma, p1_sums, p1_fin, p1_xnt = st["dma"], st["sums"], st["fin"], st["xnt"]
        for g4 in range(4):
            dta_g4 = [dta_tile[4 * g4 + i] for i in range(4)]
            if g4 == 0:
                W("tensor", S["cNRM"], S["cNRM"].v)
                W("tensor", S["cS3"], S["cS3"].v)
                W("tensor", S["cS5"], S["cS5"].v)
            if g4 == 0 and s > 0:
                W("scalar", S["cGP"], PREV["o23"])
                W("scalar", S["cOF"], PREV["of"])
                W("vector", S["cGP"], PREV["o23"])
            if g4 == 3 and s > 0:
                W("vector", S["dY0"], 16 * (NT // 2) * s)
                W("vector", S["dY1"], 16 * (NT // 2) * s)
                W("vector", S["cS3"], PREV["s3"])
            nx = g4 + 1
            if nx < 4:
                p1_fin(nx)
                p1_xnt(nx, None)
            if g4 + 2 < 4:
                p1_dma(g4 + 2)
                p1_sums(g4 + 2)
            emit_kv(0, g4, dta_g4, dW_qkv, f4=True, k_gen2=True)
            kv_parts[g4] = [(nm, vv) for nm, vv in zip(F4S, f4_rec())]
            if g4 == 3:
                emit_qblk(0, 0, qb_rdy_blk, False, [dta_tile[i] for i in range(4)], dW_q, f4=True)

        xnt_tef = [dta_tile[12 + i] for i in range(4)]
        kv_rdy[0] = ([(nm, vv) for nm, vv in zip(F4S, f4_rec())]
                     + [("cGE0", S["cGE0"].v), ("cGE1", S["cGE1"].v)]
                     + [("cVG", S["cVG"].v), ("cTPg", S["cTPg"].v)])

        for grp in range(2):
            if grp == 1:
                W("vector", S["cS"], s_end_rec[0])
                W("vector", S["cAV"], av_end_rec[0])
                W("scalar", S["cS"], s_end_rec[0])

                W("tensor", S["cNRM"], S["cNRM"].v)
                W("tensor", S["cE"], S["cE"].v)
                W("tensor", S["cQBE"], S["cQBE"].v)
                for nb in range(4):
                    emit_kv(1, nb, xnt_tef, dW_qkv, f4=True, k_gen2=True)
                emit_qblk(1, 0, qb_rdy_blk, False, xnt_tef, dW_q, f4=True)
                kv_rdy[1] = ([(nm, vv) for nm, vv in zip(F4S, f4_rec())]
                             + [("cGE0", S["cGE0"].v), ("cGE1", S["cGE1"].v)]
                             + [("cVG", S["cVG"].v), ("cTPg", S["cTPg"].v)])

            if grp == 1:
                W("gpsimd", S["cGP"], S["cGP"].v)
                i1 = I("gpsimd", lambda: E.dma_start(
                    out=A_W1[:].rearrange("p (c d) -> p c d", d=1024),
                    in_=dram["wqfT"].rearrange("(c p) d -> p c d", p=128)))
                inc(i1, S["dW2"], 16)
                i1 = I("gpsimd", lambda: E.dma_start(
                    out=WO_SB[:].rearrange("p (c d) -> p c d", d=1024),
                    in_=dram["wo"].rearrange("(c p) d -> p c d", p=128)))
                inc(i1, S["dW2"], 16)
                dW_wo = S["dW2"].v

            NB2 = 4
            ptm_war = {0: 0, 1: 0, 2: 0, 3: 0}
            ptdw = {0: 0, 1: 0}
            mask_done = {}
            for b in range(NB2):
                for pp in range(2):
                    pairs = (2 * pp, 2 * pp + 1)
                    nkj = 4 * b + 4
                    av_started = [[False, False], [False, False]]
                    exp_idx = [[-1] * nkj, [-1] * nkj]
                    av_idx = [[-1] * nkj, [-1] * nkj]
                    for kj in range(nkj):
                        qlo = max(0, kj - 4 * b)
                        for pi in range(2):
                            p = pairs[pi]
                            if kj == 0:
                                if pp == 0 and b == 0:
                                    for sm, val in kv_rdy[grp]:
                                        W("tensor", S[sm], val)
                                    W("tensor", S["cE"], S["cE"].v)
                                    if grp == 0:
                                        for nm in F4S:
                                            W("tensor", S[nm], S[nm].v)
                                for sm, val in qb_rdy_blk[(grp, b, p)]:
                                    W("tensor", S[sm], val)
                                if pi == 1 and (grp, b, 3) in qb_rdy_blk:
                                    for sm, val in qb_rdy_blk[(grp, b, 3)]:
                                        W("tensor", S[sm], val)
                            else:
                                W("tensor", S["cE"], exp_idx[pi][kj - 1] + 1)
                            i1 = None
                            for hh in range(2):
                                i1 = I("tensor", lambda hh=hh, p=p, kj=kj, pi=pi, qlo=qlo, b=b: E.matmul(
                                    pS4[:, 2 * pi + hh, 128 * qlo:512],
                                    KG[:, N * p + 128 * kj:N * p + 128 * (kj + 1)][64 * hh:64 * (hh + 1), :],
                                    QB[:, b % 2, 512 * p + 128 * qlo:512 * (p + 1)][64 * hh:64 * (hh + 1), :],
                                    start=True, stop=True))
                            inc(i1, S["cS"])
                        d = kj % 2
                        for pi in range(2):
                            gs = S["cE"].v
                            exp_idx[pi][kj] = gs
                            W("scalar", S["cS"], S["cS"].v - (1 - pi))
                            W("scalar", S["cAV"], ptdw[d])
                            i1 = I("scalar", lambda d=d, pi=pi, qlo=qlo: E.activation(
                                out=PT[:, 2 * d + pi, :].rearrange(
                                    "p (g c) -> p g c", g=2)[:, :, 128 * qlo:512],
                                in_=pS4[:, 2 * pi:2 * pi + 2, 128 * qlo:512],
                                func=AF.Exp, scale=SCALE))
                            inc(i1, S["cE"])
                        diag = (kj >= 4 * b)
                        if diag:
                            q = kj - 4 * b
                            for pi in range(2):
                                gs = exp_idx[pi][kj]
                                mslot = 2 * (kj % 2) + pi
                                W("gpsimd", S["cE"], gs + 1)
                                W("gpsimd", S["cAV"], ptm_war[mslot])
                                pslot = 2 * (kj % 2) + pi
                                i1 = None
                                for hh in range(2):
                                    i1 = I("gpsimd", lambda hh=hh, pslot=pslot, q=q, mslot=mslot: E.tensor_mul(
                                        out=PTM[:, mslot, 128 * hh:128 * (hh + 1)],
                                        in0=PT[:, pslot, 512 * hh + 128 * q:512 * hh + 128 * q + 128],
                                        in1=MSK[:]))
                                inc(i1, S["cM"])
                                mask_done[(pi, b, pp, kj)] = S["cM"].v
                        av_kjs = [kj - 1] if kj > 0 else []
                        if kj == nkj - 1:
                            av_kjs.append(kj)
                        for akj in av_kjs:
                            aqlo = max(0, akj - 4 * b)
                            adiag = (akj >= 4 * b)
                            for pi in range(2):
                                p = pairs[pi]
                                gs = exp_idx[pi][akj]
                                W("tensor", S["cE"], gs + 1)
                                if adiag:
                                    W("tensor", S["cM"], mask_done[(pi, b, pp, akj)])
                                av_last = None
                                for q in range(aqlo, 4):
                                    qh = q // 2
                                    if not av_started[pi][qh]:
                                        W("tensor", S["cNRM"], otw_war[0])
                                        av_started[pi][qh] = True
                                        st = True
                                    else:
                                        st = False
                                    for hh in range(2):
                                        is_diag_tile = adiag and (q == akj - 4 * b)
                                        if is_diag_tile:
                                            lhs = PTM[:, 2 * (akj % 2) + pi, 128 * hh:128 * (hh + 1)]
                                        else:
                                            lhs = PT[:, 2 * (akj % 2) + pi, 512 * hh + 128 * q:512 * hh + 128 * q + 128]
                                        av_last = I("tensor", lambda lhs=lhs, q=q, hh=hh, akj=akj, p=p, pi=pi, st=(st and hh == 0): E.matmul(
                                            pAV2[pi][:, q // 2, 130 * (q % 2) + 65 * hh:130 * (q % 2) + 65 * hh + 65],
                                            lhs,
                                            VG[:, 528 * akj + 66 * (2 * p + hh):528 * akj + 66 * (2 * p + hh) + 65],
                                            start=(st and hh == 0), stop=False, skip_group_check=True))
                                inc(av_last, S["cAV"])
                                av_idx[pi][akj] = S["cAV"].v
                                if adiag and akj == kj:
                                    pass
                                if adiag:
                                    ptm_war[2 * (akj % 2) + pi] = S["cAV"].v
                            ptdw[akj % 2] = S["cAV"].v
                    if pp == 1 and b + 1 < NB2:
                        emit_qblk(grp, b + 1, qb_rdy_blk, in_attention=True)
                    nrm_vals = [0, 0]
                    for pi in range(2):
                        p = pairs[pi]
                        W("vector", S["cAV"], av_idx[pi][nkj - 1])
                        W("vector", S["cNRM"], S["cNRM"].v)
                        if pi == 0 and ot_prev[0] is not None:
                            W("vector", S[ot_prev[0][0]], ot_prev[0][1])
                        iR = I("vector", lambda pi=pi: E.reciprocal(
                            out=RCP[:, 8 * pi:8 * pi + 8].rearrange("p (a b) -> p a b", a=2),
                            in_=pAV2[pi][:, :, 64:260:65]))
                        vchain(iR)
                        vwait()
                        last = None
                        for q in range(4):
                            for hh in range(2):
                                col = 8 * pi + 4 * (q // 2) + 2 * (q % 2) + hh
                                last = I("vector", lambda q=q, hh=hh, pi=pi, col=col: E.tensor_scalar(
                                    out=ONAT[:, 4 * pi + q, 64 * hh:64 * hh + 64],
                                    in0=pAV2[pi][:, q // 2, 130 * (q % 2) + 65 * hh:130 * (q % 2) + 65 * hh + 64],
                                    scalar1=RCP[:, col:col + 1],
                                    scalar2=None, op0=ALU.mult))
                        inc(last, S["cNRM"])
                        nrm_vals[pi] = S["cNRM"].v
                    otd = f"dTo{ot_blk[0] % 2}"
                    for pi in range(2):
                        p = pairs[pi]
                        gp = grp * 4 + p
                        W("sync", S["cNRM"], nrm_vals[pi])
                        i1 = I("sync", lambda pi=pi, b=b, gp=gp: E.dma_start_transpose(
                            OT[:, N * gp + 512 * b:N * gp + 512 * (b + 1)].rearrange(
                                "p (q f) -> p q f", f=128),
                            ONAT[:, 4 * pi:4 * pi + 4, :]))
                        inc(i1, S[otd], 16)
                    ot_prev[0] = (otd, S[otd].v)
                    ot_rec[(grp, b)] = (otd, S[otd].v)
                    ot_blk[0] += 1
                    otw_war[0] = S["cNRM"].v
            s_end_rec[grp] = S["cS"].v
            av_end_rec[grp] = S["cAV"].v
            if grp == 1:
                qblk_g1_gp[0] = S["cGP"].v


        ot_dto = (S["dTo0"].v, S["dTo1"].v)

        W("gpsimd", S["cGP"], qblk_g1_gp[0])
        for wdst, wsrc, dd in ((WOUTT, "woutT", 1024),
                               (RTRT, "rtrT", 256), (LEAF, "leaf", 1024)):
            i1 = I("gpsimd", lambda wdst=wdst, wsrc=wsrc, dd=dd: E.dma_start(
                out=wdst[:].rearrange("p (c d) -> p c d", d=dd),
                in_=dram[wsrc].rearrange("(c p) d -> p c d", p=128)))
            inc(i1, S["dW3"], 16)
        dW_ffn = S["dW3"].v

        base_dXB = (S["dXB0"].v, S["dXB1"].v, S["dXB2"].v, S["dXB3"].v)
        base_X1 = S["cX1"].v
        base_V4, base_S3, base_S4 = S["cV4"].v, S["cS3"].v, S["cS4"].v
        base_V6, base_S6, base_XN2 = S["cV6"].v, S["cS6"].v, S["cXN2"].v
        base_V8, base_S7, base_S8, base_OF = S["cV8"].v, S["cS7"].v, S["cS8"].v, S["cOF"].v
        base_dY = (S["dY0"].v, S["dY1"].v)
        base_VSt = S["cVSt"].v
        pipe = {"qt_gp": 0, "z_gp": 0, "sel_gp": 0, "ret_gp": 0, "o2_gp": 0,
                "xn2t": {}, "xn2t_tile": {}}

        def wo_stage(b4):
            gt0 = 4 * b4
            for tt in range(4):
                t = gt0 + tt
                r = t % 4
                W("gpsimd", S["cX1"], base_X1 + t - 3)
                if t < 4:
                    W("gpsimd", S["cG1"], S["cG1"].v)
                    W("gpsimd", S["cS1"], S["cS1"].v)
                    W("gpsimd", S["cLN"], S["cLN"].v)
                i1 = I("gpsimd", lambda t=t, r=r: E.dma_start(
                    out=XINA[:, r, :], in_=xd[s, 128 * t:128 * (t + 1), :]))
                inc(i1, S[f"dXB{r}"], 16)
            for tt in range(4):
                t = gt0 + tt
                ro = t % 4
                tile_banks = []
                for e in range(2):
                    bank = f4_acquire()
                    if tt == 0 and e == 0:
                        for g in range(2):
                            W("tensor", S[ot_rec[(g, b4)][0]], ot_rec[(g, b4)][1])
                        if b4 == 0:
                            W("tensor", S["dC"], 16 * 10)
                            W("tensor", S["cE"], S["cE"].v)
                            W("tensor", S["cQBE"], S["cQBE"].v)
                            W("tensor", S["cNRM"], S["cNRM"].v)
                    W("tensor", S["dW2"], dW_wo)
                    i1 = I("tensor", lambda e=e, bank=bank: E.matmul(
                        pF2[bank // 2][:, 512 * (bank % 2):512 * (bank % 2) + 512],
                        ONES2[:, :], BOHL[:, 512 * e:512 * (e + 1)],
                        start=True, stop=False))
                    for c in range(NCH):
                        i1 = I("tensor", lambda c=c, t=t, e=e, bank=bank: E.matmul(
                            pF2[bank // 2][:, 512 * (bank % 2):512 * (bank % 2) + 512],
                            OT[:, N * c + 128 * t:N * c + 128 * (t + 1)],
                            WO_SB[:, 1024 * c + 512 * e:1024 * c + 512 * (e + 1)],
                            start=False, stop=(c == NCH - 1)))
                    inc(i1, S["cGP"])
                    tile_banks.append(bank)
                assert tile_banks[0] % 2 == 0 and tile_banks[1] == tile_banks[0] + 1
                mmv = S["cGP"].v
                if b4 == 0 and tt == 0:
                    W("scalar", S["cNRM"], S["cNRM"].v)
                    W("scalar", S["cVG"], S["cVG"].v)
                    W("scalar", S["cTPg"], S["cTPg"].v)
                if b4 == 1 and tt == 0:
                    W("scalar", S["cAV"], S["cAV"].v)
                    W("scalar", S["cM"], S["cM"].v)
                    W("scalar", S["cNRM"], S["cNRM"].v)
                    W("scalar", S["dTo0"], ot_dto[0])
                    W("scalar", S["dTo1"], ot_dto[1])
                W("vector", S["cGP"], mmv)
                W("vector", S["cV4"], base_V4 + t - 3)
                W("vector", S["cS4"], S["cS4"].v - 3)
                iR = I("vector", lambda r4=t % 4, pr=tile_banks[0] // 2: E.reduce_sum(
                    out=lnt(28 + r4), in_=pF2[pr][:], axis=AXX))
                inc(iR, S["cG2"])
                for e in range(2):
                    bank = tile_banks[e]
                    W("vector", S["cGP"], mmv)
                    W("vector", S["cVSt"], base_VSt + t - 3)
                    i2 = I("vector", lambda e=e, ro=ro, bank=bank: E.tensor_copy(
                        out=OATT[:, ro, 512 * e:512 * (e + 1)],
                        in_=pF2[bank // 2][:, 512 * (bank % 2):512 * (bank % 2) + 512]))
                    f4_release(i2, bank)
                for sm, val in f4_pairs(f4_rec()):
                    W("scalar", sm, val)
                W("scalar", S["cE"], S["cE"].v)
                W("scalar", S["cS1"], S["cS1"].v)
                W("scalar", S["cS3"], S["cS3"].v)
                W("scalar", S["cS5"], S["cS5"].v)
                W("scalar", S["cS7"], S["cS7"].v)
                swait()
                sq_dmp = SQS if b4 == 0 else SQD
                iS = I("scalar", lambda ro=ro, r4=t % 4, sq_dmp=sq_dmp: E.activation(
                    out=sq_dmp[:], in_=OATT[:, ro, :], func=AF.Square,
                    accum_out=lnt(32 + r4)))
                inc(iS, S["cS3"])
            for tt in range(4):
                t = gt0 + tt
                ln_finish(t % 4, 28, (S["cG2"], S["cS3"], S["cV4"], S["cS4"]))
            for tt in range(4):
                t = gt0 + tt
                r = t % 4
                W("vector", S["cS4"], base_S4 + t + 1)
                W("vector", S[f"dXB{r}"], base_dXB[r] + 16 * (t // 4 + 1))
                W("vector", S["cX1"], S["cX1"].v)
                W("vector", S["cOF"], S["cOF"].v)
                i1 = I("vector", lambda r=r, r4=t % 4: E.tensor_scalar(
                    out=VTMP[:], in0=OATT[:, r, :], scalar1=lnt(36 + r4),
                    scalar2=lnt(52 + r4), op0=ALU.subtract, op1=ALU.mult))
                inc(i1, S["cVSt"])
                W("vector", S["cVSt"], S["cVSt"].v)
                i2 = I("vector", lambda t=t, r=r: E.tensor_add(
                    out=X1[:, t % 8, :], in0=XINA[:, r, :], in1=VTMP[:]))
                inc(i2, S["cX1"])
            for tt in range(4):
                t = gt0 + tt
                ln_sums(X1[:, t % 8, :], t % 4, 56,
                        (S["cG3"], S["cS5"], S["cV6"], S["cS6"]),
                        [(S["cX1"], base_X1 + t + 1), (S["cV6"], base_V6 + t - 3)],
                        dump=(SQS if b4 == 0 else None))
            xtd_prev = f"dTf{(b4 - 1) % 2}"
            for tt in range(4):
                t = gt0 + tt
                r = t % 4
                ln_finish(r, 56, (S["cG3"], S["cS5"], S["cV6"], S["cS6"]),
                          negmu=True, war=("cXN2", base_XN2 + t - 3))
                if t >= 4:
                    W("scalar", S[xtd_prev], pipe["xn2t"][b4 - 1][1])
                else:
                    W("scalar", S[dta_tile[12 + r][0]], dta_tile[12 + r][1])
                swait()
                i1 = I("scalar", lambda r=r, t=t, r4=t % 4: E.activation(
                    out=XN[:, r, :], in_=X1[:, t % 8, :], func=AF.Identity,
                    scale=lnt(80 + r4), bias=lnt(76 + r4)))
                inc(i1, S["cXN2"])
            xtd = f"dTf{b4 % 2}"
            for tt in range(4):
                t = gt0 + tt
                r = t % 4
                W("sync", S["cXN2"], base_XN2 + t + 1)
                i1 = I("sync", lambda r=r, tt=tt, b4=b4: E.dma_start_transpose(
                    XN2T[:, b4 % 2, :].rearrange("p (c q f) -> p c q f", q=4, f=128)[:, :, tt, :],
                    XN[:, r, :].rearrange("p (c f) -> p c f", f=128)))
                inc(i1, S[xtd], 16)
            pipe["xn2t"][b4] = (xtd, S[xtd].v)

        def ffn_stage(b4, hook=None):
            gt0 = 4 * b4
            xslot = b4 % 2
            for e in range(NCH):
                bank = f4_acquire()
                W("tensor", S[pipe["xn2t"][b4][0]], pipe["xn2t"][b4][1])
                W("tensor", S["dW2"], dW_wo)
                W("tensor", S["dW3"], dW_ffn)
                i1 = None
                for c in range(NCH):
                    i1 = I("tensor", lambda c=c, e=e, bank=bank, xslot=xslot: E.matmul(
                        pF[bank][:],
                        A_W1[:, 1024 * c + 128 * e:1024 * c + 128 * e + 128],
                        XN2T[:, xslot, 512 * c:512 * (c + 1)],
                        start=(c == 0), stop=(c == NCH - 1)))
                inc(i1, S["cGP"])
                W("vector", S["cGP"], S["cGP"].v)
                W("vector", S["cGP"], pipe["z_gp"])
                i2 = I("vector", lambda e=e, bank=bank: E.tensor_copy(
                    out=QT[:, 512 * e:512 * (e + 1)], in_=pF[bank][:]))
                f4_release(i2, bank)
            pipe["qt_gp"] = S["cGP"].v
            qt_rdy = f4_rec()

            for nch2 in range(2):
                bank = f4_acquire()
                f4_wait("tensor", qt_rdy)
                i1 = None
                for c in range(NCH):
                    i1 = I("tensor", lambda c=c, nch2=nch2, bank=bank: E.matmul(
                        pF[bank][:],
                        RTRT[:, 256 * c + 128 * nch2:256 * c + 128 * (nch2 + 1)],
                        QT[:, 512 * c:512 * (c + 1)],
                        start=(c == 0), stop=(c == NCH - 1)))
                inc(i1, S["cGP"])
                zv = S["cGP"].v
                W("scalar", S["cGP"], zv)
                W("scalar", S["cHL"], S["cHL"].v)
                if nch2 == 0:
                    W("scalar", S["cGP"], pipe["sel_gp"])
                i1 = I("scalar", lambda bank=bank: E.activation(
                    out=SPE[:, 0, :], in_=pF[bank][:], func=AF.Exp, scale=1.0))
                schain(i1)
                swait()
                i3 = I("scalar", lambda nch2=nch2: E.activation(
                    out=SPL[:, nch2, :], in_=SPE[:, 0, :], func=AF.Ln, bias=oneap))
                inc(i3, S["cHL"])
                W("vector", S["cGP"], zv)
                W("vector", S["cHL"], S["cHL"].v)
                if nch2 == 0:
                    W("vector", S["cGP"], pipe["sel_gp"])
                iv = I("vector", lambda nch2=nch2, bank=bank: E.tensor_sub(
                    out=SPL[:, nch2 + 2, :], in0=SPL[:, nch2, :], in1=pF[bank][:]))
                f4_release(iv, bank)
            pipe["z_gp"] = S["cGP"].v
            z_rdy = f4_rec()

            for mh in range(2):
                bank = f4_acquire()
                f4_wait("tensor", z_rdy)
                i1 = None
                for c in range(4):
                    i1 = I("tensor", lambda c=c, mh=mh, bank=bank: E.matmul(
                        pF[bank][:],
                        SEL[:, 256 * c + 128 * mh:256 * c + 128 * (mh + 1)],
                        SPL[:, c, :],
                        start=(c == 0), stop=(c == 3)))
                inc(i1, S["cGP"])
                W("scalar", S["cGP"], S["cGP"].v)
                if mh == 0:
                    W("scalar", S["cGP"], pipe["ret_gp"])
                i2 = I("scalar", lambda mh=mh, bank=bank: E.activation(
                    out=LPT[:, 512 * mh:512 * (mh + 1)], in_=pF[bank][:], func=AF.Exp))
                f4_release(i2, bank)
            pipe["sel_gp"] = S["cGP"].v
            lp_rdy = f4_rec()

            for dc in range(NCH):
                bank = f4_acquire()
                f4_wait("tensor", lp_rdy)
                i1 = None
                for mh in range(2):
                    i1 = I("tensor", lambda mh=mh, dc=dc, bank=bank: E.matmul(
                        pF[bank][:],
                        LEAF[:, 1024 * mh + 128 * dc:1024 * mh + 128 * dc + 128],
                        LPT[:, 512 * mh:512 * (mh + 1)],
                        start=(mh == 0), stop=(mh == 1)))
                inc(i1, S["cGP"])
                W("vector", S["cGP"], S["cGP"].v)
                if dc == 0:
                    W("vector", S["cGP"], pipe["o2_gp"])
                i2 = I("vector", lambda dc=dc, bank=bank: E.tensor_copy(
                    out=RETT[:, 512 * dc:512 * (dc + 1)], in_=pF[bank][:]))
                f4_release(i2, bank)
            pipe["ret_gp"] = S["cGP"].v
            ret_rdy = f4_rec()
            if hook is not None:
                hook()

            for tt in range(4):
                t = gt0 + tt
                ro = t % 4
                tile_banks = []
                for e in range(2):
                    bank = f4_acquire()
                    f4_wait("tensor", ret_rdy)
                    i1 = I("tensor", lambda e=e, bank=bank: E.matmul(
                        pF2[bank // 2][:, 512 * (bank % 2):512 * (bank % 2) + 512],
                        ONES2[:, :], WOBHL[:, 512 * e:512 * (e + 1)],
                        start=True, stop=False))
                    for c in range(NCH):
                        i1 = I("tensor", lambda c=c, tt=tt, e=e, bank=bank: E.matmul(
                            pF2[bank // 2][:, 512 * (bank % 2):512 * (bank % 2) + 512],
                            RETT[:, 512 * c + 128 * tt:512 * c + 128 * (tt + 1)],
                            WOUTT[:, 1024 * c + 512 * e:1024 * c + 512 * (e + 1)],
                            start=False, stop=(c == NCH - 1)))
                    inc(i1, S["cGP"])
                    tile_banks.append(bank)
                assert tile_banks[0] % 2 == 0 and tile_banks[1] == tile_banks[0] + 1
                mmv = S["cGP"].v
                for e in range(2):
                    bank = tile_banks[e]
                    W("vector", S["cGP"], mmv)
                    W("vector", S["cOF"], base_OF + t - 3)
                    W("vector", S["cS7"], base_S7 + t - 3)
                    i2 = I("vector", lambda e=e, ro=ro, bank=bank: E.tensor_copy(
                        out=OUT2[:, ro, 512 * e:512 * (e + 1)],
                        in_=pF2[bank // 2][:, 512 * (bank % 2):512 * (bank % 2) + 512]))
                    f4_release(i2, bank)
                ln_sums(OUT2[:, t % 4, :], t % 4, 84,
                        (S["cG4"], S["cS7"], S["cV8"], S["cS8"]),
                        f4_pairs(f4_rec()) + [(S["cV8"], base_V8 + t - 3)])
            for tt in range(4):
                t = gt0 + tt
                ln_finish(t % 4, 84, (S["cG4"], S["cS7"], S["cV8"], S["cS8"]))
            for tt in range(4):
                t = gt0 + tt
                r = t % 2
                W("vector", S["cS8"], base_S8 + t + 1)
                W("vector", S["dY0" if r == 0 else "dY1"], base_dY[r] + 16 * (t // 2))
                W("vector", S["cX1"], S["cX1"].v)
                W("vector", S["cOF"], S["cOF"].v)
                i1 = I("vector", lambda ro=t % 4, r4=t % 4: E.tensor_scalar(
                    out=VTMP[:], in0=OUT2[:, ro, :], scalar1=lnt(92 + r4),
                    scalar2=lnt(108 + r4), op0=ALU.subtract, op1=ALU.mult))
                vchain(i1)
                vwait()
                i2 = I("vector", lambda t=t, r=r: E.tensor_add(
                    out=OUTF[:, r, :], in0=X1[:, t % 8, :], in1=VTMP[:]))
                inc(i2, S["cOF"])
                W("sync", S["cOF"], S["cOF"].v)
                i3 = I("sync", lambda t=t, r=r: E.dma_start(
                    out=yd[s, 128 * t:128 * (t + 1), :], in_=OUTF[:, r, :]))
                inc(i3, S["dY0" if r == 0 else "dY1"], 16)
            pipe["o2_gp"] = S["cGP"].v

        wo_stage(0)
        wo_stage(1)
        ffn_stage(0)
        wo_stage(2)
        ffn_stage(1)
        wo_stage(3)
        wo3_gp = S["cGP"].v
        if s + 1 < nseq:
            PRO[0] = phi1a_prologue(s + 1, base_X1, S["dTf1"].v)
            PRO[0]["dma"](0)
        ffn_stage(2)
        if s + 1 < nseq:
            PRO[0]["sums"](0)

        def f3_hook():
            if s + 1 < nseq:
                PRO[0]["fin"](0)
                PRO[0]["xnt"](0, S["cGP"].v)
        ffn_stage(3, hook=f3_hook)
        xn2t_prev[0] = S["dTf1"].v
        PREV.update(wo3=wo3_gp, qt3=pipe["qt_gp"], ret3=pipe["ret_gp"],
                    o23=pipe["o2_gp"], x1b=base_X1, s3=S["cS3"].v, of=S["cOF"].v)

    W("sync", S["dY0"], 16 * (NT // 2) * nseq)
    W("sync", S["dY1"], 16 * (NT // 2) * nseq)


_NC_CACHE = {}


def _prep_weights(inputs):
    f = np.float32
    routers_pad = np.zeros((256, D), dtype=f)
    routers_pad[:255] = np.asarray(inputs["L0_ffn_routers"], f)
    return {
        "wq": np.ascontiguousarray(np.asarray(inputs["L0_attn_Wq"], f)),
        "wk": np.ascontiguousarray(np.asarray(inputs["L0_attn_Wk"], f)),
        "wv": np.ascontiguousarray(np.asarray(inputs["L0_attn_Wv"], f)),
        "wo": np.ascontiguousarray(np.asarray(inputs["L0_attn_Wo"], f)),
        "bo": np.ascontiguousarray(np.asarray(inputs["L0_attn_bo"], f).reshape(1, D)),
        "wqfT": np.ascontiguousarray(np.asarray(inputs["L0_ffn_Wq"], f).T),
        "rtrT": np.ascontiguousarray(routers_pad.T),
        "leaf": np.ascontiguousarray(np.asarray(inputs["L0_ffn_leaf_values"], f)),
        "woutT": np.ascontiguousarray(np.asarray(inputs["L0_ffn_Wout_w"], f).T),
        "wob": np.ascontiguousarray(np.asarray(inputs["L0_ffn_Wout_b"], f).reshape(1, D)),
    }


def kernel(**inputs):
    if "nc" not in _NC_CACHE:
        _NC_CACHE["nc"] = build_nc()
    nc = _NC_CACHE["nc"]
    x = np.asarray(inputs["L0_attn_x"], np.float32)
    w = _prep_weights(inputs)
    in_maps = []
    for c in range(NCORES):
        m = dict(w)
        m["x"] = np.ascontiguousarray(x[c * NSEQ:(c + 1) * NSEQ])
        in_maps.append(m)
    res = run_bass_kernel_spmd(nc, in_maps, core_ids=list(range(NCORES)))
    y = np.concatenate([r["y"] for r in res.results], axis=0)
    return y.reshape(B, N, D)


if __name__ == "__main__":
    nc = build_nc()
    print("built ok, arena bytes:", ARENA_SZ)


# version 60
# speedup vs baseline: 1.1708x; 1.1708x over previous
import contextlib
import numpy as np
import ml_dtypes

import concourse.bass as bass
import concourse.mybir as mybir
from concourse.bass_utils import run_bass_kernel_spmd

F32 = mybir.dt.float32
BF16 = mybir.dt.bfloat16
U8 = mybir.dt.uint8
AF = mybir.ActivationFunctionType
ALU = mybir.AluOpType
AXX = mybir.AxisListType.X

B, N, D, H, DH = 32, 2048, 1024, 16, 64
NCORES = 8
NSEQ = B // NCORES
NT = N // 128
NBLK = N // 256
NB4 = N // 512
NCH = D // 128
EPS = 1e-5
SCALE = DH ** -0.5

bfarr = lambda a: np.ascontiguousarray(np.asarray(a, dtype=ml_dtypes.bfloat16))


def _consts():
    h = np.arange(1, H + 1, dtype=np.float64)
    slopes = 1.0 / (2.0 ** 8) ** (h / H)
    j = np.arange(N, dtype=np.float64)
    u = np.exp(-slopes[None, :] * j[:, None])
    u_tab = np.zeros((128, NT * H), dtype=np.float32)
    for t in range(NT):
        u_tab[:, t * H:(t + 1) * H] = u[t * 128:(t + 1) * 128, :]
    mask = (np.arange(128)[None, :] >= np.arange(128)[:, None]).astype(np.float32)
    ident = np.eye(128, dtype=np.float32)
    sel = np.zeros((512, 256), dtype=np.float32)
    for lev in range(8):
        n = 1 << lev
        span = 256 // (2 * n)
        for i in range(n):
            r = n - 1 + i
            lo = i * 2 * span
            sel[r, lo:lo + span] = -1.0
            sel[256 + r, lo + span:lo + 2 * span] = -1.0
    sel_pk = np.zeros((128, 4 * 256), dtype=np.float32)
    for c in range(4):
        sel_pk[:, 256 * c:256 * (c + 1)] = sel[128 * c:128 * (c + 1), :]
    ones2 = np.ones((2, 128), dtype=np.float32)
    epsone = np.zeros((128, 2), dtype=np.float32)
    epsone[:, 0] = EPS
    epsone[:, 1] = 1.0
    return dict(u=u_tab, mask=mask, ident=ident, sel=sel_pk, ones2=ones2, epsone=epsone)


class Arena:
    def __init__(self, nc, ctx, name, size_bytes):
        self.nc = nc
        self.h = ctx.enter_context(nc.sbuf_tensor(name, [128, size_bytes], U8))
        self.base = nc.lookup_mloc(self.h).addr
        self.size = size_bytes

    def t(self, name, off, shape, dtype):
        per_part = int(np.prod(shape[1:])) * mybir.dt.size(dtype)
        assert off + per_part <= self.size, f"{name}: {off}+{per_part} > {self.size}"
        hnd = self.nc.virtual_sbuf_tensor(name, list(shape), dtype)
        self.nc._tensor(hnd.name, hnd.shape, hnd.dtype, type="SB")
        mloc = self.nc.lookup_mloc(hnd)
        mloc.allocated = not self.nc.target_bir_lowering
        mloc.addr = self.base + off
        return hnd


class PsumViews:
    def __init__(self, nc, ctx):
        self.nc = nc
        self.h = ctx.enter_context(nc.psum_tensor("psb", [128, 4096], F32))
        self.base = nc.lookup_mloc(self.h).addr

    def t(self, name, byte_off, shape, dtype):
        hnd = self.nc.virtual_psum_tensor(name, list(shape), dtype)
        self.nc._tensor(hnd.name, hnd.shape, hnd.dtype, type="PSUM")
        mloc = self.nc.lookup_mloc(hnd)
        mloc.allocated = not self.nc.target_bir_lowering
        mloc.addr = self.base + byte_off
        return hnd


class Sem:
    __slots__ = ("h", "v", "name")

    def __init__(self, h, name=""):
        self.h = h
        self.v = 0
        self.name = name


PRUNE_SAME = [True]
SAME_ENG_SEMS = {
    "vector": {"cG1", "cG2", "cG3", "cG4", "cV2", "cV4", "cV6", "cV8",
               "cX1", "cOF", "cNRM", "cQBE", "cVSt", "cVC"},
    "scalar": {"cS1", "cS2", "cS3", "cS4", "cS5", "cS6", "cS7", "cS8",
               "cE", "cLN", "cXN2", "cHL", "cSC"},
    "tensor": {"cGP", "cS", "cAV"},
    "gpsimd": {"cM"},
}


SEM_NAMES = [
    "dXA0", "dXA1", "dXA2", "dXA3", "dXB0", "dXB1", "dXB2", "dXB3", "dW", "dW2", "dW3", "dC", "dY0", "dY1",
    "cG1", "cS1", "cV2", "cS2", "cLN",
    "cGP", "cGE0", "cGE1",
    "cS", "cE", "cM", "cAV", "cNRM",
    "cG2", "cS3", "cV4", "cS4", "cX1",
    "cG3", "cS5", "cV6", "cS6", "cXN2",
    "cTPf", "cTEf",
    "cZL", "cHL",
    "cG4", "cS7", "cV8", "cS8", "cOF",
    "cVC", "cSC", "cGC", "cQBE", "cTPo", "cTEo", "cVSt", "cVG", "cRCP", "cTEfV", "cTPg",
    "dTo0", "dTo1", "dTf0", "dTf1", "dTa0", "dTa1", "dTa2", "dTa3",
    "dWk", "dWv", "dWq",
    "sMT", "sMV", "sMS", "sInit",
]

LAYOUT = {}
_off = 0


def _lay(name, nbytes, align=32):
    global _off
    _off = (_off + align - 1) // align * align
    LAYOUT[name] = _off
    _off += nbytes


_lay("OT", 32768)
_lay("XIN", 8192)
_lay("XN", 8192)
_lay("UC", 1024)
_lay("MSK", 256)
_lay("IDN", 256)
_lay("SEL", 4096)
_lay("EPS1", 32)
_lay("BOHL", 2048)
_lay("WOBHL", 2048)
_lay("ONES2", 256)
_lay("VTMP", 2048)
_lay("LNT", 512)
_lay("ONAT", 2048)
_lay("PTM", 2048)
_lay("RCP", 64)
_UB = (_off + 31) // 32 * 32
LAYOUT["A_W0"] = _UB + 0
LAYOUT["A_W1"] = _UB + 16384
LAYOUT["A_W2"] = _UB + 32768
LAYOUT["XNT"] = _UB + 49152
LAYOUT["KG"] = _UB + 81920
LAYOUT["VG"] = _UB + 98304
LAYOUT["QB"] = _UB + 115200
LAYOUT["PT"] = _UB + 123392
_BB = _UB + 32768
_bc = _BB + 16384
def _layB(name, nbytes):
    global _bc
    LAYOUT[name] = _bc
    _bc += nbytes
_layB("RTRT", 4096)
_layB("LEAF", 4096)
_layB("XN2T", 16384)
_layB("QT", 8192)
_layB("SPE", 4096)
_layB("SPL", 8192)
_layB("LPT", 2048)
_layB("RETT", 8192)
_layB("OUT2", 8192)
_layB("OUTF", 8192)
_layB("OATT", 8192)
_layB("X1", 16384)
LAYOUT["XINA"] = LAYOUT["XIN"]
ARENA_SZ = max(LAYOUT["X1"] + 16384, LAYOUT["PT"] + 8192)
assert ARENA_SZ <= 212800, ARENA_SZ


def build_nc(nseq=NSEQ):
    nc = bass.Bass("TRN2", target_bir_lowering=False)
    C = _consts()

    dram = {}
    dram["x"] = nc.dram_tensor("x", [nseq, N, D], F32, kind="ExternalInput").ap()
    for nm, shp in (("wq", [D, D]), ("wk", [D, D]), ("wv", [D, D]), ("wo", [D, D]),
                    ("bo", [1, D]), ("wqfT", [D, D]), ("rtrT", [D, 256]),
                    ("leaf", [256, D]), ("woutT", [D, D]), ("wob", [1, D])):
        dram[nm] = nc.dram_tensor(nm, shp, F32, kind="ExternalInput").ap()
    dram["y"] = nc.dram_tensor("y", [nseq, N, D], F32, kind="ExternalOutput").ap()
    dram["u_c"] = nc.inline_tensor(C["u"], "u_c").ap()
    dram["mask_c"] = nc.inline_tensor(bfarr(C["mask"]), "mask_c").ap()
    dram["id_c"] = nc.inline_tensor(bfarr(C["ident"]), "id_c").ap()
    dram["sel_c"] = nc.inline_tensor(C["sel"], "sel_c").ap()
    dram["ones2_c"] = nc.inline_tensor(bfarr(C["ones2"]), "ones2_c").ap()
    dram["epsone_c"] = nc.inline_tensor(C["epsone"], "epsone_c").ap()

    ctx = contextlib.ExitStack()
    with ctx:
        ar = Arena(nc, ctx, "arena", ARENA_SZ)
        L = LAYOUT
        T = {}
        T["OT"] = ar.t("OT", L["OT"], [128, NCH * N], BF16)
        
        T["XN"] = ar.t("XN", L["XN"], [128, 4, 1024], BF16)
        T["UC"] = ar.t("UC", L["UC"], [128, 256], F32)
        T["MSK"] = ar.t("MSK", L["MSK"], [128, 128], BF16)
        T["IDN"] = ar.t("IDN", L["IDN"], [128, 128], BF16)
        T["SEL"] = ar.t("SEL", L["SEL"], [128, 1024], F32)
        T["BSBO"] = ar.t("BSBO", L["XIN"], [1, 1024], F32)
        T["BSWOB"] = ar.t("BSWOB", L["XIN"] + 4096, [1, 1024], F32)
        T["LOT"] = ar.t("LOT", L["XN"], [1, 2048], BF16)
        T["SQSB"] = T["LOT"]
        T["EPS1"] = ar.t("EPS1", L["EPS1"], [128, 2], F32)
        T["BOHL"] = ar.t("BOHL", L["BOHL"], [2, 1024], BF16)
        T["WOBHL"] = ar.t("WOBHL", L["WOBHL"], [2, 1024], BF16)
        T["ONES2"] = ar.t("ONES2", L["ONES2"], [2, 128], BF16)

        T["VTMP"] = ar.t("VTMP", L["VTMP"], [128, 1024], BF16)
        T["LNT"] = ar.t("LNT", L["LNT"], [128, 128], F32)
        T["XINA"] = ar.t("XINA", L["XINA"], [128, 4, 1024], BF16)
        T["ONAT"] = ar.t("ONAT", L["ONAT"], [128, 8, 128], BF16)
        T["PTM"] = ar.t("PTM", L["PTM"], [128, 4, 256], BF16)
        T["RCP"] = ar.t("RCP", L["RCP"], [128, 16], F32)
        T["A_W0"] = ar.t("A_W0", L["A_W0"], [128, NCH * 1024], BF16)
        T["A_W1"] = ar.t("A_W1", L["A_W1"], [128, NCH * 1024], BF16)
        T["A_W2"] = ar.t("A_W2", L["A_W2"], [128, NCH * 1024], BF16)
        T["XNT"] = ar.t("XNT", L["XNT"], [128, NCH * N], BF16)
        T["KG"] = ar.t("KG", L["KG"], [128, 4 * N], BF16)
        T["VG"] = ar.t("VG", L["VG"], [128, NT * 528], BF16)
        T["QB"] = ar.t("QB", L["QB"], [128, 2, 2048], BF16)
        T["PT"] = ar.t("PT", L["PT"], [128, 4, 1024], BF16)
        T["SQD"] = ar.t("SQD", L["ONAT"], [128, 1024], F32)
        T["WOUTT"] = ar.t("WOUTT", L["A_W0"], [128, NCH * 1024], BF16)
        T["WO_SB"] = ar.t("WO_SB", L["A_W2"], [128, NCH * 1024], BF16)
        T["RTRT"] = ar.t("RTRT", L["RTRT"], [128, NCH * 256], BF16)
        T["LEAF"] = ar.t("LEAF", L["LEAF"], [128, 2 * 1024], BF16)
        T["XN2T"] = ar.t("XN2T", L["XN2T"], [128, 2, NCH * 512], BF16)
        T["QT"] = ar.t("QT", L["QT"], [128, NCH * 512], BF16)
        T["SPE"] = ar.t("SPE", L["SPE"], [128, 2, 512], F32)
        T["SPL"] = ar.t("SPL", L["SPL"], [128, 4, 512], F32)
        T["LPT"] = ar.t("LPT", L["LPT"], [128, 2 * 512], BF16)
        T["RETT"] = ar.t("RETT", L["RETT"], [128, NCH * 512], BF16)
        T["OUT2"] = ar.t("OUT2", L["OUT2"], [128, 4, 1024], BF16)
        T["OUTF"] = ar.t("OUTF", L["OUTF"], [128, 2, 1024], F32)
        T["OATT"] = ar.t("OATT", L["OATT"], [128, 4, 1024], BF16)
        T["X1"] = ar.t("X1", L["X1"], [128, 8, 1024], BF16)

        pv = PsumViews(nc, ctx)
        T["pS"] = [pv.t(f"pS{i}", 4096 * i, [128, 2, 512], F32) for i in range(2)]
        T["pAV"] = [pv.t(f"pAV{i}", 8192 + 2048 * i, [128, 264], F32) for i in range(2)]
        T["pG"] = [pv.t(f"pG{i}", 12288 + 2048 * i, [128, 512], F32) for i in range(2)]
        T["pGb"] = [pv.t(f"pGb{i}", 12288 + 2048 * i, [128, 128], BF16) for i in range(2)]
        T["pTb"] = [pv.t(f"pTb{i}", 8192 + 2048 * i, [128, 128], BF16) for i in range(2)]
        T["pTa"] = [pv.t(f"pTa{i}", 2048 * i, [128, 128], BF16) for i in range(4)]
        T["pS2"] = [pv.t(f"pS2{i}", 4096 * i, [128, 2, 512], F32) for i in range(2)]
        T["pS4"] = pv.t("pS4", 0, [128, 4, 512], F32)
        T["pF"] = [pv.t(f"pF{i}", 2048 * i, [128, 512], F32) for i in range(4)]
        T["pBV"] = [pv.t(f"pBV{i}", 8192 + 2048 * i, [128, 512], F32) for i in range(2)]
        T["pF2"] = [pv.t(f"pF2{i}", 4096 * i, [128, 1024], F32) for i in range(2)]
        T["pAV2"] = [pv.t(f"pAV2{i}", 8192 + 4096 * i, [128, 2, 512], F32) for i in range(2)]
        T["pQB"] = [pv.t(f"pQB{i}", 4096 + 2048 * i, [128, 512], F32) for i in range(2)]
        T["pTo"] = [pv.t(f"pTo{i}", 8192 + 2048 * i, [128, 128], BF16) for i in range(4)]
        T["SQS"] = pv.t("SQSp", 8192, [128, 1024], F32)

        SM = {n: Sem(ctx.enter_context(nc.semaphore(n)), n) for n in SEM_NAMES}

        blk = ctx.enter_context(nc.Block())
        for role, starter in (("sync", blk.sync), ("gpsimd", blk.gpsimd),
                              ("vector", blk.vector), ("scalar", blk.scalar),
                              ("tensor", blk.tensor)):
            for s in SM.values():
                s.v = 0
            starter(lambda e, r=role: emit(r, e, T, dram, SM, nseq))

    return nc


def emit(role, E, T, dram, S, nseq):
    OT, XN, UC, MSK, IDN, SEL = (T["OT"], T["XN"], T["UC"],
                                 T["MSK"], T["IDN"], T["SEL"])
    BSBO, BSWOB, LOT = T["BSBO"], T["BSWOB"], T["LOT"]
    EPS1, BOHL, WOBHL, ONES2 = T["EPS1"], T["BOHL"], T["WOBHL"], T["ONES2"]
    SQD = T["SQD"]
    SQS, VTMP, LNT, ONAT, PTM, RCP = (T["SQS"], T["VTMP"], T["LNT"],
                                      T["ONAT"], T["PTM"], T["RCP"])
    XINA = T["XINA"]
    A_W0, A_W1, A_W2, XNT, KG, VG, QB, PT = (T["A_W0"], T["A_W1"], T["A_W2"],
                                             T["XNT"], T["KG"], T["VG"],
                                             T["QB"], T["PT"])
    WOUTT, RTRT, LEAF, XN2T, QT = T["WOUTT"], T["RTRT"], T["LEAF"], T["XN2T"], T["QT"]
    WO_SB = T["WO_SB"]
    SPE, SPL, LPT, RETT = T["SPE"], T["SPL"], T["LPT"], T["RETT"]
    OUT2, OUTF, OATT, X1 = T["OUT2"], T["OUTF"], T["OATT"], T["X1"]
    pS, pAV, pG, pGb, pTb, pTa = T["pS"], T["pAV"], T["pG"], T["pGb"], T["pTb"], T["pTa"]
    pS2, pAV2, pQB, pTo = T["pS2"], T["pAV2"], T["pQB"], T["pTo"]
    pS4, pF, pF2 = T["pS4"], T["pF"], T["pF2"]
    pBV = T["pBV"]
    xd, yd = dram["x"], dram["y"]

    def I(name, f):
        return f() if role == name else None

    def inc(ins, sem, k=1):
        if ins is not None:
            ins.then_inc(sem.h, k)
        sem.v += k

    _whw = {}

    def W(name, sem, val):
        if role == name and val > 0:
            if _whw.get(id(sem), -1) >= val:
                return
            _whw[id(sem)] = val
            E.wait_ge(sem.h, val)

    genk = [0]

    def gen_acquire():
        k = genk[0]
        genk[0] += 1
        bank = k % 2
        W("tensor", S["cGE0"] if bank == 0 else S["cGE1"], k // 2)
        return k, bank

    def gen_release(ins, bank):
        inc(ins, S["cGE0"] if bank == 0 else S["cGE1"])

    def ge_rec():
        return (S["cGE0"].v, S["cGE1"].v)

    def wait_ge_rec(name, rec):
        W(name, S["cGE0"], rec[0])
        W(name, S["cGE1"], rec[1])

    bvk = [0]
    BVS = ("cVG", "cTPg")

    def bv_acquire():
        k = bvk[0]
        bvk[0] += 1
        bank = k % 2
        W("tensor", S[BVS[bank]], k // 2)
        return bank

    def bv_release(ins, bank):
        inc(ins, S[BVS[bank]])

    f4k = [0]
    F4S = ("cTPf", "cTEf", "cTPo", "cTEo")
    f4_extra = [None] * 4

    def f4_acquire():
        k = f4k[0]
        f4k[0] += 1
        bank = k % 4
        W("tensor", S[F4S[bank]], k // 4)
        if f4_extra[bank] is not None:
            W("tensor", S[f4_extra[bank][0]], f4_extra[bank][1])
            f4_extra[bank] = None
        return bank

    def f4_release(ins, bank):
        inc(ins, S[F4S[bank]])

    def f4_rec():
        return tuple(S[nm].v for nm in F4S)

    def f4_wait(name, rec):
        for nm, vv in zip(F4S, rec):
            W(name, S[nm], vv)

    def f4_pairs(rec):
        return [(S[nm], vv) for nm, vv in zip(F4S, rec)]

    def lnt(col):
        return LNT[:, col:col + 1]

    epsap = EPS1[:, 0:1]
    oneap = EPS1[:, 1:2]

    def vchain(ins):
        inc(ins, S["cVC"])

    def vwait():
        W("vector", S["cVC"], S["cVC"].v)

    def schain(ins):
        inc(ins, S["cSC"])

    def swait():
        W("scalar", S["cSC"], S["cSC"].v)

    def ln_sums(src_ap, r, colbase, sems, extra_waits, extra_waits_s=None, dump=None):
        cR, cSq = sems[0], sems[1]
        for sm, val in extra_waits:
            W("vector", sm, val)
        W("vector", sems[3], sems[3].v - 3)
        i1 = I("vector", lambda: E.reduce_sum(out=lnt(colbase + 0 + r), in_=src_ap, axis=AXX))
        inc(i1, cR)
        for sm, val in (extra_waits_s if extra_waits_s is not None else extra_waits):
            W("scalar", sm, val)
        W("scalar", S["cE"], S["cE"].v)
        W("scalar", S["cS1"], S["cS1"].v)
        W("scalar", S["cS3"], S["cS3"].v)
        W("scalar", S["cS5"], S["cS5"].v)
        W("scalar", S["cS7"], S["cS7"].v)
        swait()
        dmp = SQD if dump is None else dump
        i1 = I("scalar", lambda: E.activation(out=dmp[:], in_=src_ap,
                                              func=AF.Square, accum_out=lnt(colbase + 4 + r)))
        inc(i1, cSq)

    def ln_sums_psum(pair_ap, r, colbase, sems, mm_val, vring):
        cR, cSq = sems[0], sems[1]
        W("vector", S["cGP"], mm_val)
        for sm, val in vring:
            W("vector", sm, val)
        W("vector", sems[3], sems[3].v - 3)
        i1 = I("vector", lambda: E.reduce_sum(out=lnt(colbase + 0 + r), in_=pair_ap, axis=AXX))
        inc(i1, cR)
        W("scalar", S["cGP"], mm_val)
        W("scalar", S["cE"], S["cE"].v)
        W("scalar", S["cS1"], S["cS1"].v)
        W("scalar", S["cS3"], S["cS3"].v)
        W("scalar", S["cS5"], S["cS5"].v)
        W("scalar", S["cS7"], S["cS7"].v)
        swait()
        i1 = I("scalar", lambda: E.activation(out=SQD[:], in_=pair_ap,
                                              func=AF.Square, accum_out=lnt(colbase + 4 + r)))
        inc(i1, cSq)

    def ln_finish(r, colbase, sems, negmu=False, war=None):
        cR, cSq, cV, cRs = sems
        W("vector", cSq, cSq.v)
        W("vector", cR, cR.v)
        if war is not None:
            W("vector", S[war[0]], war[1])
        vwait()
        i1 = I("vector", lambda: E.tensor_scalar(out=lnt(colbase + 8 + r), in0=lnt(colbase + 0 + r),
                                                 scalar1=(-1.0 if negmu else 1.0) / D,
                                                 scalar2=None, op0=ALU.mult))
        vchain(i1)
        i2 = I("vector", lambda: E.tensor_scalar(out=lnt(colbase + 20 + r), in0=lnt(colbase + 0 + r),
                                                 scalar1=-1.0 / (D * D), scalar2=lnt(colbase + 0 + r),
                                                 op0=ALU.mult, op1=ALU.mult))
        vchain(i2)
        vwait()
        i3 = I("vector", lambda: E.scalar_tensor_tensor(out=lnt(colbase + 12 + r), in0=lnt(colbase + 4 + r),
                                                        scalar=1.0 / D, in1=lnt(colbase + 20 + r),
                                                        op0=ALU.mult, op1=ALU.add))
        inc(i3, cV)
        W("scalar", cV, cV.v)
        i4 = I("scalar", lambda: E.activation(out=lnt(colbase + 16 + r), in_=lnt(colbase + 12 + r),
                                              func=AF.Ln, bias=epsap))
        schain(i4)
        swait()
        i5 = I("scalar", lambda: E.activation(out=lnt(colbase + 24 + r), in_=lnt(colbase + 16 + r),
                                              func=AF.Exp, scale=-0.5))
        inc(i5, cRs)
        if negmu:
            W("scalar", cRs, cRs.v)
            i6 = I("scalar", lambda: E.activation(out=lnt(colbase + 20 + r), in_=lnt(colbase + 8 + r),
                                                  func=AF.Identity, scale=lnt(colbase + 24 + r),
                                                  bias=0.0))
            schain(i6)

    for dst, src in ((UC, "u_c"), (MSK, "mask_c"), (IDN, "id_c"), (SEL, "sel_c"),
                     (ONES2, "ones2_c"), (EPS1, "epsone_c")):
        i1 = I("sync", lambda dst=dst, src=src: E.dma_start(out=dst[:], in_=dram[src]))
        inc(i1, S["dC"], 16)
    i1 = I("sync", lambda: E.dma_start(out=BSBO[:], in_=dram["bo"]))
    inc(i1, S["dC"], 16)
    i1 = I("sync", lambda: E.dma_start(out=BSWOB[:], in_=dram["wob"]))
    inc(i1, S["dC"], 16)
    W("scalar", S["dC"], 16 * 8)
    i1 = I("scalar", lambda: E.copy(out=BOHL[0:1, :], in_=BSBO[:]))
    i2 = I("scalar", lambda: E.copy(out=WOBHL[0:1, :], in_=BSWOB[:]))
    inc(i2, S["sInit"])
    W("vector", S["sInit"], 1)
    W("vector", S["dC"], 16 * 8)
    i1 = I("vector", lambda: E.tensor_sub(out=LOT[0:1, 0:1024], in0=BSBO[:], in1=BOHL[0:1, :]))
    i2 = I("vector", lambda: E.tensor_sub(out=LOT[0:1, 1024:2048], in0=BSWOB[:], in1=WOBHL[0:1, :]))
    inc(i2, S["sInit"])
    W("sync", S["sInit"], 2)
    i1 = I("sync", lambda: E.dma_start(out=BOHL[1:2, :], in_=LOT[0:1, 0:1024]))
    inc(i1, S["dC"], 16)
    i1 = I("sync", lambda: E.dma_start(out=WOBHL[1:2, :], in_=LOT[0:1, 1024:2048]))
    inc(i1, S["dC"], 16)
    W("gpsimd", S["dC"], 16 * 8)

    def emit_kv(grp, nb, tef_need, dw_need, f4=False, k_gen2=False):
        def acq():
            if f4:
                return f4_acquire()
            return gen_acquire()[1]

        def rel(ins, bank):
            if f4:
                f4_release(ins, bank)
            else:
                gen_release(ins, bank)

        def pb(bank):
            return pF[bank] if f4 else pG[bank]
        for ti, t in enumerate(range(4 * nb, 4 * nb + 4)):
            bank = acq()
            W("tensor", S[tef_need[ti][0]], tef_need[ti][1])
            W("tensor", S[dw_need[1][0]], dw_need[1][1])
            i1 = None
            for c in range(NCH):
                i1 = I("tensor", lambda c=c, t=t, grp=grp, bank=bank: E.matmul(
                    pb(bank)[:],
                    XNT[:, 4096 * (t // 4) + 512 * c + 128 * (t % 4):4096 * (t // 4) + 512 * c + 128 * (t % 4) + 128],
                    A_W2[:, 1024 * c + 512 * grp:1024 * c + 512 * grp + 512],
                    start=(c == 0), stop=(c == NCH - 1)))
            inc(i1, S["cGP"])
            W("vector", S["cGP"], S["cGP"].v)
            for hh in range(8):
                I("vector", lambda t=t, hh=hh, grp=grp, bank=bank: E.tensor_scalar(
                    out=VG[:, 528 * t + 66 * hh:528 * t + 66 * hh + 64],
                    in0=pb(bank)[:, 64 * hh:64 * hh + 64],
                    scalar1=UC[:, H * t + 8 * grp + hh:H * t + 8 * grp + hh + 1],
                    scalar2=None, op0=ALU.mult))
            i2 = I("vector", lambda t=t, grp=grp: E.tensor_copy(
                out=VG[:, 528 * t + 64:528 * t + 527:66],
                in_=UC[:, H * t + 8 * grp:H * t + 8 * grp + 8]))
            rel(i2, bank)
        for c2 in range(4):
            use_gen = k_gen2 and c2 < 2
            use_bv = k_gen2 and c2 >= 2
            if use_gen:
                k, bank = gen_acquire()
                dst = pG[bank]
            elif use_bv:
                bank = bv_acquire()
                dst = pBV[bank]
            else:
                bank = acq()
                dst = pb(bank)
            if c2 == 0:
                W("tensor", S[dw_need[0][0]], dw_need[0][1])
            i1 = None
            for c in range(NCH):
                i1 = I("tensor", lambda c=c, c2=c2, nb=nb, grp=grp, dst=dst: E.matmul(
                    dst[:],
                    A_W1[:, 1024 * c + 128 * (grp * 4 + c2):1024 * c + 128 * (grp * 4 + c2) + 128],
                    XNT[:, 4096 * nb + 512 * c:4096 * nb + 512 * c + 512],
                    start=(c == 0), stop=(c == NCH - 1)))
            inc(i1, S["cGP"])
            W("vector", S["cGP"], S["cGP"].v)
            i2 = I("vector", lambda c2=c2, nb=nb, dst=dst: E.tensor_copy(
                out=KG[:, N * c2 + 512 * nb:N * c2 + 512 * (nb + 1)], in_=dst[:]))
            if use_gen:
                gen_release(i2, bank)
            elif use_bv:
                bv_release(i2, bank)
            else:
                rel(i2, bank)

    qbe_rec = {0: 0, 1: 0}

    def emit_qblk(grp, b, qb_rdy_blk, in_attention, tef_need=None, dw_need=0, f4=False):
        for c2 in range(4):
            if in_attention:
                bank = c2 % 2
                W("tensor", S["cE"], S["cE"].v)
                W("tensor", S["cQBE"], qbe_rec[bank])
                dst = pQB[bank]
            else:
                bank = f4_acquire() if f4 else gen_acquire()[1]
                if tef_need:
                    for nm, vv in tef_need:
                        W("tensor", S[nm], vv)
                if dw_need:
                    W("tensor", S[dw_need[0]], dw_need[1])
                dst = pF[bank] if f4 else pG[bank]
            i1 = None
            for c in range(NCH):
                i1 = I("tensor", lambda c=c, c2=c2, b=b, grp=grp, dst=dst: E.matmul(
                    dst[:],
                    A_W0[:, 1024 * c + 128 * (grp * 4 + c2):1024 * c + 128 * (grp * 4 + c2) + 128],
                    XNT[:, 4096 * b + 512 * c:4096 * b + 512 * c + 512],
                    start=(c == 0), stop=(c == NCH - 1)))
            inc(i1, S["cGP"])
            W("vector", S["cGP"], S["cGP"].v)
            i2 = I("vector", lambda c2=c2, b=b, dst=dst: E.tensor_copy(
                out=QB[:, b % 2, 512 * c2:512 * (c2 + 1)], in_=dst[:]))
            if in_attention:
                inc(i2, S["cQBE"])
                qbe_rec[bank] = S["cQBE"].v
                qb_rdy_blk[(grp, b, c2)] = [("cQBE", S["cQBE"].v)]
            else:
                if f4:
                    f4_release(i2, bank)
                    qb_rdy_blk[(grp, b, c2)] = [(nm, vv) for nm, vv in zip(F4S, f4_rec())]
                else:
                    gen_release(i2, bank)
                    qb_rdy_blk[(grp, b, c2)] = [("cGE0", S["cGE0"].v), ("cGE1", S["cGE1"].v)]

    ot_blk = [0]
    ot_prev = [None]
    xn2t_prev = [0]
    PREV = {}
    PRO = [None]

    def phi1a_prologue(s2, x1b_gate, dtf1_gate):
        base_LN, base_S1 = S["cLN"].v, S["cS1"].v
        base_V2, base_S2 = S["cV2"].v, S["cS2"].v
        base_G1 = S["cG1"].v
        base_dXA = (S["dXA0"].v, S["dXA1"].v, S["dXA2"].v, S["dXA3"].v)
        dta_tile = {}

        def p1_dma(g4):
            for tt in range(4):
                t = 4 * g4 + tt
                r = t % 4
                W("gpsimd", S["cLN"], base_LN + t - 3)
                W("gpsimd", S["cS1"], base_S1 + t - 3)
                W("gpsimd", S["cG1"], base_G1 + t - 3)
                if s2 == 0 and t == 0:
                    W("gpsimd", S["sInit"], 2)
                if x1b_gate is not None and t < 4:
                    W("gpsimd", S["cX1"], x1b_gate + 13 + r)
                i1 = I("gpsimd", lambda t=t, r=r: E.dma_start(
                    out=XINA[:, r, :], in_=xd[s2, 128 * t:128 * (t + 1), :]))
                inc(i1, S[f"dXA{r}"], 16)

        def p1_sums(g4):
            for tt in range(4):
                t = 4 * g4 + tt
                r = t % 4
                if t == 0:
                    W("scalar", S["cAV"], S["cAV"].v)
                    W("scalar", S["cM"], S["cM"].v)
                    W("scalar", S["cNRM"], S["cNRM"].v)
                    W("scalar", S["dTo0"], S["dTo0"].v)
                    W("scalar", S["dTo1"], S["dTo1"].v)
                ln_sums(XINA[:, r, :], r, 0,
                        (S["cG1"], S["cS1"], S["cV2"], S["cS2"]),
                        [(S[f"dXA{r}"], base_dXA[r] + 16 * (t // 4 + 1)),
                         (S["cV2"], base_V2 + t - 3)])

        def p1_fin(g4):
            for tt in range(4):
                t = 4 * g4 + tt
                r = t % 4
                ln_finish(r, 0, (S["cG1"], S["cS1"], S["cV2"], S["cS2"]),
                          negmu=True, war=("cLN", base_LN + t - 3))
                if s2 == 0 and t == 0:
                    W("scalar", S["dC"], 16 * 10)
                if dtf1_gate and t == 0:
                    W("scalar", S["dTf1"], dtf1_gate)
                if t >= 4:
                    W("scalar", S[dta_tile[t - 4][0]], dta_tile[t - 4][1])
                swait()
                i1 = I("scalar", lambda r=r: E.activation(
                    out=XN[:, r, :], in_=XINA[:, r, :], func=AF.Identity,
                    scale=lnt(24 + r), bias=lnt(20 + r)))
                inc(i1, S["cLN"])

        def p1_xnt(g4, ret3_gate):
            for tt in range(4):
                t = 4 * g4 + tt
                r = t % 4
                atd = f"dTa{t % 4}"
                W("sync", S["cLN"], base_LN + t + 1)
                if s2 == 0 and t == 0:
                    W("sync", S["dC"], 16 * 10)
                if ret3_gate and tt == 0:
                    W("sync", S["cGP"], ret3_gate)
                i1 = I("sync", lambda r=r, t=t: E.dma_start_transpose(
                    XNT[:].rearrange("p (nb c u) -> p nb c u", c=8, u=512)[
                        :, t // 4, :, 128 * (t % 4):128 * (t % 4) + 128],
                    XN[:, r, :].rearrange("p (c f) -> p c f", f=128)))
                inc(i1, S[atd], 16)
                dta_tile[t] = (atd, S[atd].v)

        return {"dma": p1_dma, "sums": p1_sums, "fin": p1_fin, "xnt": p1_xnt,
                "dta_tile": dta_tile}

    for s in range(nseq):
        for wdst, wsrc, gp, sem in ((A_W2, "wv", "wo3", "dWv"),
                                    (A_W1, "wk", "qt3", "dWk"),
                                    (A_W0, "wq", "o23", "dWq")):
            if s > 0:
                W("gpsimd", S["cGP"], PREV[gp])
            i1 = I("gpsimd", lambda wdst=wdst, wsrc=wsrc: E.dma_start(
                out=wdst[:].rearrange("p (c d) -> p c d", d=1024),
                in_=dram[wsrc].rearrange("(c p) d -> p c d", p=128)))
            inc(i1, S[sem], 16)
        dW_qkv = (("dWk", S["dWk"].v), ("dWv", S["dWv"].v))
        dW_q = ("dWq", S["dWq"].v)

        s_end_rec = [None, None]
        ot_rec = {}
        av_end_rec = [None, None]
        kv_rdy = [None, None]
        qblk_g1_gp = [0]
        qb_rdy_blk = {}
        otw_war = [0]

        kv_parts = {}
        if PRO[0] is not None:
            st = PRO[0]
            PRO[0] = None
        else:
            st = phi1a_prologue(s, None, 0)
            st["dma"](0)
            st["sums"](0)
            st["fin"](0)
            st["xnt"](0, None)
        st["dma"](1)
        st["sums"](1)
        dta_tile = st["dta_tile"]
        p1_dma, p1_sums, p1_fin, p1_xnt = st["dma"], st["sums"], st["fin"], st["xnt"]
        for g4 in range(4):
            dta_g4 = [dta_tile[4 * g4 + i] for i in range(4)]
            if g4 == 0:
                W("tensor", S["cNRM"], S["cNRM"].v)
                W("tensor", S["cS3"], S["cS3"].v)
                W("tensor", S["cS5"], S["cS5"].v)
            if g4 == 0 and s > 0:
                W("scalar", S["cGP"], PREV["o23"])
                W("scalar", S["cOF"], PREV["of"])
                W("vector", S["cGP"], PREV["o23"])
            if g4 == 3 and s > 0:
                W("vector", S["dY0"], 16 * (NT // 2) * s)
                W("vector", S["dY1"], 16 * (NT // 2) * s)
                W("vector", S["cS3"], PREV["s3"])
            nx = g4 + 1
            if nx < 4:
                p1_fin(nx)
                p1_xnt(nx, None)
            if g4 + 2 < 4:
                p1_dma(g4 + 2)
                p1_sums(g4 + 2)
            emit_kv(0, g4, dta_g4, dW_qkv, f4=True, k_gen2=True)
            kv_parts[g4] = [(nm, vv) for nm, vv in zip(F4S, f4_rec())]
            if g4 == 3:
                emit_qblk(0, 0, qb_rdy_blk, False, [dta_tile[i] for i in range(4)], dW_q, f4=True)

        xnt_tef = [dta_tile[12 + i] for i in range(4)]
        kv_rdy[0] = ([(nm, vv) for nm, vv in zip(F4S, f4_rec())]
                     + [("cGE0", S["cGE0"].v), ("cGE1", S["cGE1"].v)]
                     + [("cVG", S["cVG"].v), ("cTPg", S["cTPg"].v)])

        for grp in range(2):
            if grp == 1:
                W("vector", S["cS"], s_end_rec[0])
                W("vector", S["cAV"], av_end_rec[0])
                W("scalar", S["cS"], s_end_rec[0])

                W("tensor", S["cNRM"], S["cNRM"].v)
                W("tensor", S["cE"], S["cE"].v)
                W("tensor", S["cQBE"], S["cQBE"].v)
                for nb in range(4):
                    emit_kv(1, nb, xnt_tef, dW_qkv, f4=True, k_gen2=True)
                emit_qblk(1, 0, qb_rdy_blk, False, xnt_tef, dW_q, f4=True)
                kv_rdy[1] = ([(nm, vv) for nm, vv in zip(F4S, f4_rec())]
                             + [("cGE0", S["cGE0"].v), ("cGE1", S["cGE1"].v)]
                             + [("cVG", S["cVG"].v), ("cTPg", S["cTPg"].v)])

            if grp == 1:
                W("gpsimd", S["cGP"], S["cGP"].v)
                i1 = I("gpsimd", lambda: E.dma_start(
                    out=A_W1[:].rearrange("p (c d) -> p c d", d=1024),
                    in_=dram["wqfT"].rearrange("(c p) d -> p c d", p=128)))
                inc(i1, S["dW2"], 16)
                i1 = I("gpsimd", lambda: E.dma_start(
                    out=WO_SB[:].rearrange("p (c d) -> p c d", d=1024),
                    in_=dram["wo"].rearrange("(c p) d -> p c d", p=128)))
                inc(i1, S["dW2"], 16)
                dW_wo = S["dW2"].v

            NB2 = 4
            ptm_war = {0: 0, 1: 0, 2: 0, 3: 0}
            ptdw = {0: 0, 1: 0}
            mask_done = {}
            for b in range(NB2):
                for pp in range(2):
                    pairs = (2 * pp, 2 * pp + 1)
                    nkj = 4 * b + 4
                    av_started = [[False, False], [False, False]]
                    exp_idx = [[-1] * nkj, [-1] * nkj]
                    av_idx = [[-1] * nkj, [-1] * nkj]
                    for kj in range(nkj):
                        qlo = max(0, kj - 4 * b)
                        for pi in range(2):
                            p = pairs[pi]
                            if kj == 0:
                                if pp == 0 and b == 0:
                                    for sm, val in kv_rdy[grp]:
                                        W("tensor", S[sm], val)
                                    W("tensor", S["cE"], S["cE"].v)
                                    if grp == 0:
                                        for nm in F4S:
                                            W("tensor", S[nm], S[nm].v)
                                for sm, val in qb_rdy_blk[(grp, b, p)]:
                                    W("tensor", S[sm], val)
                                if pi == 1 and (grp, b, 3) in qb_rdy_blk:
                                    for sm, val in qb_rdy_blk[(grp, b, 3)]:
                                        W("tensor", S[sm], val)
                            else:
                                W("tensor", S["cE"], exp_idx[pi][kj - 1] + 1)
                            i1 = None
                            for hh in range(2):
                                i1 = I("tensor", lambda hh=hh, p=p, kj=kj, pi=pi, qlo=qlo, b=b: E.matmul(
                                    pS4[:, 2 * pi + hh, 128 * qlo:512],
                                    KG[:, N * p + 128 * kj:N * p + 128 * (kj + 1)][64 * hh:64 * (hh + 1), :],
                                    QB[:, b % 2, 512 * p + 128 * qlo:512 * (p + 1)][64 * hh:64 * (hh + 1), :],
                                    start=True, stop=True))
                            inc(i1, S["cS"])
                        d = kj % 2
                        for pi in range(2):
                            gs = S["cE"].v
                            exp_idx[pi][kj] = gs
                            W("scalar", S["cS"], S["cS"].v - (1 - pi))
                            W("scalar", S["cAV"], ptdw[d])
                            i1 = I("scalar", lambda d=d, pi=pi, qlo=qlo: E.activation(
                                out=PT[:, 2 * d + pi, :].rearrange(
                                    "p (g c) -> p g c", g=2)[:, :, 128 * qlo:512],
                                in_=pS4[:, 2 * pi:2 * pi + 2, 128 * qlo:512],
                                func=AF.Exp, scale=SCALE))
                            inc(i1, S["cE"])
                        diag = (kj >= 4 * b)
                        if diag:
                            q = kj - 4 * b
                            for pi in range(2):
                                gs = exp_idx[pi][kj]
                                mslot = 2 * (kj % 2) + pi
                                W("gpsimd", S["cE"], gs + 1)
                                W("gpsimd", S["cAV"], ptm_war[mslot])
                                pslot = 2 * (kj % 2) + pi
                                i1 = None
                                for hh in range(2):
                                    i1 = I("gpsimd", lambda hh=hh, pslot=pslot, q=q, mslot=mslot: E.tensor_mul(
                                        out=PTM[:, mslot, 128 * hh:128 * (hh + 1)],
                                        in0=PT[:, pslot, 512 * hh + 128 * q:512 * hh + 128 * q + 128],
                                        in1=MSK[:]))
                                inc(i1, S["cM"])
                                mask_done[(pi, b, pp, kj)] = S["cM"].v
                        av_kjs = [kj - 1] if kj > 0 else []
                        if kj == nkj - 1:
                            av_kjs.append(kj)
                        for akj in av_kjs:
                            aqlo = max(0, akj - 4 * b)
                            adiag = (akj >= 4 * b)
                            for pi in range(2):
                                p = pairs[pi]
                                gs = exp_idx[pi][akj]
                                W("tensor", S["cE"], gs + 1)
                                if adiag:
                                    W("tensor", S["cM"], mask_done[(pi, b, pp, akj)])
                                av_last = None
                                for q in range(aqlo, 4):
                                    qh = q // 2
                                    if not av_started[pi][qh]:
                                        W("tensor", S["cNRM"], otw_war[0])
                                        av_started[pi][qh] = True
                                        st = True
                                    else:
                                        st = False
                                    for hh in range(2):
                                        is_diag_tile = adiag and (q == akj - 4 * b)
                                        if is_diag_tile:
                                            lhs = PTM[:, 2 * (akj % 2) + pi, 128 * hh:128 * (hh + 1)]
                                        else:
                                            lhs = PT[:, 2 * (akj % 2) + pi, 512 * hh + 128 * q:512 * hh + 128 * q + 128]
                                        av_last = I("tensor", lambda lhs=lhs, q=q, hh=hh, akj=akj, p=p, pi=pi, st=(st and hh == 0): E.matmul(
                                            pAV2[pi][:, q // 2, 130 * (q % 2) + 65 * hh:130 * (q % 2) + 65 * hh + 65],
                                            lhs,
                                            VG[:, 528 * akj + 66 * (2 * p + hh):528 * akj + 66 * (2 * p + hh) + 65],
                                            start=(st and hh == 0), stop=False, skip_group_check=True))
                                inc(av_last, S["cAV"])
                                av_idx[pi][akj] = S["cAV"].v
                                if adiag and akj == kj:
                                    pass
                                if adiag:
                                    ptm_war[2 * (akj % 2) + pi] = S["cAV"].v
                            ptdw[akj % 2] = S["cAV"].v
                    if pp == 1 and b + 1 < NB2:
                        emit_qblk(grp, b + 1, qb_rdy_blk, in_attention=True)
                    nrm_vals = [0, 0]
                    for pi in range(2):
                        p = pairs[pi]
                        W("vector", S["cAV"], av_idx[pi][nkj - 1])
                        W("vector", S["cNRM"], S["cNRM"].v)
                        if pi == 0 and ot_prev[0] is not None:
                            W("vector", S[ot_prev[0][0]], ot_prev[0][1])
                        iR = I("vector", lambda pi=pi: E.reciprocal(
                            out=RCP[:, 8 * pi:8 * pi + 8].rearrange("p (a b) -> p a b", a=2),
                            in_=pAV2[pi][:, :, 64:260:65]))
                        vchain(iR)
                        vwait()
                        last = None
                        for q in range(4):
                            for hh in range(2):
                                col = 8 * pi + 4 * (q // 2) + 2 * (q % 2) + hh
                                last = I("vector", lambda q=q, hh=hh, pi=pi, col=col: E.tensor_scalar(
                                    out=ONAT[:, 4 * pi + q, 64 * hh:64 * hh + 64],
                                    in0=pAV2[pi][:, q // 2, 130 * (q % 2) + 65 * hh:130 * (q % 2) + 65 * hh + 64],
                                    scalar1=RCP[:, col:col + 1],
                                    scalar2=None, op0=ALU.mult))
                        inc(last, S["cNRM"])
                        nrm_vals[pi] = S["cNRM"].v
                    otd = f"dTo{ot_blk[0] % 2}"
                    for pi in range(2):
                        p = pairs[pi]
                        gp = grp * 4 + p
                        W("sync", S["cNRM"], nrm_vals[pi])
                        i1 = I("sync", lambda pi=pi, b=b, gp=gp: E.dma_start_transpose(
                            OT[:, N * gp + 512 * b:N * gp + 512 * (b + 1)].rearrange(
                                "p (q f) -> p q f", f=128),
                            ONAT[:, 4 * pi:4 * pi + 4, :]))
                        inc(i1, S[otd], 16)
                    ot_prev[0] = (otd, S[otd].v)
                    ot_rec[(grp, b)] = (otd, S[otd].v)
                    ot_blk[0] += 1
                    otw_war[0] = S["cNRM"].v
            s_end_rec[grp] = S["cS"].v
            av_end_rec[grp] = S["cAV"].v
            if grp == 1:
                qblk_g1_gp[0] = S["cGP"].v


        ot_dto = (S["dTo0"].v, S["dTo1"].v)

        W("gpsimd", S["cGP"], qblk_g1_gp[0])
        for wdst, wsrc, dd in ((WOUTT, "woutT", 1024),
                               (RTRT, "rtrT", 256), (LEAF, "leaf", 1024)):
            i1 = I("gpsimd", lambda wdst=wdst, wsrc=wsrc, dd=dd: E.dma_start(
                out=wdst[:].rearrange("p (c d) -> p c d", d=dd),
                in_=dram[wsrc].rearrange("(c p) d -> p c d", p=128)))
            inc(i1, S["dW3"], 16)
        dW_ffn = S["dW3"].v

        base_dXB = (S["dXB0"].v, S["dXB1"].v, S["dXB2"].v, S["dXB3"].v)
        base_X1 = S["cX1"].v
        base_V4, base_S3, base_S4 = S["cV4"].v, S["cS3"].v, S["cS4"].v
        base_V6, base_S6, base_XN2 = S["cV6"].v, S["cS6"].v, S["cXN2"].v
        base_V8, base_S7, base_S8, base_OF = S["cV8"].v, S["cS7"].v, S["cS8"].v, S["cOF"].v
        base_dY = (S["dY0"].v, S["dY1"].v)
        base_VSt = S["cVSt"].v
        pipe = {"qt_gp": 0, "z_gp": 0, "sel_gp": 0, "ret_gp": 0, "o2_gp": 0,
                "xn2t": {}, "xn2t_tile": {}}

        def wo_stage(b4):
            gt0 = 4 * b4
            for tt in range(4):
                t = gt0 + tt
                r = t % 4
                W("gpsimd", S["cX1"], base_X1 + t - 3)
                if t < 4:
                    W("gpsimd", S["cG1"], S["cG1"].v)
                    W("gpsimd", S["cS1"], S["cS1"].v)
                    W("gpsimd", S["cLN"], S["cLN"].v)
                i1 = I("gpsimd", lambda t=t, r=r: E.dma_start(
                    out=XINA[:, r, :], in_=xd[s, 128 * t:128 * (t + 1), :]))
                inc(i1, S[f"dXB{r}"], 16)
            for tt in range(4):
                t = gt0 + tt
                ro = t % 4
                tile_banks = []
                for e in range(2):
                    bank = f4_acquire()
                    if tt == 0 and e == 0:
                        for g in range(2):
                            W("tensor", S[ot_rec[(g, b4)][0]], ot_rec[(g, b4)][1])
                        if b4 == 0:
                            W("tensor", S["dC"], 16 * 10)
                            W("tensor", S["cE"], S["cE"].v)
                            W("tensor", S["cQBE"], S["cQBE"].v)
                            W("tensor", S["cNRM"], S["cNRM"].v)
                    W("tensor", S["dW2"], dW_wo)
                    i1 = I("tensor", lambda e=e, bank=bank: E.matmul(
                        pF2[bank // 2][:, 512 * (bank % 2):512 * (bank % 2) + 512],
                        ONES2[:, :], BOHL[:, 512 * e:512 * (e + 1)],
                        start=True, stop=False))
                    for c in range(NCH):
                        i1 = I("tensor", lambda c=c, t=t, e=e, bank=bank: E.matmul(
                            pF2[bank // 2][:, 512 * (bank % 2):512 * (bank % 2) + 512],
                            OT[:, N * c + 128 * t:N * c + 128 * (t + 1)],
                            WO_SB[:, 1024 * c + 512 * e:1024 * c + 512 * (e + 1)],
                            start=False, stop=(c == NCH - 1)))
                    inc(i1, S["cGP"])
                    tile_banks.append(bank)
                assert tile_banks[0] % 2 == 0 and tile_banks[1] == tile_banks[0] + 1
                mmv = S["cGP"].v
                if b4 == 0 and tt == 0:
                    W("scalar", S["cNRM"], S["cNRM"].v)
                    W("scalar", S["cVG"], S["cVG"].v)
                    W("scalar", S["cTPg"], S["cTPg"].v)
                if b4 == 1 and tt == 0:
                    W("scalar", S["cAV"], S["cAV"].v)
                    W("scalar", S["cM"], S["cM"].v)
                    W("scalar", S["cNRM"], S["cNRM"].v)
                    W("scalar", S["dTo0"], ot_dto[0])
                    W("scalar", S["dTo1"], ot_dto[1])
                W("vector", S["cGP"], mmv)
                W("vector", S["cV4"], base_V4 + t - 3)
                W("vector", S["cS4"], S["cS4"].v - 3)
                iR = I("vector", lambda r4=t % 4, pr=tile_banks[0] // 2: E.reduce_sum(
                    out=lnt(28 + r4), in_=pF2[pr][:], axis=AXX))
                inc(iR, S["cG2"])
                for e in range(2):
                    bank = tile_banks[e]
                    W("vector", S["cGP"], mmv)
                    W("vector", S["cVSt"], base_VSt + t - 3)
                    i2 = I("vector", lambda e=e, ro=ro, bank=bank: E.tensor_copy(
                        out=OATT[:, ro, 512 * e:512 * (e + 1)],
                        in_=pF2[bank // 2][:, 512 * (bank % 2):512 * (bank % 2) + 512]))
                    f4_release(i2, bank)
                for sm, val in f4_pairs(f4_rec()):
                    W("scalar", sm, val)
                W("scalar", S["cE"], S["cE"].v)
                W("scalar", S["cS1"], S["cS1"].v)
                W("scalar", S["cS3"], S["cS3"].v)
                W("scalar", S["cS5"], S["cS5"].v)
                W("scalar", S["cS7"], S["cS7"].v)
                swait()
                sq_dmp = SQS if b4 == 0 else SQD
                iS = I("scalar", lambda ro=ro, r4=t % 4, sq_dmp=sq_dmp: E.activation(
                    out=sq_dmp[:], in_=OATT[:, ro, :], func=AF.Square,
                    accum_out=lnt(32 + r4)))
                inc(iS, S["cS3"])
            for tt in range(4):
                t = gt0 + tt
                ln_finish(t % 4, 28, (S["cG2"], S["cS3"], S["cV4"], S["cS4"]))
            for tt in range(4):
                t = gt0 + tt
                r = t % 4
                W("vector", S["cS4"], base_S4 + t + 1)
                W("vector", S[f"dXB{r}"], base_dXB[r] + 16 * (t // 4 + 1))
                W("vector", S["cX1"], S["cX1"].v)
                W("vector", S["cOF"], S["cOF"].v)
                i1 = I("vector", lambda r=r, r4=t % 4: E.tensor_scalar(
                    out=VTMP[:], in0=OATT[:, r, :], scalar1=lnt(36 + r4),
                    scalar2=lnt(52 + r4), op0=ALU.subtract, op1=ALU.mult))
                inc(i1, S["cVSt"])
                W("vector", S["cVSt"], S["cVSt"].v)
                i2 = I("vector", lambda t=t, r=r: E.tensor_add(
                    out=X1[:, t % 8, :], in0=XINA[:, r, :], in1=VTMP[:]))
                inc(i2, S["cX1"])
            for tt in range(4):
                t = gt0 + tt
                ln_sums(X1[:, t % 8, :], t % 4, 56,
                        (S["cG3"], S["cS5"], S["cV6"], S["cS6"]),
                        [(S["cX1"], base_X1 + t + 1), (S["cV6"], base_V6 + t - 3)],
                        dump=(SQS if b4 == 0 else None))
            xtd_prev = f"dTf{(b4 - 1) % 2}"
            for tt in range(4):
                t = gt0 + tt
                r = t % 4
                ln_finish(r, 56, (S["cG3"], S["cS5"], S["cV6"], S["cS6"]),
                          negmu=True, war=("cXN2", base_XN2 + t - 3))
                if t >= 4:
                    W("scalar", S[xtd_prev], pipe["xn2t"][b4 - 1][1])
                else:
                    W("scalar", S[dta_tile[12 + r][0]], dta_tile[12 + r][1])
                swait()
                i1 = I("scalar", lambda r=r, t=t, r4=t % 4: E.activation(
                    out=XN[:, r, :], in_=X1[:, t % 8, :], func=AF.Identity,
                    scale=lnt(80 + r4), bias=lnt(76 + r4)))
                inc(i1, S["cXN2"])
            xtd = f"dTf{b4 % 2}"
            for tt in range(4):
                t = gt0 + tt
                r = t % 4
                W("sync", S["cXN2"], base_XN2 + t + 1)
                i1 = I("sync", lambda r=r, tt=tt, b4=b4: E.dma_start_transpose(
                    XN2T[:, b4 % 2, :].rearrange("p (c q f) -> p c q f", q=4, f=128)[:, :, tt, :],
                    XN[:, r, :].rearrange("p (c f) -> p c f", f=128)))
                inc(i1, S[xtd], 16)
            pipe["xn2t"][b4] = (xtd, S[xtd].v)

        def ffn_stage(b4, hook=None):
            gt0 = 4 * b4
            xslot = b4 % 2
            for e in range(NCH):
                k, bank = gen_acquire()
                W("tensor", S[pipe["xn2t"][b4][0]], pipe["xn2t"][b4][1])
                W("tensor", S["dW2"], dW_wo)
                W("tensor", S["dW3"], dW_ffn)
                i1 = None
                for c in range(NCH):
                    i1 = I("tensor", lambda c=c, e=e, bank=bank, xslot=xslot: E.matmul(
                        pG[bank][:],
                        A_W1[:, 1024 * c + 128 * e:1024 * c + 128 * e + 128],
                        XN2T[:, xslot, 512 * c:512 * (c + 1)],
                        start=(c == 0), stop=(c == NCH - 1)))
                inc(i1, S["cGP"])
                W("vector", S["cGP"], S["cGP"].v)
                W("vector", S["cGP"], pipe["z_gp"])
                i2 = I("vector", lambda e=e, bank=bank: E.tensor_copy(
                    out=QT[:, 512 * e:512 * (e + 1)], in_=pG[bank][:]))
                gen_release(i2, bank)
            pipe["qt_gp"] = S["cGP"].v
            qt_rdy = [("cGE0", S["cGE0"].v), ("cGE1", S["cGE1"].v)]

            for nch2 in range(2):
                bank = f4_acquire()
                for nm, vv in qt_rdy:
                    W("tensor", S[nm], vv)
                i1 = None
                for c in range(NCH):
                    i1 = I("tensor", lambda c=c, nch2=nch2, bank=bank: E.matmul(
                        pF[bank][:],
                        RTRT[:, 256 * c + 128 * nch2:256 * c + 128 * (nch2 + 1)],
                        QT[:, 512 * c:512 * (c + 1)],
                        start=(c == 0), stop=(c == NCH - 1)))
                inc(i1, S["cGP"])
                zv = S["cGP"].v
                W("scalar", S["cGP"], zv)
                W("scalar", S["cHL"], S["cHL"].v)
                if nch2 == 0:
                    W("scalar", S["cGP"], pipe["sel_gp"])
                i1 = I("scalar", lambda bank=bank: E.activation(
                    out=SPE[:, 0, :], in_=pF[bank][:], func=AF.Exp, scale=1.0))
                schain(i1)
                swait()
                i3 = I("scalar", lambda nch2=nch2: E.activation(
                    out=SPL[:, nch2, :], in_=SPE[:, 0, :], func=AF.Ln, bias=oneap))
                inc(i3, S["cHL"])
                W("vector", S["cGP"], zv)
                W("vector", S["cHL"], S["cHL"].v)
                if nch2 == 0:
                    W("vector", S["cGP"], pipe["sel_gp"])
                iv = I("vector", lambda nch2=nch2, bank=bank: E.tensor_sub(
                    out=SPL[:, nch2 + 2, :], in0=SPL[:, nch2, :], in1=pF[bank][:]))
                f4_release(iv, bank)
            pipe["z_gp"] = S["cGP"].v
            z_rdy = f4_rec()

            for mh in range(2):
                bank = f4_acquire()
                f4_wait("tensor", z_rdy)
                i1 = None
                for c in range(4):
                    i1 = I("tensor", lambda c=c, mh=mh, bank=bank: E.matmul(
                        pF[bank][:],
                        SEL[:, 256 * c + 128 * mh:256 * c + 128 * (mh + 1)],
                        SPL[:, c, :],
                        start=(c == 0), stop=(c == 3)))
                inc(i1, S["cGP"])
                W("scalar", S["cGP"], S["cGP"].v)
                if mh == 0:
                    W("scalar", S["cGP"], pipe["ret_gp"])
                i2 = I("scalar", lambda mh=mh, bank=bank: E.activation(
                    out=LPT[:, 512 * mh:512 * (mh + 1)], in_=pF[bank][:], func=AF.Exp))
                f4_release(i2, bank)
            pipe["sel_gp"] = S["cGP"].v
            lp_rdy = f4_rec()

            for dc in range(NCH):
                bank = f4_acquire()
                f4_wait("tensor", lp_rdy)
                i1 = None
                for mh in range(2):
                    i1 = I("tensor", lambda mh=mh, dc=dc, bank=bank: E.matmul(
                        pF[bank][:],
                        LEAF[:, 1024 * mh + 128 * dc:1024 * mh + 128 * dc + 128],
                        LPT[:, 512 * mh:512 * (mh + 1)],
                        start=(mh == 0), stop=(mh == 1)))
                inc(i1, S["cGP"])
                W("vector", S["cGP"], S["cGP"].v)
                if dc == 0:
                    W("vector", S["cGP"], pipe["o2_gp"])
                i2 = I("vector", lambda dc=dc, bank=bank: E.tensor_copy(
                    out=RETT[:, 512 * dc:512 * (dc + 1)], in_=pF[bank][:]))
                f4_release(i2, bank)
            pipe["ret_gp"] = S["cGP"].v
            ret_rdy = f4_rec()
            if hook is not None:
                hook()

            for tt in range(4):
                t = gt0 + tt
                ro = t % 4
                tile_banks = []
                for e in range(2):
                    bank = f4_acquire()
                    f4_wait("tensor", ret_rdy)
                    i1 = I("tensor", lambda e=e, bank=bank: E.matmul(
                        pF2[bank // 2][:, 512 * (bank % 2):512 * (bank % 2) + 512],
                        ONES2[:, :], WOBHL[:, 512 * e:512 * (e + 1)],
                        start=True, stop=False))
                    for c in range(NCH):
                        i1 = I("tensor", lambda c=c, tt=tt, e=e, bank=bank: E.matmul(
                            pF2[bank // 2][:, 512 * (bank % 2):512 * (bank % 2) + 512],
                            RETT[:, 512 * c + 128 * tt:512 * c + 128 * (tt + 1)],
                            WOUTT[:, 1024 * c + 512 * e:1024 * c + 512 * (e + 1)],
                            start=False, stop=(c == NCH - 1)))
                    inc(i1, S["cGP"])
                    tile_banks.append(bank)
                assert tile_banks[0] % 2 == 0 and tile_banks[1] == tile_banks[0] + 1
                mmv = S["cGP"].v
                for e in range(2):
                    bank = tile_banks[e]
                    W("vector", S["cGP"], mmv)
                    W("vector", S["cOF"], base_OF + t - 3)
                    W("vector", S["cS7"], base_S7 + t - 3)
                    i2 = I("vector", lambda e=e, ro=ro, bank=bank: E.tensor_copy(
                        out=OUT2[:, ro, 512 * e:512 * (e + 1)],
                        in_=pF2[bank // 2][:, 512 * (bank % 2):512 * (bank % 2) + 512]))
                    f4_release(i2, bank)
                ln_sums(OUT2[:, t % 4, :], t % 4, 84,
                        (S["cG4"], S["cS7"], S["cV8"], S["cS8"]),
                        f4_pairs(f4_rec()) + [(S["cV8"], base_V8 + t - 3)])
            for tt in range(4):
                t = gt0 + tt
                ln_finish(t % 4, 84, (S["cG4"], S["cS7"], S["cV8"], S["cS8"]))
            for tt in range(4):
                t = gt0 + tt
                r = t % 2
                W("vector", S["cS8"], base_S8 + t + 1)
                W("vector", S["dY0" if r == 0 else "dY1"], base_dY[r] + 16 * (t // 2))
                W("vector", S["cX1"], S["cX1"].v)
                W("vector", S["cOF"], S["cOF"].v)
                i1 = I("vector", lambda ro=t % 4, r4=t % 4: E.tensor_scalar(
                    out=VTMP[:], in0=OUT2[:, ro, :], scalar1=lnt(92 + r4),
                    scalar2=lnt(108 + r4), op0=ALU.subtract, op1=ALU.mult))
                vchain(i1)
                vwait()
                i2 = I("vector", lambda t=t, r=r: E.tensor_add(
                    out=OUTF[:, r, :], in0=X1[:, t % 8, :], in1=VTMP[:]))
                inc(i2, S["cOF"])
                W("sync", S["cOF"], S["cOF"].v)
                i3 = I("sync", lambda t=t, r=r: E.dma_start(
                    out=yd[s, 128 * t:128 * (t + 1), :], in_=OUTF[:, r, :]))
                inc(i3, S["dY0" if r == 0 else "dY1"], 16)
            pipe["o2_gp"] = S["cGP"].v

        wo_stage(0)
        wo_stage(1)
        ffn_stage(0)
        wo_stage(2)
        ffn_stage(1)
        wo_stage(3)
        wo3_gp = S["cGP"].v
        if s + 1 < nseq:
            PRO[0] = phi1a_prologue(s + 1, base_X1, S["dTf1"].v)
            PRO[0]["dma"](0)
        ffn_stage(2)
        if s + 1 < nseq:
            PRO[0]["sums"](0)

        def f3_hook():
            if s + 1 < nseq:
                PRO[0]["fin"](0)
                PRO[0]["xnt"](0, S["cGP"].v)
        ffn_stage(3, hook=f3_hook)
        xn2t_prev[0] = S["dTf1"].v
        PREV.update(wo3=wo3_gp, qt3=pipe["qt_gp"], ret3=pipe["ret_gp"],
                    o23=pipe["o2_gp"], x1b=base_X1, s3=S["cS3"].v, of=S["cOF"].v)

    W("sync", S["dY0"], 16 * (NT // 2) * nseq)
    W("sync", S["dY1"], 16 * (NT // 2) * nseq)


_NC_CACHE = {}


def _prep_weights(inputs):
    f = np.float32
    routers_pad = np.zeros((256, D), dtype=f)
    routers_pad[:255] = np.asarray(inputs["L0_ffn_routers"], f)
    return {
        "wq": np.ascontiguousarray(np.asarray(inputs["L0_attn_Wq"], f)),
        "wk": np.ascontiguousarray(np.asarray(inputs["L0_attn_Wk"], f)),
        "wv": np.ascontiguousarray(np.asarray(inputs["L0_attn_Wv"], f)),
        "wo": np.ascontiguousarray(np.asarray(inputs["L0_attn_Wo"], f)),
        "bo": np.ascontiguousarray(np.asarray(inputs["L0_attn_bo"], f).reshape(1, D)),
        "wqfT": np.ascontiguousarray(np.asarray(inputs["L0_ffn_Wq"], f).T),
        "rtrT": np.ascontiguousarray(routers_pad.T),
        "leaf": np.ascontiguousarray(np.asarray(inputs["L0_ffn_leaf_values"], f)),
        "woutT": np.ascontiguousarray(np.asarray(inputs["L0_ffn_Wout_w"], f).T),
        "wob": np.ascontiguousarray(np.asarray(inputs["L0_ffn_Wout_b"], f).reshape(1, D)),
    }


def kernel(**inputs):
    if "nc" not in _NC_CACHE:
        _NC_CACHE["nc"] = build_nc()
    nc = _NC_CACHE["nc"]
    x = np.asarray(inputs["L0_attn_x"], np.float32)
    w = _prep_weights(inputs)
    in_maps = []
    for c in range(NCORES):
        m = dict(w)
        m["x"] = np.ascontiguousarray(x[c * NSEQ:(c + 1) * NSEQ])
        in_maps.append(m)
    res = run_bass_kernel_spmd(nc, in_maps, core_ids=list(range(NCORES)))
    y = np.concatenate([r["y"] for r in res.results], axis=0)
    return y.reshape(B, N, D)


if __name__ == "__main__":
    nc = build_nc()
    print("built ok, arena bytes:", ARENA_SZ)


# version 61 (final)
# speedup vs baseline: 1.1816x; 1.0091x over previous
import contextlib
import numpy as np
import ml_dtypes

import concourse.bass as bass
import concourse.mybir as mybir
from concourse.bass_utils import run_bass_kernel_spmd

F32 = mybir.dt.float32
BF16 = mybir.dt.bfloat16
U8 = mybir.dt.uint8
AF = mybir.ActivationFunctionType
ALU = mybir.AluOpType
AXX = mybir.AxisListType.X

B, N, D, H, DH = 32, 2048, 1024, 16, 64
NCORES = 8
NSEQ = B // NCORES
NT = N // 128
NBLK = N // 256
NB4 = N // 512
NCH = D // 128
EPS = 1e-5
SCALE = DH ** -0.5

bfarr = lambda a: np.ascontiguousarray(np.asarray(a, dtype=ml_dtypes.bfloat16))


def _consts():
    h = np.arange(1, H + 1, dtype=np.float64)
    slopes = 1.0 / (2.0 ** 8) ** (h / H)
    j = np.arange(N, dtype=np.float64)
    u = np.exp(-slopes[None, :] * j[:, None])
    u_tab = np.zeros((128, NT * H), dtype=np.float32)
    for t in range(NT):
        u_tab[:, t * H:(t + 1) * H] = u[t * 128:(t + 1) * 128, :]
    mask = (np.arange(128)[None, :] >= np.arange(128)[:, None]).astype(np.float32)
    ident = np.eye(128, dtype=np.float32)
    sel = np.zeros((512, 256), dtype=np.float32)
    for lev in range(8):
        n = 1 << lev
        span = 256 // (2 * n)
        for i in range(n):
            r = n - 1 + i
            lo = i * 2 * span
            sel[r, lo:lo + span] = -1.0
            sel[256 + r, lo + span:lo + 2 * span] = -1.0
    sel_pk = np.zeros((128, 4 * 256), dtype=np.float32)
    for c in range(4):
        sel_pk[:, 256 * c:256 * (c + 1)] = sel[128 * c:128 * (c + 1), :]
    ones2 = np.ones((2, 128), dtype=np.float32)
    epsone = np.zeros((128, 2), dtype=np.float32)
    epsone[:, 0] = EPS
    epsone[:, 1] = 1.0
    return dict(u=u_tab, mask=mask, ident=ident, sel=sel_pk, ones2=ones2, epsone=epsone)


class Arena:
    def __init__(self, nc, ctx, name, size_bytes):
        self.nc = nc
        self.h = ctx.enter_context(nc.sbuf_tensor(name, [128, size_bytes], U8))
        self.base = nc.lookup_mloc(self.h).addr
        self.size = size_bytes

    def t(self, name, off, shape, dtype):
        per_part = int(np.prod(shape[1:])) * mybir.dt.size(dtype)
        assert off + per_part <= self.size, f"{name}: {off}+{per_part} > {self.size}"
        hnd = self.nc.virtual_sbuf_tensor(name, list(shape), dtype)
        self.nc._tensor(hnd.name, hnd.shape, hnd.dtype, type="SB")
        mloc = self.nc.lookup_mloc(hnd)
        mloc.allocated = not self.nc.target_bir_lowering
        mloc.addr = self.base + off
        return hnd


class PsumViews:
    def __init__(self, nc, ctx):
        self.nc = nc
        self.h = ctx.enter_context(nc.psum_tensor("psb", [128, 4096], F32))
        self.base = nc.lookup_mloc(self.h).addr

    def t(self, name, byte_off, shape, dtype):
        hnd = self.nc.virtual_psum_tensor(name, list(shape), dtype)
        self.nc._tensor(hnd.name, hnd.shape, hnd.dtype, type="PSUM")
        mloc = self.nc.lookup_mloc(hnd)
        mloc.allocated = not self.nc.target_bir_lowering
        mloc.addr = self.base + byte_off
        return hnd


class Sem:
    __slots__ = ("h", "v", "name")

    def __init__(self, h, name=""):
        self.h = h
        self.v = 0
        self.name = name


PRUNE_SAME = [True]
SAME_ENG_SEMS = {
    "vector": {"cG1", "cG2", "cG3", "cG4", "cV2", "cV4", "cV6", "cV8",
               "cX1", "cOF", "cNRM", "cQBE", "cVSt", "cVC"},
    "scalar": {"cS1", "cS2", "cS3", "cS4", "cS5", "cS6", "cS7", "cS8",
               "cE", "cLN", "cXN2", "cHL", "cSC"},
    "tensor": {"cGP", "cS", "cAV"},
    "gpsimd": {"cM"},
}


SEM_NAMES = [
    "dXA0", "dXA1", "dXA2", "dXA3", "dXB0", "dXB1", "dXB2", "dXB3", "dW", "dW2", "dW3", "dC", "dY0", "dY1",
    "cG1", "cS1", "cV2", "cS2", "cLN",
    "cGP", "cGE0", "cGE1",
    "cS", "cE", "cM", "cAV", "cNRM",
    "cG2", "cS3", "cV4", "cS4", "cX1",
    "cG3", "cS5", "cV6", "cS6", "cXN2",
    "cTPf", "cTEf",
    "cZL", "cHL",
    "cG4", "cS7", "cV8", "cS8", "cOF",
    "cVC", "cSC", "cGC", "cQBE", "cTPo", "cTEo", "cVSt", "cVG", "cRCP", "cTEfV", "cTPg",
    "dTo0", "dTo1", "dTf0", "dTf1", "dTa0", "dTa1", "dTa2", "dTa3",
    "dWk", "dWv", "dWq",
    "sMT", "sMV", "sMS", "sInit",
]

LAYOUT = {}
_off = 0


def _lay(name, nbytes, align=32):
    global _off
    _off = (_off + align - 1) // align * align
    LAYOUT[name] = _off
    _off += nbytes


_lay("OT", 32768)
_lay("XIN", 8192)
_lay("XN", 8192)
_lay("UC", 1024)
_lay("MSK", 256)
_lay("IDN", 256)
_lay("SEL", 4096)
_lay("EPS1", 32)
_lay("BOHL", 2048)
_lay("WOBHL", 2048)
_lay("ONES2", 256)
_lay("VTMP", 2048)
_lay("LNT", 512)
_lay("ONAT", 2048)
_lay("PTM", 2048)
_lay("RCP", 64)
_UB = (_off + 31) // 32 * 32
LAYOUT["A_W0"] = _UB + 0
LAYOUT["A_W1"] = _UB + 16384
LAYOUT["A_W2"] = _UB + 32768
LAYOUT["XNT"] = _UB + 49152
LAYOUT["KG"] = _UB + 81920
LAYOUT["VG"] = _UB + 98304
LAYOUT["QB"] = _UB + 115200
LAYOUT["PT"] = _UB + 123392
_BB = _UB + 32768
_bc = _BB + 16384
def _layB(name, nbytes):
    global _bc
    LAYOUT[name] = _bc
    _bc += nbytes
_layB("RTRT", 4096)
_layB("LEAF", 4096)
_layB("XN2T", 16384)
_layB("QT", 8192)
_layB("SPE", 4096)
_layB("SPL", 8192)
_layB("LPT", 2048)
_layB("RETT", 8192)
_layB("OUT2", 8192)
_layB("OUTF", 8192)
_layB("OATT", 8192)
_layB("X1", 16384)
LAYOUT["XINA"] = LAYOUT["XIN"]
ARENA_SZ = max(LAYOUT["X1"] + 16384, LAYOUT["PT"] + 8192)
assert ARENA_SZ <= 212800, ARENA_SZ


def build_nc(nseq=NSEQ):
    nc = bass.Bass("TRN2", target_bir_lowering=False)
    C = _consts()

    dram = {}
    dram["x"] = nc.dram_tensor("x", [nseq, N, D], F32, kind="ExternalInput").ap()
    for nm, shp in (("wq", [D, D]), ("wk", [D, D]), ("wv", [D, D]), ("wo", [D, D]),
                    ("bo", [1, D]), ("wqfT", [D, D]), ("rtrT", [D, 256]),
                    ("leaf", [256, D]), ("woutT", [D, D]), ("wob", [1, D])):
        dram[nm] = nc.dram_tensor(nm, shp, F32, kind="ExternalInput").ap()
    dram["y"] = nc.dram_tensor("y", [nseq, N, D], F32, kind="ExternalOutput").ap()
    dram["u_c"] = nc.inline_tensor(C["u"], "u_c").ap()
    dram["mask_c"] = nc.inline_tensor(bfarr(C["mask"]), "mask_c").ap()
    dram["id_c"] = nc.inline_tensor(bfarr(C["ident"]), "id_c").ap()
    dram["sel_c"] = nc.inline_tensor(C["sel"], "sel_c").ap()
    dram["ones2_c"] = nc.inline_tensor(bfarr(C["ones2"]), "ones2_c").ap()
    dram["epsone_c"] = nc.inline_tensor(C["epsone"], "epsone_c").ap()

    ctx = contextlib.ExitStack()
    with ctx:
        ar = Arena(nc, ctx, "arena", ARENA_SZ)
        L = LAYOUT
        T = {}
        T["OT"] = ar.t("OT", L["OT"], [128, NCH * N], BF16)
        
        T["XN"] = ar.t("XN", L["XN"], [128, 4, 1024], BF16)
        T["UC"] = ar.t("UC", L["UC"], [128, 256], F32)
        T["MSK"] = ar.t("MSK", L["MSK"], [128, 128], BF16)
        T["IDN"] = ar.t("IDN", L["IDN"], [128, 128], BF16)
        T["SEL"] = ar.t("SEL", L["SEL"], [128, 1024], F32)
        T["BSBO"] = ar.t("BSBO", L["XIN"], [1, 1024], F32)
        T["BSWOB"] = ar.t("BSWOB", L["XIN"] + 4096, [1, 1024], F32)
        T["LOT"] = ar.t("LOT", L["XN"], [1, 2048], BF16)
        T["SQSB"] = T["LOT"]
        T["EPS1"] = ar.t("EPS1", L["EPS1"], [128, 2], F32)
        T["BOHL"] = ar.t("BOHL", L["BOHL"], [2, 1024], BF16)
        T["WOBHL"] = ar.t("WOBHL", L["WOBHL"], [2, 1024], BF16)
        T["ONES2"] = ar.t("ONES2", L["ONES2"], [2, 128], BF16)

        T["VTMP"] = ar.t("VTMP", L["VTMP"], [128, 1024], BF16)
        T["LNT"] = ar.t("LNT", L["LNT"], [128, 128], F32)
        T["XINA"] = ar.t("XINA", L["XINA"], [128, 4, 1024], BF16)
        T["ONAT"] = ar.t("ONAT", L["ONAT"], [128, 8, 128], BF16)
        T["PTM"] = ar.t("PTM", L["PTM"], [128, 4, 256], BF16)
        T["RCP"] = ar.t("RCP", L["RCP"], [128, 16], F32)
        T["A_W0"] = ar.t("A_W0", L["A_W0"], [128, NCH * 1024], BF16)
        T["A_W1"] = ar.t("A_W1", L["A_W1"], [128, NCH * 1024], BF16)
        T["A_W2"] = ar.t("A_W2", L["A_W2"], [128, NCH * 1024], BF16)
        T["XNT"] = ar.t("XNT", L["XNT"], [128, NCH * N], BF16)
        T["KG"] = ar.t("KG", L["KG"], [128, 4 * N], BF16)
        T["VG"] = ar.t("VG", L["VG"], [128, NT * 528], BF16)
        T["QB"] = ar.t("QB", L["QB"], [128, 2, 2048], BF16)
        T["PT"] = ar.t("PT", L["PT"], [128, 4, 1024], BF16)
        T["SQD"] = ar.t("SQD", L["ONAT"], [128, 1024], F32)
        T["WOUTT"] = ar.t("WOUTT", L["A_W0"], [128, NCH * 1024], BF16)
        T["WO_SB"] = ar.t("WO_SB", L["A_W2"], [128, NCH * 1024], BF16)
        T["RTRT"] = ar.t("RTRT", L["RTRT"], [128, NCH * 256], BF16)
        T["LEAF"] = ar.t("LEAF", L["LEAF"], [128, 2 * 1024], BF16)
        T["XN2T"] = ar.t("XN2T", L["XN2T"], [128, 2, NCH * 512], BF16)
        T["QT"] = ar.t("QT", L["QT"], [128, NCH * 512], BF16)
        T["SPE"] = ar.t("SPE", L["SPE"], [128, 2, 512], F32)
        T["SPL"] = ar.t("SPL", L["SPL"], [128, 4, 512], F32)
        T["LPT"] = ar.t("LPT", L["LPT"], [128, 2 * 512], BF16)
        T["RETT"] = ar.t("RETT", L["RETT"], [128, NCH * 512], BF16)
        T["OUT2"] = ar.t("OUT2", L["OUT2"], [128, 4, 1024], BF16)
        T["OUTF"] = ar.t("OUTF", L["OUTF"], [128, 2, 1024], F32)
        T["OATT"] = ar.t("OATT", L["OATT"], [128, 4, 1024], BF16)
        T["X1"] = ar.t("X1", L["X1"], [128, 8, 1024], BF16)

        pv = PsumViews(nc, ctx)
        T["pS"] = [pv.t(f"pS{i}", 4096 * i, [128, 2, 512], F32) for i in range(2)]
        T["pAV"] = [pv.t(f"pAV{i}", 8192 + 2048 * i, [128, 264], F32) for i in range(2)]
        T["pG"] = [pv.t(f"pG{i}", 12288 + 2048 * i, [128, 512], F32) for i in range(2)]
        T["pGb"] = [pv.t(f"pGb{i}", 12288 + 2048 * i, [128, 128], BF16) for i in range(2)]
        T["pTb"] = [pv.t(f"pTb{i}", 8192 + 2048 * i, [128, 128], BF16) for i in range(2)]
        T["pTa"] = [pv.t(f"pTa{i}", 2048 * i, [128, 128], BF16) for i in range(4)]
        T["pS2"] = [pv.t(f"pS2{i}", 4096 * i, [128, 2, 512], F32) for i in range(2)]
        T["pS4"] = pv.t("pS4", 0, [128, 4, 512], F32)
        T["pF"] = [pv.t(f"pF{i}", 2048 * i, [128, 512], F32) for i in range(4)]
        T["pBV"] = [pv.t(f"pBV{i}", 8192 + 2048 * i, [128, 512], F32) for i in range(2)]
        T["pF2"] = [pv.t(f"pF2{i}", 4096 * i, [128, 1024], F32) for i in range(2)]
        T["pAV2"] = [pv.t(f"pAV2{i}", 8192 + 4096 * i, [128, 2, 512], F32) for i in range(2)]
        T["pQB"] = [pv.t(f"pQB{i}", 4096 + 2048 * i, [128, 512], F32) for i in range(2)]
        T["pTo"] = [pv.t(f"pTo{i}", 8192 + 2048 * i, [128, 128], BF16) for i in range(4)]
        T["SQS"] = pv.t("SQSp", 8192, [128, 1024], F32)

        SM = {n: Sem(ctx.enter_context(nc.semaphore(n)), n) for n in SEM_NAMES}

        blk = ctx.enter_context(nc.Block())
        for role, starter in (("sync", blk.sync), ("gpsimd", blk.gpsimd),
                              ("vector", blk.vector), ("scalar", blk.scalar),
                              ("tensor", blk.tensor)):
            for s in SM.values():
                s.v = 0
            starter(lambda e, r=role: emit(r, e, T, dram, SM, nseq))

    return nc


def emit(role, E, T, dram, S, nseq):
    OT, XN, UC, MSK, IDN, SEL = (T["OT"], T["XN"], T["UC"],
                                 T["MSK"], T["IDN"], T["SEL"])
    BSBO, BSWOB, LOT = T["BSBO"], T["BSWOB"], T["LOT"]
    EPS1, BOHL, WOBHL, ONES2 = T["EPS1"], T["BOHL"], T["WOBHL"], T["ONES2"]
    SQD = T["SQD"]
    SQS, VTMP, LNT, ONAT, PTM, RCP = (T["SQS"], T["VTMP"], T["LNT"],
                                      T["ONAT"], T["PTM"], T["RCP"])
    XINA = T["XINA"]
    A_W0, A_W1, A_W2, XNT, KG, VG, QB, PT = (T["A_W0"], T["A_W1"], T["A_W2"],
                                             T["XNT"], T["KG"], T["VG"],
                                             T["QB"], T["PT"])
    WOUTT, RTRT, LEAF, XN2T, QT = T["WOUTT"], T["RTRT"], T["LEAF"], T["XN2T"], T["QT"]
    WO_SB = T["WO_SB"]
    SPE, SPL, LPT, RETT = T["SPE"], T["SPL"], T["LPT"], T["RETT"]
    OUT2, OUTF, OATT, X1 = T["OUT2"], T["OUTF"], T["OATT"], T["X1"]
    pS, pAV, pG, pGb, pTb, pTa = T["pS"], T["pAV"], T["pG"], T["pGb"], T["pTb"], T["pTa"]
    pS2, pAV2, pQB, pTo = T["pS2"], T["pAV2"], T["pQB"], T["pTo"]
    pS4, pF, pF2 = T["pS4"], T["pF"], T["pF2"]
    pBV = T["pBV"]
    xd, yd = dram["x"], dram["y"]

    def I(name, f):
        return f() if role == name else None

    def inc(ins, sem, k=1):
        if ins is not None:
            ins.then_inc(sem.h, k)
        sem.v += k

    _whw = {}

    def W(name, sem, val):
        if role == name and val > 0:
            if _whw.get(id(sem), -1) >= val:
                return
            _whw[id(sem)] = val
            E.wait_ge(sem.h, val)

    genk = [0]

    def gen_acquire():
        k = genk[0]
        genk[0] += 1
        bank = k % 2
        W("tensor", S["cGE0"] if bank == 0 else S["cGE1"], k // 2)
        return k, bank

    def gen_release(ins, bank):
        inc(ins, S["cGE0"] if bank == 0 else S["cGE1"])

    def ge_rec():
        return (S["cGE0"].v, S["cGE1"].v)

    def wait_ge_rec(name, rec):
        W(name, S["cGE0"], rec[0])
        W(name, S["cGE1"], rec[1])

    bvk = [0]
    BVS = ("cVG", "cTPg")

    def bv_acquire():
        k = bvk[0]
        bvk[0] += 1
        bank = k % 2
        W("tensor", S[BVS[bank]], k // 2)
        return bank

    def bv_release(ins, bank):
        inc(ins, S[BVS[bank]])

    f4k = [0]
    F4S = ("cTPf", "cTEf", "cTPo", "cTEo")
    f4_extra = [None] * 4

    def f4_acquire():
        k = f4k[0]
        f4k[0] += 1
        bank = k % 4
        W("tensor", S[F4S[bank]], k // 4)
        if f4_extra[bank] is not None:
            W("tensor", S[f4_extra[bank][0]], f4_extra[bank][1])
            f4_extra[bank] = None
        return bank

    def f4_release(ins, bank):
        inc(ins, S[F4S[bank]])

    def f4_rec():
        return tuple(S[nm].v for nm in F4S)

    def f4_wait(name, rec):
        for nm, vv in zip(F4S, rec):
            W(name, S[nm], vv)

    def f4_pairs(rec):
        return [(S[nm], vv) for nm, vv in zip(F4S, rec)]

    def lnt(col):
        return LNT[:, col:col + 1]

    epsap = EPS1[:, 0:1]
    oneap = EPS1[:, 1:2]

    def vchain(ins):
        inc(ins, S["cVC"])

    def vwait():
        W("vector", S["cVC"], S["cVC"].v)

    def schain(ins):
        inc(ins, S["cSC"])

    def swait():
        W("scalar", S["cSC"], S["cSC"].v)

    def ln_sums(src_ap, r, colbase, sems, extra_waits, extra_waits_s=None, dump=None):
        cR, cSq = sems[0], sems[1]
        for sm, val in extra_waits:
            W("vector", sm, val)
        W("vector", sems[3], sems[3].v - 3)
        i1 = I("vector", lambda: E.reduce_sum(out=lnt(colbase + 0 + r), in_=src_ap, axis=AXX))
        inc(i1, cR)
        for sm, val in (extra_waits_s if extra_waits_s is not None else extra_waits):
            W("scalar", sm, val)
        W("scalar", S["cE"], S["cE"].v)
        W("scalar", S["cS1"], S["cS1"].v)
        W("scalar", S["cS3"], S["cS3"].v)
        W("scalar", S["cS5"], S["cS5"].v)
        W("scalar", S["cS7"], S["cS7"].v)
        swait()
        dmp = SQD if dump is None else dump
        i1 = I("scalar", lambda: E.activation(out=dmp[:], in_=src_ap,
                                              func=AF.Square, accum_out=lnt(colbase + 4 + r)))
        inc(i1, cSq)

    def ln_sums_psum(pair_ap, r, colbase, sems, mm_val, vring):
        cR, cSq = sems[0], sems[1]
        W("vector", S["cGP"], mm_val)
        for sm, val in vring:
            W("vector", sm, val)
        W("vector", sems[3], sems[3].v - 3)
        i1 = I("vector", lambda: E.reduce_sum(out=lnt(colbase + 0 + r), in_=pair_ap, axis=AXX))
        inc(i1, cR)
        W("scalar", S["cGP"], mm_val)
        W("scalar", S["cE"], S["cE"].v)
        W("scalar", S["cS1"], S["cS1"].v)
        W("scalar", S["cS3"], S["cS3"].v)
        W("scalar", S["cS5"], S["cS5"].v)
        W("scalar", S["cS7"], S["cS7"].v)
        swait()
        i1 = I("scalar", lambda: E.activation(out=SQD[:], in_=pair_ap,
                                              func=AF.Square, accum_out=lnt(colbase + 4 + r)))
        inc(i1, cSq)

    def ln_finish(r, colbase, sems, negmu=False, war=None):
        cR, cSq, cV, cRs = sems
        W("vector", cSq, cSq.v)
        W("vector", cR, cR.v)
        if war is not None:
            W("vector", S[war[0]], war[1])
        vwait()
        i1 = I("vector", lambda: E.tensor_scalar(out=lnt(colbase + 8 + r), in0=lnt(colbase + 0 + r),
                                                 scalar1=(-1.0 if negmu else 1.0) / D,
                                                 scalar2=None, op0=ALU.mult))
        vchain(i1)
        i2 = I("vector", lambda: E.tensor_scalar(out=lnt(colbase + 20 + r), in0=lnt(colbase + 0 + r),
                                                 scalar1=-1.0 / (D * D), scalar2=lnt(colbase + 0 + r),
                                                 op0=ALU.mult, op1=ALU.mult))
        vchain(i2)
        vwait()
        i3 = I("vector", lambda: E.scalar_tensor_tensor(out=lnt(colbase + 12 + r), in0=lnt(colbase + 4 + r),
                                                        scalar=1.0 / D, in1=lnt(colbase + 20 + r),
                                                        op0=ALU.mult, op1=ALU.add))
        inc(i3, cV)
        W("scalar", cV, cV.v)
        i4 = I("scalar", lambda: E.activation(out=lnt(colbase + 16 + r), in_=lnt(colbase + 12 + r),
                                              func=AF.Ln, bias=epsap))
        schain(i4)
        swait()
        i5 = I("scalar", lambda: E.activation(out=lnt(colbase + 24 + r), in_=lnt(colbase + 16 + r),
                                              func=AF.Exp, scale=-0.5))
        inc(i5, cRs)
        if negmu:
            W("scalar", cRs, cRs.v)
            i6 = I("scalar", lambda: E.activation(out=lnt(colbase + 20 + r), in_=lnt(colbase + 8 + r),
                                                  func=AF.Identity, scale=lnt(colbase + 24 + r),
                                                  bias=0.0))
            schain(i6)

    for dst, src in ((UC, "u_c"), (MSK, "mask_c"), (IDN, "id_c"), (SEL, "sel_c"),
                     (ONES2, "ones2_c"), (EPS1, "epsone_c")):
        i1 = I("sync", lambda dst=dst, src=src: E.dma_start(out=dst[:], in_=dram[src]))
        inc(i1, S["dC"], 16)
    i1 = I("sync", lambda: E.dma_start(out=BSBO[:], in_=dram["bo"]))
    inc(i1, S["dC"], 16)
    i1 = I("sync", lambda: E.dma_start(out=BSWOB[:], in_=dram["wob"]))
    inc(i1, S["dC"], 16)
    W("scalar", S["dC"], 16 * 8)
    i1 = I("scalar", lambda: E.copy(out=BOHL[0:1, :], in_=BSBO[:]))
    i2 = I("scalar", lambda: E.copy(out=WOBHL[0:1, :], in_=BSWOB[:]))
    inc(i2, S["sInit"])
    W("vector", S["sInit"], 1)
    W("vector", S["dC"], 16 * 8)
    i1 = I("vector", lambda: E.tensor_sub(out=LOT[0:1, 0:1024], in0=BSBO[:], in1=BOHL[0:1, :]))
    i2 = I("vector", lambda: E.tensor_sub(out=LOT[0:1, 1024:2048], in0=BSWOB[:], in1=WOBHL[0:1, :]))
    inc(i2, S["sInit"])
    W("sync", S["sInit"], 2)
    i1 = I("sync", lambda: E.dma_start(out=BOHL[1:2, :], in_=LOT[0:1, 0:1024]))
    inc(i1, S["dC"], 16)
    i1 = I("sync", lambda: E.dma_start(out=WOBHL[1:2, :], in_=LOT[0:1, 1024:2048]))
    inc(i1, S["dC"], 16)
    W("gpsimd", S["dC"], 16 * 8)

    def emit_kv(grp, nb, tef_need, dw_need, f4=False, k_gen2=False,
                k_first=False, av_gate=0):
        def acq():
            if f4:
                return f4_acquire()
            return gen_acquire()[1]

        def rel(ins, bank):
            if f4:
                f4_release(ins, bank)
            else:
                gen_release(ins, bank)

        def pb(bank):
            return pF[bank] if f4 else pG[bank]
        def emit_v():
         if av_gate:
            W("vector", S["cAV"], av_gate)
         for ti, t in enumerate(range(4 * nb, 4 * nb + 4)):
            bank = acq()
            W("tensor", S[tef_need[ti][0]], tef_need[ti][1])
            W("tensor", S[dw_need[1][0]], dw_need[1][1])
            i1 = None
            for c in range(NCH):
                i1 = I("tensor", lambda c=c, t=t, grp=grp, bank=bank: E.matmul(
                    pb(bank)[:],
                    XNT[:, 4096 * (t // 4) + 512 * c + 128 * (t % 4):4096 * (t // 4) + 512 * c + 128 * (t % 4) + 128],
                    A_W2[:, 1024 * c + 512 * grp:1024 * c + 512 * grp + 512],
                    start=(c == 0), stop=(c == NCH - 1)))
            inc(i1, S["cGP"])
            W("vector", S["cGP"], S["cGP"].v)
            for hh in range(8):
                I("vector", lambda t=t, hh=hh, grp=grp, bank=bank: E.tensor_scalar(
                    out=VG[:, 528 * t + 66 * hh:528 * t + 66 * hh + 64],
                    in0=pb(bank)[:, 64 * hh:64 * hh + 64],
                    scalar1=UC[:, H * t + 8 * grp + hh:H * t + 8 * grp + hh + 1],
                    scalar2=None, op0=ALU.mult))
            i2 = I("vector", lambda t=t, grp=grp: E.tensor_copy(
                out=VG[:, 528 * t + 64:528 * t + 527:66],
                in_=UC[:, H * t + 8 * grp:H * t + 8 * grp + 8]))
            rel(i2, bank)
        def emit_k():
         for c2 in range(4):
            use_gen = k_gen2 and c2 < 2
            use_bv = k_gen2 and c2 >= 2
            if k_first and c2 == 0:
                for nm, vv in tef_need:
                    W("tensor", S[nm], vv)
            if use_gen:
                k, bank = gen_acquire()
                dst = pG[bank]
            elif use_bv:
                bank = bv_acquire()
                dst = pBV[bank]
            else:
                bank = acq()
                dst = pb(bank)
            if c2 == 0:
                W("tensor", S[dw_need[0][0]], dw_need[0][1])
            i1 = None
            for c in range(NCH):
                i1 = I("tensor", lambda c=c, c2=c2, nb=nb, grp=grp, dst=dst: E.matmul(
                    dst[:],
                    A_W1[:, 1024 * c + 128 * (grp * 4 + c2):1024 * c + 128 * (grp * 4 + c2) + 128],
                    XNT[:, 4096 * nb + 512 * c:4096 * nb + 512 * c + 512],
                    start=(c == 0), stop=(c == NCH - 1)))
            inc(i1, S["cGP"])
            W("vector", S["cGP"], S["cGP"].v)
            i2 = I("vector", lambda c2=c2, nb=nb, dst=dst: E.tensor_copy(
                out=KG[:, N * c2 + 512 * nb:N * c2 + 512 * (nb + 1)], in_=dst[:]))
            if use_gen:
                gen_release(i2, bank)
            elif use_bv:
                bv_release(i2, bank)
            else:
                rel(i2, bank)
        if k_first:
            emit_k()
            emit_v()
        else:
            emit_v()
            emit_k()

    qbe_rec = {0: 0, 1: 0}

    def emit_qblk(grp, b, qb_rdy_blk, in_attention, tef_need=None, dw_need=0, f4=False):
        for c2 in range(4):
            if in_attention:
                bank = c2 % 2
                W("tensor", S["cE"], S["cE"].v)
                W("tensor", S["cQBE"], qbe_rec[bank])
                dst = pQB[bank]
            else:
                bank = f4_acquire() if f4 else gen_acquire()[1]
                if tef_need:
                    for nm, vv in tef_need:
                        W("tensor", S[nm], vv)
                if dw_need:
                    W("tensor", S[dw_need[0]], dw_need[1])
                dst = pF[bank] if f4 else pG[bank]
            i1 = None
            for c in range(NCH):
                i1 = I("tensor", lambda c=c, c2=c2, b=b, grp=grp, dst=dst: E.matmul(
                    dst[:],
                    A_W0[:, 1024 * c + 128 * (grp * 4 + c2):1024 * c + 128 * (grp * 4 + c2) + 128],
                    XNT[:, 4096 * b + 512 * c:4096 * b + 512 * c + 512],
                    start=(c == 0), stop=(c == NCH - 1)))
            inc(i1, S["cGP"])
            W("vector", S["cGP"], S["cGP"].v)
            i2 = I("vector", lambda c2=c2, b=b, dst=dst: E.tensor_copy(
                out=QB[:, b % 2, 512 * c2:512 * (c2 + 1)], in_=dst[:]))
            if in_attention:
                inc(i2, S["cQBE"])
                qbe_rec[bank] = S["cQBE"].v
                qb_rdy_blk[(grp, b, c2)] = [("cQBE", S["cQBE"].v)]
            else:
                if f4:
                    f4_release(i2, bank)
                    qb_rdy_blk[(grp, b, c2)] = [(nm, vv) for nm, vv in zip(F4S, f4_rec())]
                else:
                    gen_release(i2, bank)
                    qb_rdy_blk[(grp, b, c2)] = [("cGE0", S["cGE0"].v), ("cGE1", S["cGE1"].v)]

    ot_blk = [0]
    ot_prev = [None]
    xn2t_prev = [0]
    PREV = {}
    PRO = [None]

    def phi1a_prologue(s2, x1b_gate, dtf1_gate):
        base_LN, base_S1 = S["cLN"].v, S["cS1"].v
        base_V2, base_S2 = S["cV2"].v, S["cS2"].v
        base_G1 = S["cG1"].v
        base_dXA = (S["dXA0"].v, S["dXA1"].v, S["dXA2"].v, S["dXA3"].v)
        dta_tile = {}

        def p1_dma(g4):
            for tt in range(4):
                t = 4 * g4 + tt
                r = t % 4
                W("gpsimd", S["cLN"], base_LN + t - 3)
                W("gpsimd", S["cS1"], base_S1 + t - 3)
                W("gpsimd", S["cG1"], base_G1 + t - 3)
                if s2 == 0 and t == 0:
                    W("gpsimd", S["sInit"], 2)
                if x1b_gate is not None and t < 4:
                    W("gpsimd", S["cX1"], x1b_gate + 13 + r)
                i1 = I("gpsimd", lambda t=t, r=r: E.dma_start(
                    out=XINA[:, r, :], in_=xd[s2, 128 * t:128 * (t + 1), :]))
                inc(i1, S[f"dXA{r}"], 16)

        def p1_sums(g4):
            for tt in range(4):
                t = 4 * g4 + tt
                r = t % 4
                if t == 0:
                    W("scalar", S["cAV"], S["cAV"].v)
                    W("scalar", S["cM"], S["cM"].v)
                    W("scalar", S["cNRM"], S["cNRM"].v)
                    W("scalar", S["dTo0"], S["dTo0"].v)
                    W("scalar", S["dTo1"], S["dTo1"].v)
                ln_sums(XINA[:, r, :], r, 0,
                        (S["cG1"], S["cS1"], S["cV2"], S["cS2"]),
                        [(S[f"dXA{r}"], base_dXA[r] + 16 * (t // 4 + 1)),
                         (S["cV2"], base_V2 + t - 3)])

        def p1_fin(g4):
            for tt in range(4):
                t = 4 * g4 + tt
                r = t % 4
                ln_finish(r, 0, (S["cG1"], S["cS1"], S["cV2"], S["cS2"]),
                          negmu=True, war=("cLN", base_LN + t - 3))
                if s2 == 0 and t == 0:
                    W("scalar", S["dC"], 16 * 10)
                if dtf1_gate and t == 0:
                    W("scalar", S["dTf1"], dtf1_gate)
                if t >= 4:
                    W("scalar", S[dta_tile[t - 4][0]], dta_tile[t - 4][1])
                swait()
                i1 = I("scalar", lambda r=r: E.activation(
                    out=XN[:, r, :], in_=XINA[:, r, :], func=AF.Identity,
                    scale=lnt(24 + r), bias=lnt(20 + r)))
                inc(i1, S["cLN"])

        def p1_xnt(g4, ret3_gate):
            for tt in range(4):
                t = 4 * g4 + tt
                r = t % 4
                atd = f"dTa{t % 4}"
                W("sync", S["cLN"], base_LN + t + 1)
                if s2 == 0 and t == 0:
                    W("sync", S["dC"], 16 * 10)
                if ret3_gate and tt == 0:
                    W("sync", S["cGP"], ret3_gate)
                i1 = I("sync", lambda r=r, t=t: E.dma_start_transpose(
                    XNT[:].rearrange("p (nb c u) -> p nb c u", c=8, u=512)[
                        :, t // 4, :, 128 * (t % 4):128 * (t % 4) + 128],
                    XN[:, r, :].rearrange("p (c f) -> p c f", f=128)))
                inc(i1, S[atd], 16)
                dta_tile[t] = (atd, S[atd].v)

        return {"dma": p1_dma, "sums": p1_sums, "fin": p1_fin, "xnt": p1_xnt,
                "dta_tile": dta_tile}

    for s in range(nseq):
        for wdst, wsrc, gp, sem in ((A_W2, "wv", "wo3", "dWv"),
                                    (A_W1, "wk", "qt3", "dWk"),
                                    (A_W0, "wq", "o23", "dWq")):
            if s > 0:
                W("gpsimd", S["cGP"], PREV[gp])
            i1 = I("gpsimd", lambda wdst=wdst, wsrc=wsrc: E.dma_start(
                out=wdst[:].rearrange("p (c d) -> p c d", d=1024),
                in_=dram[wsrc].rearrange("(c p) d -> p c d", p=128)))
            inc(i1, S[sem], 16)
        dW_qkv = (("dWk", S["dWk"].v), ("dWv", S["dWv"].v))
        dW_q = ("dWq", S["dWq"].v)

        s_end_rec = [None, None]
        ot_rec = {}
        av_end_rec = [None, None]
        kv_rdy = [None, None]
        qblk_g1_gp = [0]
        qb_rdy_blk = {}
        otw_war = [0]

        kv_parts = {}
        if PRO[0] is not None:
            st = PRO[0]
            PRO[0] = None
        else:
            st = phi1a_prologue(s, None, 0)
            st["dma"](0)
            st["sums"](0)
            st["fin"](0)
            st["xnt"](0, None)
        st["dma"](1)
        st["sums"](1)
        dta_tile = st["dta_tile"]
        p1_dma, p1_sums, p1_fin, p1_xnt = st["dma"], st["sums"], st["fin"], st["xnt"]
        for g4 in range(4):
            dta_g4 = [dta_tile[4 * g4 + i] for i in range(4)]
            if g4 == 0:
                W("tensor", S["cNRM"], S["cNRM"].v)
                W("tensor", S["cS3"], S["cS3"].v)
                W("tensor", S["cS5"], S["cS5"].v)
            if g4 == 0 and s > 0:
                W("scalar", S["cGP"], PREV["o23"])
                W("scalar", S["cOF"], PREV["of"])
                W("vector", S["cGP"], PREV["o23"])
            if g4 == 3 and s > 0:
                W("vector", S["dY0"], 16 * (NT // 2) * s)
                W("vector", S["dY1"], 16 * (NT // 2) * s)
                W("vector", S["cS3"], PREV["s3"])
            nx = g4 + 1
            if nx < 4:
                p1_fin(nx)
                p1_xnt(nx, None)
            if g4 + 2 < 4:
                p1_dma(g4 + 2)
                p1_sums(g4 + 2)
            emit_kv(0, g4, dta_g4, dW_qkv, f4=True, k_gen2=True)
            kv_parts[g4] = [(nm, vv) for nm, vv in zip(F4S, f4_rec())]
            if g4 == 3:
                emit_qblk(0, 0, qb_rdy_blk, False, [dta_tile[i] for i in range(4)], dW_q, f4=True)

        xnt_tef = [dta_tile[12 + i] for i in range(4)]
        kv_rdy[0] = ([(nm, vv) for nm, vv in zip(F4S, f4_rec())]
                     + [("cGE0", S["cGE0"].v), ("cGE1", S["cGE1"].v)]
                     + [("cVG", S["cVG"].v), ("cTPg", S["cTPg"].v)])

        for grp in range(2):
            if grp == 1:
                W("vector", S["cS"], s_end_rec[0])
                W("scalar", S["cS"], s_end_rec[0])

                W("tensor", S["cNRM"], S["cNRM"].v)
                W("tensor", S["cE"], S["cE"].v)
                W("tensor", S["cQBE"], S["cQBE"].v)
                for nb in range(4):
                    emit_kv(1, nb, xnt_tef, dW_qkv, f4=True, k_gen2=True,
                            k_first=True, av_gate=av_end_rec[0])
                emit_qblk(1, 0, qb_rdy_blk, False, xnt_tef, dW_q, f4=True)
                kv_rdy[1] = ([(nm, vv) for nm, vv in zip(F4S, f4_rec())]
                             + [("cGE0", S["cGE0"].v), ("cGE1", S["cGE1"].v)]
                             + [("cVG", S["cVG"].v), ("cTPg", S["cTPg"].v)])

            if grp == 1:
                W("gpsimd", S["cGP"], S["cGP"].v)
                i1 = I("gpsimd", lambda: E.dma_start(
                    out=A_W1[:].rearrange("p (c d) -> p c d", d=1024),
                    in_=dram["wqfT"].rearrange("(c p) d -> p c d", p=128)))
                inc(i1, S["dW2"], 16)
                i1 = I("gpsimd", lambda: E.dma_start(
                    out=WO_SB[:].rearrange("p (c d) -> p c d", d=1024),
                    in_=dram["wo"].rearrange("(c p) d -> p c d", p=128)))
                inc(i1, S["dW2"], 16)
                dW_wo = S["dW2"].v

            NB2 = 4
            ptm_war = {0: 0, 1: 0, 2: 0, 3: 0}
            ptdw = {0: 0, 1: 0}
            mask_done = {}
            for b in range(NB2):
                for pp in range(2):
                    pairs = (2 * pp, 2 * pp + 1)
                    nkj = 4 * b + 4
                    av_started = [[False, False], [False, False]]
                    exp_idx = [[-1] * nkj, [-1] * nkj]
                    av_idx = [[-1] * nkj, [-1] * nkj]
                    for kj in range(nkj):
                        qlo = max(0, kj - 4 * b)
                        for pi in range(2):
                            p = pairs[pi]
                            if kj == 0:
                                if pp == 0 and b == 0:
                                    for sm, val in kv_rdy[grp]:
                                        W("tensor", S[sm], val)
                                    W("tensor", S["cE"], S["cE"].v)
                                    if grp == 0:
                                        for nm in F4S:
                                            W("tensor", S[nm], S[nm].v)
                                for sm, val in qb_rdy_blk[(grp, b, p)]:
                                    W("tensor", S[sm], val)
                                if pi == 1 and (grp, b, 3) in qb_rdy_blk:
                                    for sm, val in qb_rdy_blk[(grp, b, 3)]:
                                        W("tensor", S[sm], val)
                            else:
                                W("tensor", S["cE"], exp_idx[pi][kj - 1] + 1)
                            i1 = None
                            for hh in range(2):
                                i1 = I("tensor", lambda hh=hh, p=p, kj=kj, pi=pi, qlo=qlo, b=b: E.matmul(
                                    pS4[:, 2 * pi + hh, 128 * qlo:512],
                                    KG[:, N * p + 128 * kj:N * p + 128 * (kj + 1)][64 * hh:64 * (hh + 1), :],
                                    QB[:, b % 2, 512 * p + 128 * qlo:512 * (p + 1)][64 * hh:64 * (hh + 1), :],
                                    start=True, stop=True))
                            inc(i1, S["cS"])
                        d = kj % 2
                        for pi in range(2):
                            gs = S["cE"].v
                            exp_idx[pi][kj] = gs
                            W("scalar", S["cS"], S["cS"].v - (1 - pi))
                            W("scalar", S["cAV"], ptdw[d])
                            i1 = I("scalar", lambda d=d, pi=pi, qlo=qlo: E.activation(
                                out=PT[:, 2 * d + pi, :].rearrange(
                                    "p (g c) -> p g c", g=2)[:, :, 128 * qlo:512],
                                in_=pS4[:, 2 * pi:2 * pi + 2, 128 * qlo:512],
                                func=AF.Exp, scale=SCALE))
                            inc(i1, S["cE"])
                        diag = (kj >= 4 * b)
                        if diag:
                            q = kj - 4 * b
                            for pi in range(2):
                                gs = exp_idx[pi][kj]
                                mslot = 2 * (kj % 2) + pi
                                W("gpsimd", S["cE"], gs + 1)
                                W("gpsimd", S["cAV"], ptm_war[mslot])
                                pslot = 2 * (kj % 2) + pi
                                i1 = None
                                for hh in range(2):
                                    i1 = I("gpsimd", lambda hh=hh, pslot=pslot, q=q, mslot=mslot: E.tensor_mul(
                                        out=PTM[:, mslot, 128 * hh:128 * (hh + 1)],
                                        in0=PT[:, pslot, 512 * hh + 128 * q:512 * hh + 128 * q + 128],
                                        in1=MSK[:]))
                                inc(i1, S["cM"])
                                mask_done[(pi, b, pp, kj)] = S["cM"].v
                        av_kjs = [kj - 1] if kj > 0 else []
                        if kj == nkj - 1:
                            av_kjs.append(kj)
                        for akj in av_kjs:
                            aqlo = max(0, akj - 4 * b)
                            adiag = (akj >= 4 * b)
                            for pi in range(2):
                                p = pairs[pi]
                                gs = exp_idx[pi][akj]
                                W("tensor", S["cE"], gs + 1)
                                if adiag:
                                    W("tensor", S["cM"], mask_done[(pi, b, pp, akj)])
                                av_last = None
                                for q in range(aqlo, 4):
                                    qh = q // 2
                                    if not av_started[pi][qh]:
                                        W("tensor", S["cNRM"], otw_war[0])
                                        av_started[pi][qh] = True
                                        st = True
                                    else:
                                        st = False
                                    for hh in range(2):
                                        is_diag_tile = adiag and (q == akj - 4 * b)
                                        if is_diag_tile:
                                            lhs = PTM[:, 2 * (akj % 2) + pi, 128 * hh:128 * (hh + 1)]
                                        else:
                                            lhs = PT[:, 2 * (akj % 2) + pi, 512 * hh + 128 * q:512 * hh + 128 * q + 128]
                                        av_last = I("tensor", lambda lhs=lhs, q=q, hh=hh, akj=akj, p=p, pi=pi, st=(st and hh == 0): E.matmul(
                                            pAV2[pi][:, q // 2, 130 * (q % 2) + 65 * hh:130 * (q % 2) + 65 * hh + 65],
                                            lhs,
                                            VG[:, 528 * akj + 66 * (2 * p + hh):528 * akj + 66 * (2 * p + hh) + 65],
                                            start=(st and hh == 0), stop=False, skip_group_check=True))
                                inc(av_last, S["cAV"])
                                av_idx[pi][akj] = S["cAV"].v
                                if adiag and akj == kj:
                                    pass
                                if adiag:
                                    ptm_war[2 * (akj % 2) + pi] = S["cAV"].v
                            ptdw[akj % 2] = S["cAV"].v
                    if pp == 1 and b + 1 < NB2:
                        emit_qblk(grp, b + 1, qb_rdy_blk, in_attention=True)
                    nrm_vals = [0, 0]
                    for pi in range(2):
                        p = pairs[pi]
                        W("vector", S["cAV"], av_idx[pi][nkj - 1])
                        W("vector", S["cNRM"], S["cNRM"].v)
                        if pi == 0 and ot_prev[0] is not None:
                            W("vector", S[ot_prev[0][0]], ot_prev[0][1])
                        iR = I("vector", lambda pi=pi: E.reciprocal(
                            out=RCP[:, 8 * pi:8 * pi + 8].rearrange("p (a b) -> p a b", a=2),
                            in_=pAV2[pi][:, :, 64:260:65]))
                        vchain(iR)
                        vwait()
                        last = None
                        for q in range(4):
                            for hh in range(2):
                                col = 8 * pi + 4 * (q // 2) + 2 * (q % 2) + hh
                                last = I("vector", lambda q=q, hh=hh, pi=pi, col=col: E.tensor_scalar(
                                    out=ONAT[:, 4 * pi + q, 64 * hh:64 * hh + 64],
                                    in0=pAV2[pi][:, q // 2, 130 * (q % 2) + 65 * hh:130 * (q % 2) + 65 * hh + 64],
                                    scalar1=RCP[:, col:col + 1],
                                    scalar2=None, op0=ALU.mult))
                        inc(last, S["cNRM"])
                        nrm_vals[pi] = S["cNRM"].v
                    otd = f"dTo{ot_blk[0] % 2}"
                    for pi in range(2):
                        p = pairs[pi]
                        gp = grp * 4 + p
                        W("sync", S["cNRM"], nrm_vals[pi])
                        i1 = I("sync", lambda pi=pi, b=b, gp=gp: E.dma_start_transpose(
                            OT[:, N * gp + 512 * b:N * gp + 512 * (b + 1)].rearrange(
                                "p (q f) -> p q f", f=128),
                            ONAT[:, 4 * pi:4 * pi + 4, :]))
                        inc(i1, S[otd], 16)
                    ot_prev[0] = (otd, S[otd].v)
                    ot_rec[(grp, b)] = (otd, S[otd].v)
                    ot_blk[0] += 1
                    otw_war[0] = S["cNRM"].v
            s_end_rec[grp] = S["cS"].v
            av_end_rec[grp] = S["cAV"].v
            if grp == 1:
                qblk_g1_gp[0] = S["cGP"].v


        ot_dto = (S["dTo0"].v, S["dTo1"].v)

        W("gpsimd", S["cGP"], qblk_g1_gp[0])
        for wdst, wsrc, dd in ((WOUTT, "woutT", 1024),
                               (RTRT, "rtrT", 256), (LEAF, "leaf", 1024)):
            i1 = I("gpsimd", lambda wdst=wdst, wsrc=wsrc, dd=dd: E.dma_start(
                out=wdst[:].rearrange("p (c d) -> p c d", d=dd),
                in_=dram[wsrc].rearrange("(c p) d -> p c d", p=128)))
            inc(i1, S["dW3"], 16)
        dW_ffn = S["dW3"].v

        base_dXB = (S["dXB0"].v, S["dXB1"].v, S["dXB2"].v, S["dXB3"].v)
        base_X1 = S["cX1"].v
        base_V4, base_S3, base_S4 = S["cV4"].v, S["cS3"].v, S["cS4"].v
        base_V6, base_S6, base_XN2 = S["cV6"].v, S["cS6"].v, S["cXN2"].v
        base_V8, base_S7, base_S8, base_OF = S["cV8"].v, S["cS7"].v, S["cS8"].v, S["cOF"].v
        base_dY = (S["dY0"].v, S["dY1"].v)
        base_VSt = S["cVSt"].v
        pipe = {"qt_gp": 0, "z_gp": 0, "sel_gp": 0, "ret_gp": 0, "o2_gp": 0,
                "xn2t": {}, "xn2t_tile": {}}

        def wo_stage(b4):
            gt0 = 4 * b4
            for tt in range(4):
                t = gt0 + tt
                r = t % 4
                W("gpsimd", S["cX1"], base_X1 + t - 3)
                if t < 4:
                    W("gpsimd", S["cG1"], S["cG1"].v)
                    W("gpsimd", S["cS1"], S["cS1"].v)
                    W("gpsimd", S["cLN"], S["cLN"].v)
                i1 = I("gpsimd", lambda t=t, r=r: E.dma_start(
                    out=XINA[:, r, :], in_=xd[s, 128 * t:128 * (t + 1), :]))
                inc(i1, S[f"dXB{r}"], 16)
            for tt in range(4):
                t = gt0 + tt
                ro = t % 4
                tile_banks = []
                for e in range(2):
                    bank = f4_acquire()
                    if tt == 0 and e == 0:
                        for g in range(2):
                            W("tensor", S[ot_rec[(g, b4)][0]], ot_rec[(g, b4)][1])
                        if b4 == 0:
                            W("tensor", S["dC"], 16 * 10)
                            W("tensor", S["cE"], S["cE"].v)
                            W("tensor", S["cQBE"], S["cQBE"].v)
                            W("tensor", S["cNRM"], S["cNRM"].v)
                    W("tensor", S["dW2"], dW_wo)
                    i1 = I("tensor", lambda e=e, bank=bank: E.matmul(
                        pF2[bank // 2][:, 512 * (bank % 2):512 * (bank % 2) + 512],
                        ONES2[:, :], BOHL[:, 512 * e:512 * (e + 1)],
                        start=True, stop=False))
                    for c in range(NCH):
                        i1 = I("tensor", lambda c=c, t=t, e=e, bank=bank: E.matmul(
                            pF2[bank // 2][:, 512 * (bank % 2):512 * (bank % 2) + 512],
                            OT[:, N * c + 128 * t:N * c + 128 * (t + 1)],
                            WO_SB[:, 1024 * c + 512 * e:1024 * c + 512 * (e + 1)],
                            start=False, stop=(c == NCH - 1)))
                    inc(i1, S["cGP"])
                    tile_banks.append(bank)
                assert tile_banks[0] % 2 == 0 and tile_banks[1] == tile_banks[0] + 1
                mmv = S["cGP"].v
                if b4 == 0 and tt == 0:
                    W("scalar", S["cNRM"], S["cNRM"].v)
                    W("scalar", S["cVG"], S["cVG"].v)
                    W("scalar", S["cTPg"], S["cTPg"].v)
                if b4 == 1 and tt == 0:
                    W("scalar", S["cAV"], S["cAV"].v)
                    W("scalar", S["cM"], S["cM"].v)
                    W("scalar", S["cNRM"], S["cNRM"].v)
                    W("scalar", S["dTo0"], ot_dto[0])
                    W("scalar", S["dTo1"], ot_dto[1])
                W("vector", S["cGP"], mmv)
                W("vector", S["cV4"], base_V4 + t - 3)
                W("vector", S["cS4"], S["cS4"].v - 3)
                iR = I("vector", lambda r4=t % 4, pr=tile_banks[0] // 2: E.reduce_sum(
                    out=lnt(28 + r4), in_=pF2[pr][:], axis=AXX))
                inc(iR, S["cG2"])
                for e in range(2):
                    bank = tile_banks[e]
                    W("vector", S["cGP"], mmv)
                    W("vector", S["cVSt"], base_VSt + t - 3)
                    i2 = I("vector", lambda e=e, ro=ro, bank=bank: E.tensor_copy(
                        out=OATT[:, ro, 512 * e:512 * (e + 1)],
                        in_=pF2[bank // 2][:, 512 * (bank % 2):512 * (bank % 2) + 512]))
                    f4_release(i2, bank)
                for sm, val in f4_pairs(f4_rec()):
                    W("scalar", sm, val)
                W("scalar", S["cE"], S["cE"].v)
                W("scalar", S["cS1"], S["cS1"].v)
                W("scalar", S["cS3"], S["cS3"].v)
                W("scalar", S["cS5"], S["cS5"].v)
                W("scalar", S["cS7"], S["cS7"].v)
                swait()
                sq_dmp = SQS if b4 == 0 else SQD
                iS = I("scalar", lambda ro=ro, r4=t % 4, sq_dmp=sq_dmp: E.activation(
                    out=sq_dmp[:], in_=OATT[:, ro, :], func=AF.Square,
                    accum_out=lnt(32 + r4)))
                inc(iS, S["cS3"])
            for tt in range(4):
                t = gt0 + tt
                ln_finish(t % 4, 28, (S["cG2"], S["cS3"], S["cV4"], S["cS4"]))
            for tt in range(4):
                t = gt0 + tt
                r = t % 4
                W("vector", S["cS4"], base_S4 + t + 1)
                W("vector", S[f"dXB{r}"], base_dXB[r] + 16 * (t // 4 + 1))
                W("vector", S["cX1"], S["cX1"].v)
                W("vector", S["cOF"], S["cOF"].v)
                i1 = I("vector", lambda r=r, r4=t % 4: E.tensor_scalar(
                    out=VTMP[:], in0=OATT[:, r, :], scalar1=lnt(36 + r4),
                    scalar2=lnt(52 + r4), op0=ALU.subtract, op1=ALU.mult))
                inc(i1, S["cVSt"])
                W("vector", S["cVSt"], S["cVSt"].v)
                i2 = I("vector", lambda t=t, r=r: E.tensor_add(
                    out=X1[:, t % 8, :], in0=XINA[:, r, :], in1=VTMP[:]))
                inc(i2, S["cX1"])
            for tt in range(4):
                t = gt0 + tt
                ln_sums(X1[:, t % 8, :], t % 4, 56,
                        (S["cG3"], S["cS5"], S["cV6"], S["cS6"]),
                        [(S["cX1"], base_X1 + t + 1), (S["cV6"], base_V6 + t - 3)],
                        dump=(SQS if b4 == 0 else None))
            xtd_prev = f"dTf{(b4 - 1) % 2}"
            for tt in range(4):
                t = gt0 + tt
                r = t % 4
                ln_finish(r, 56, (S["cG3"], S["cS5"], S["cV6"], S["cS6"]),
                          negmu=True, war=("cXN2", base_XN2 + t - 3))
                if t >= 4:
                    W("scalar", S[xtd_prev], pipe["xn2t"][b4 - 1][1])
                else:
                    W("scalar", S[dta_tile[12 + r][0]], dta_tile[12 + r][1])
                swait()
                i1 = I("scalar", lambda r=r, t=t, r4=t % 4: E.activation(
                    out=XN[:, r, :], in_=X1[:, t % 8, :], func=AF.Identity,
                    scale=lnt(80 + r4), bias=lnt(76 + r4)))
                inc(i1, S["cXN2"])
            xtd = f"dTf{b4 % 2}"
            for tt in range(4):
                t = gt0 + tt
                r = t % 4
                W("sync", S["cXN2"], base_XN2 + t + 1)
                i1 = I("sync", lambda r=r, tt=tt, b4=b4: E.dma_start_transpose(
                    XN2T[:, b4 % 2, :].rearrange("p (c q f) -> p c q f", q=4, f=128)[:, :, tt, :],
                    XN[:, r, :].rearrange("p (c f) -> p c f", f=128)))
                inc(i1, S[xtd], 16)
            pipe["xn2t"][b4] = (xtd, S[xtd].v)

        def ffn_stage(b4, hook=None):
            gt0 = 4 * b4
            xslot = b4 % 2
            for e in range(NCH):
                bank = f4_acquire()
                W("tensor", S[pipe["xn2t"][b4][0]], pipe["xn2t"][b4][1])
                W("tensor", S["dW2"], dW_wo)
                W("tensor", S["dW3"], dW_ffn)
                i1 = None
                for c in range(NCH):
                    i1 = I("tensor", lambda c=c, e=e, bank=bank, xslot=xslot: E.matmul(
                        pF[bank][:],
                        A_W1[:, 1024 * c + 128 * e:1024 * c + 128 * e + 128],
                        XN2T[:, xslot, 512 * c:512 * (c + 1)],
                        start=(c == 0), stop=(c == NCH - 1)))
                inc(i1, S["cGP"])
                W("vector", S["cGP"], S["cGP"].v)
                W("vector", S["cGP"], pipe["z_gp"])
                i2 = I("vector", lambda e=e, bank=bank: E.tensor_copy(
                    out=QT[:, 512 * e:512 * (e + 1)], in_=pF[bank][:]))
                f4_release(i2, bank)
            pipe["qt_gp"] = S["cGP"].v
            qt_rdy = f4_rec()

            for nch2 in range(2):
                bank = f4_acquire()
                f4_wait("tensor", qt_rdy)
                i1 = None
                for c in range(NCH):
                    i1 = I("tensor", lambda c=c, nch2=nch2, bank=bank: E.matmul(
                        pF[bank][:],
                        RTRT[:, 256 * c + 128 * nch2:256 * c + 128 * (nch2 + 1)],
                        QT[:, 512 * c:512 * (c + 1)],
                        start=(c == 0), stop=(c == NCH - 1)))
                inc(i1, S["cGP"])
                zv = S["cGP"].v
                W("scalar", S["cGP"], zv)
                W("scalar", S["cHL"], S["cHL"].v)
                if nch2 == 0:
                    W("scalar", S["cGP"], pipe["sel_gp"])
                i1 = I("scalar", lambda bank=bank: E.activation(
                    out=SPE[:, 0, :], in_=pF[bank][:], func=AF.Exp, scale=1.0))
                schain(i1)
                swait()
                i3 = I("scalar", lambda nch2=nch2: E.activation(
                    out=SPL[:, nch2, :], in_=SPE[:, 0, :], func=AF.Ln, bias=oneap))
                inc(i3, S["cHL"])
                W("vector", S["cGP"], zv)
                W("vector", S["cHL"], S["cHL"].v)
                if nch2 == 0:
                    W("vector", S["cGP"], pipe["sel_gp"])
                iv = I("vector", lambda nch2=nch2, bank=bank: E.tensor_sub(
                    out=SPL[:, nch2 + 2, :], in0=SPL[:, nch2, :], in1=pF[bank][:]))
                f4_release(iv, bank)
            pipe["z_gp"] = S["cGP"].v
            z_rdy = f4_rec()

            for mh in range(2):
                bank = f4_acquire()
                f4_wait("tensor", z_rdy)
                i1 = None
                for c in range(4):
                    i1 = I("tensor", lambda c=c, mh=mh, bank=bank: E.matmul(
                        pF[bank][:],
                        SEL[:, 256 * c + 128 * mh:256 * c + 128 * (mh + 1)],
                        SPL[:, c, :],
                        start=(c == 0), stop=(c == 3)))
                inc(i1, S["cGP"])
                W("scalar", S["cGP"], S["cGP"].v)
                if mh == 0:
                    W("scalar", S["cGP"], pipe["ret_gp"])
                i2 = I("scalar", lambda mh=mh, bank=bank: E.activation(
                    out=LPT[:, 512 * mh:512 * (mh + 1)], in_=pF[bank][:], func=AF.Exp))
                f4_release(i2, bank)
            pipe["sel_gp"] = S["cGP"].v
            lp_rdy = f4_rec()

            for dc in range(NCH):
                bank = f4_acquire()
                f4_wait("tensor", lp_rdy)
                i1 = None
                for mh in range(2):
                    i1 = I("tensor", lambda mh=mh, dc=dc, bank=bank: E.matmul(
                        pF[bank][:],
                        LEAF[:, 1024 * mh + 128 * dc:1024 * mh + 128 * dc + 128],
                        LPT[:, 512 * mh:512 * (mh + 1)],
                        start=(mh == 0), stop=(mh == 1)))
                inc(i1, S["cGP"])
                W("vector", S["cGP"], S["cGP"].v)
                if dc == 0:
                    W("vector", S["cGP"], pipe["o2_gp"])
                i2 = I("vector", lambda dc=dc, bank=bank: E.tensor_copy(
                    out=RETT[:, 512 * dc:512 * (dc + 1)], in_=pF[bank][:]))
                f4_release(i2, bank)
            pipe["ret_gp"] = S["cGP"].v
            ret_rdy = f4_rec()
            if hook is not None:
                hook()

            for tt in range(4):
                t = gt0 + tt
                ro = t % 4
                tile_banks = []
                for e in range(2):
                    bank = f4_acquire()
                    f4_wait("tensor", ret_rdy)
                    i1 = I("tensor", lambda e=e, bank=bank: E.matmul(
                        pF2[bank // 2][:, 512 * (bank % 2):512 * (bank % 2) + 512],
                        ONES2[:, :], WOBHL[:, 512 * e:512 * (e + 1)],
                        start=True, stop=False))
                    for c in range(NCH):
                        i1 = I("tensor", lambda c=c, tt=tt, e=e, bank=bank: E.matmul(
                            pF2[bank // 2][:, 512 * (bank % 2):512 * (bank % 2) + 512],
                            RETT[:, 512 * c + 128 * tt:512 * c + 128 * (tt + 1)],
                            WOUTT[:, 1024 * c + 512 * e:1024 * c + 512 * (e + 1)],
                            start=False, stop=(c == NCH - 1)))
                    inc(i1, S["cGP"])
                    tile_banks.append(bank)
                assert tile_banks[0] % 2 == 0 and tile_banks[1] == tile_banks[0] + 1
                mmv = S["cGP"].v
                for e in range(2):
                    bank = tile_banks[e]
                    W("vector", S["cGP"], mmv)
                    W("vector", S["cOF"], base_OF + t - 3)
                    W("vector", S["cS7"], base_S7 + t - 3)
                    i2 = I("vector", lambda e=e, ro=ro, bank=bank: E.tensor_copy(
                        out=OUT2[:, ro, 512 * e:512 * (e + 1)],
                        in_=pF2[bank // 2][:, 512 * (bank % 2):512 * (bank % 2) + 512]))
                    f4_release(i2, bank)
                ln_sums(OUT2[:, t % 4, :], t % 4, 84,
                        (S["cG4"], S["cS7"], S["cV8"], S["cS8"]),
                        f4_pairs(f4_rec()) + [(S["cV8"], base_V8 + t - 3)])
            for tt in range(4):
                t = gt0 + tt
                ln_finish(t % 4, 84, (S["cG4"], S["cS7"], S["cV8"], S["cS8"]))
            for tt in range(4):
                t = gt0 + tt
                r = t % 2
                W("vector", S["cS8"], base_S8 + t + 1)
                W("vector", S["dY0" if r == 0 else "dY1"], base_dY[r] + 16 * (t // 2))
                W("vector", S["cX1"], S["cX1"].v)
                W("vector", S["cOF"], S["cOF"].v)
                i1 = I("vector", lambda ro=t % 4, r4=t % 4: E.tensor_scalar(
                    out=VTMP[:], in0=OUT2[:, ro, :], scalar1=lnt(92 + r4),
                    scalar2=lnt(108 + r4), op0=ALU.subtract, op1=ALU.mult))
                vchain(i1)
                vwait()
                i2 = I("vector", lambda t=t, r=r: E.tensor_add(
                    out=OUTF[:, r, :], in0=X1[:, t % 8, :], in1=VTMP[:]))
                inc(i2, S["cOF"])
                W("sync", S["cOF"], S["cOF"].v)
                i3 = I("sync", lambda t=t, r=r: E.dma_start(
                    out=yd[s, 128 * t:128 * (t + 1), :], in_=OUTF[:, r, :]))
                inc(i3, S["dY0" if r == 0 else "dY1"], 16)
            pipe["o2_gp"] = S["cGP"].v

        wo_stage(0)
        wo_stage(1)
        ffn_stage(0)
        wo_stage(2)
        ffn_stage(1)
        wo_stage(3)
        wo3_gp = S["cGP"].v
        if s + 1 < nseq:
            PRO[0] = phi1a_prologue(s + 1, base_X1, S["dTf1"].v)
            PRO[0]["dma"](0)
        ffn_stage(2)
        if s + 1 < nseq:
            PRO[0]["sums"](0)

        def f3_hook():
            if s + 1 < nseq:
                PRO[0]["fin"](0)
                PRO[0]["xnt"](0, S["cGP"].v)
        ffn_stage(3, hook=f3_hook)
        xn2t_prev[0] = S["dTf1"].v
        PREV.update(wo3=wo3_gp, qt3=pipe["qt_gp"], ret3=pipe["ret_gp"],
                    o23=pipe["o2_gp"], x1b=base_X1, s3=S["cS3"].v, of=S["cOF"].v)

    W("sync", S["dY0"], 16 * (NT // 2) * nseq)
    W("sync", S["dY1"], 16 * (NT // 2) * nseq)


_NC_CACHE = {}


def _prep_weights(inputs):
    f = np.float32
    routers_pad = np.zeros((256, D), dtype=f)
    routers_pad[:255] = np.asarray(inputs["L0_ffn_routers"], f)
    return {
        "wq": np.ascontiguousarray(np.asarray(inputs["L0_attn_Wq"], f)),
        "wk": np.ascontiguousarray(np.asarray(inputs["L0_attn_Wk"], f)),
        "wv": np.ascontiguousarray(np.asarray(inputs["L0_attn_Wv"], f)),
        "wo": np.ascontiguousarray(np.asarray(inputs["L0_attn_Wo"], f)),
        "bo": np.ascontiguousarray(np.asarray(inputs["L0_attn_bo"], f).reshape(1, D)),
        "wqfT": np.ascontiguousarray(np.asarray(inputs["L0_ffn_Wq"], f).T),
        "rtrT": np.ascontiguousarray(routers_pad.T),
        "leaf": np.ascontiguousarray(np.asarray(inputs["L0_ffn_leaf_values"], f)),
        "woutT": np.ascontiguousarray(np.asarray(inputs["L0_ffn_Wout_w"], f).T),
        "wob": np.ascontiguousarray(np.asarray(inputs["L0_ffn_Wout_b"], f).reshape(1, D)),
    }


def kernel(**inputs):
    if "nc" not in _NC_CACHE:
        _NC_CACHE["nc"] = build_nc()
    nc = _NC_CACHE["nc"]
    x = np.asarray(inputs["L0_attn_x"], np.float32)
    w = _prep_weights(inputs)
    in_maps = []
    for c in range(NCORES):
        m = dict(w)
        m["x"] = np.ascontiguousarray(x[c * NSEQ:(c + 1) * NSEQ])
        in_maps.append(m)
    res = run_bass_kernel_spmd(nc, in_maps, core_ids=list(range(NCORES)))
    y = np.concatenate([r["y"] for r in res.results], axis=0)
    return y.reshape(B, N, D)


if __name__ == "__main__":
    nc = build_nc()
    print("built ok, arena bytes:", ARENA_SZ)


# version 62
# speedup vs baseline: 1.1847x; 1.0026x over previous
import contextlib
import numpy as np
import ml_dtypes

import concourse.bass as bass
import concourse.mybir as mybir
from concourse.bass_utils import run_bass_kernel_spmd

F32 = mybir.dt.float32
BF16 = mybir.dt.bfloat16
U8 = mybir.dt.uint8
AF = mybir.ActivationFunctionType
ALU = mybir.AluOpType
AXX = mybir.AxisListType.X

B, N, D, H, DH = 32, 2048, 1024, 16, 64
NCORES = 8
NSEQ = B // NCORES
NT = N // 128
NBLK = N // 256
NB4 = N // 512
NCH = D // 128
EPS = 1e-5
SCALE = DH ** -0.5

bfarr = lambda a: np.ascontiguousarray(np.asarray(a, dtype=ml_dtypes.bfloat16))


def _consts():
    h = np.arange(1, H + 1, dtype=np.float64)
    slopes = 1.0 / (2.0 ** 8) ** (h / H)
    j = np.arange(N, dtype=np.float64)
    u = np.exp(-slopes[None, :] * j[:, None])
    u_tab = np.zeros((128, NT * H), dtype=np.float32)
    for t in range(NT):
        u_tab[:, t * H:(t + 1) * H] = u[t * 128:(t + 1) * 128, :]
    mask = (np.arange(128)[None, :] >= np.arange(128)[:, None]).astype(np.float32)
    ident = np.eye(128, dtype=np.float32)
    sel = np.zeros((512, 256), dtype=np.float32)
    for lev in range(8):
        n = 1 << lev
        span = 256 // (2 * n)
        for i in range(n):
            r = n - 1 + i
            lo = i * 2 * span
            sel[r, lo:lo + span] = -1.0
            sel[256 + r, lo + span:lo + 2 * span] = -1.0
    sel_pk = np.zeros((128, 4 * 256), dtype=np.float32)
    for c in range(4):
        sel_pk[:, 256 * c:256 * (c + 1)] = sel[128 * c:128 * (c + 1), :]
    ones2 = np.ones((2, 128), dtype=np.float32)
    epsone = np.zeros((128, 2), dtype=np.float32)
    epsone[:, 0] = EPS
    epsone[:, 1] = 1.0
    return dict(u=u_tab, mask=mask, ident=ident, sel=sel_pk, ones2=ones2, epsone=epsone)


class Arena:
    def __init__(self, nc, ctx, name, size_bytes):
        self.nc = nc
        self.h = ctx.enter_context(nc.sbuf_tensor(name, [128, size_bytes], U8))
        self.base = nc.lookup_mloc(self.h).addr
        self.size = size_bytes

    def t(self, name, off, shape, dtype):
        per_part = int(np.prod(shape[1:])) * mybir.dt.size(dtype)
        assert off + per_part <= self.size, f"{name}: {off}+{per_part} > {self.size}"
        hnd = self.nc.virtual_sbuf_tensor(name, list(shape), dtype)
        self.nc._tensor(hnd.name, hnd.shape, hnd.dtype, type="SB")
        mloc = self.nc.lookup_mloc(hnd)
        mloc.allocated = not self.nc.target_bir_lowering
        mloc.addr = self.base + off
        return hnd


class PsumViews:
    def __init__(self, nc, ctx):
        self.nc = nc
        self.h = ctx.enter_context(nc.psum_tensor("psb", [128, 4096], F32))
        self.base = nc.lookup_mloc(self.h).addr

    def t(self, name, byte_off, shape, dtype):
        hnd = self.nc.virtual_psum_tensor(name, list(shape), dtype)
        self.nc._tensor(hnd.name, hnd.shape, hnd.dtype, type="PSUM")
        mloc = self.nc.lookup_mloc(hnd)
        mloc.allocated = not self.nc.target_bir_lowering
        mloc.addr = self.base + byte_off
        return hnd


class Sem:
    __slots__ = ("h", "v", "name")

    def __init__(self, h, name=""):
        self.h = h
        self.v = 0
        self.name = name


PRUNE_SAME = [True]
SAME_ENG_SEMS = {
    "vector": {"cG1", "cG2", "cG3", "cG4", "cV2", "cV4", "cV6", "cV8",
               "cX1", "cOF", "cNRM", "cQBE", "cVSt", "cVC"},
    "scalar": {"cS1", "cS2", "cS3", "cS4", "cS5", "cS6", "cS7", "cS8",
               "cE", "cLN", "cXN2", "cHL", "cSC"},
    "tensor": {"cGP", "cS", "cAV"},
    "gpsimd": {"cM"},
}


SEM_NAMES = [
    "dXA0", "dXA1", "dXA2", "dXA3", "dXB0", "dXB1", "dXB2", "dXB3", "dW", "dW2", "dW3", "dC", "dY0", "dY1",
    "cG1", "cS1", "cV2", "cS2", "cLN",
    "cGP", "cGE0", "cGE1",
    "cS", "cE", "cM", "cAV", "cNRM",
    "cG2", "cS3", "cV4", "cS4", "cX1",
    "cG3", "cS5", "cV6", "cS6", "cXN2",
    "cTPf", "cTEf",
    "cZL", "cHL",
    "cG4", "cS7", "cV8", "cS8", "cOF",
    "cVC", "cSC", "cGC", "cQBE", "cTPo", "cTEo", "cVSt", "cVG", "cRCP", "cTEfV", "cTPg",
    "dTo0", "dTo1", "dTf0", "dTf1", "dTa0", "dTa1", "dTa2", "dTa3",
    "dWk", "dWv", "dWq",
    "sMT", "sMV", "sMS", "sInit",
]

LAYOUT = {}
_off = 0


def _lay(name, nbytes, align=32):
    global _off
    _off = (_off + align - 1) // align * align
    LAYOUT[name] = _off
    _off += nbytes


_lay("OT", 32768)
_lay("XIN", 8192)
_lay("XN", 8192)
_lay("UC", 1024)
_lay("MSK", 256)
_lay("IDN", 256)
_lay("SEL", 4096)
_lay("EPS1", 32)
_lay("BOHL", 2048)
_lay("WOBHL", 2048)
_lay("ONES2", 256)
_lay("VTMP", 2048)
_lay("LNT", 512)
_lay("ONAT", 2048)
_lay("PTM", 2048)
_lay("RCP", 64)
_UB = (_off + 31) // 32 * 32
LAYOUT["A_W0"] = _UB + 0
LAYOUT["A_W1"] = _UB + 16384
LAYOUT["A_W2"] = _UB + 32768
LAYOUT["XNT"] = _UB + 49152
LAYOUT["KG"] = _UB + 81920
LAYOUT["VG"] = _UB + 98304
LAYOUT["QB"] = _UB + 115200
LAYOUT["PT"] = _UB + 123392
_BB = _UB + 32768
_bc = _BB + 16384
def _layB(name, nbytes):
    global _bc
    LAYOUT[name] = _bc
    _bc += nbytes
_layB("RTRT", 4096)
_layB("LEAF", 4096)
_layB("XN2T", 16384)
_layB("QT", 8192)
_layB("SPE", 4096)
_layB("SPL", 8192)
_layB("LPT", 2048)
_layB("RETT", 8192)
_layB("OUT2", 8192)
_layB("OUTF", 8192)
_layB("OATT", 8192)
_layB("X1", 16384)
LAYOUT["XINA"] = LAYOUT["XIN"]
ARENA_SZ = max(LAYOUT["X1"] + 16384, LAYOUT["PT"] + 8192)
assert ARENA_SZ <= 212800, ARENA_SZ


def build_nc(nseq=NSEQ):
    nc = bass.Bass("TRN2", target_bir_lowering=False)
    C = _consts()

    dram = {}
    dram["x"] = nc.dram_tensor("x", [nseq, N, D], F32, kind="ExternalInput").ap()
    for nm, shp in (("wq", [D, D]), ("wk", [D, D]), ("wv", [D, D]), ("wo", [D, D]),
                    ("bo", [1, D]), ("wqfT", [D, D]), ("rtrT", [D, 256]),
                    ("leaf", [256, D]), ("woutT", [D, D]), ("wob", [1, D])):
        dram[nm] = nc.dram_tensor(nm, shp, F32, kind="ExternalInput").ap()
    dram["y"] = nc.dram_tensor("y", [nseq, N, D], F32, kind="ExternalOutput").ap()
    dram["u_c"] = nc.inline_tensor(C["u"], "u_c").ap()
    dram["mask_c"] = nc.inline_tensor(bfarr(C["mask"]), "mask_c").ap()
    dram["id_c"] = nc.inline_tensor(bfarr(C["ident"]), "id_c").ap()
    dram["sel_c"] = nc.inline_tensor(C["sel"], "sel_c").ap()
    dram["ones2_c"] = nc.inline_tensor(bfarr(C["ones2"]), "ones2_c").ap()
    dram["epsone_c"] = nc.inline_tensor(C["epsone"], "epsone_c").ap()

    ctx = contextlib.ExitStack()
    with ctx:
        ar = Arena(nc, ctx, "arena", ARENA_SZ)
        L = LAYOUT
        T = {}
        T["OT"] = ar.t("OT", L["OT"], [128, NCH * N], BF16)
        
        T["XN"] = ar.t("XN", L["XN"], [128, 4, 1024], BF16)
        T["UC"] = ar.t("UC", L["UC"], [128, 256], F32)
        T["MSK"] = ar.t("MSK", L["MSK"], [128, 128], BF16)
        T["IDN"] = ar.t("IDN", L["IDN"], [128, 128], BF16)
        T["SEL"] = ar.t("SEL", L["SEL"], [128, 1024], F32)
        T["BSBO"] = ar.t("BSBO", L["XIN"], [1, 1024], F32)
        T["BSWOB"] = ar.t("BSWOB", L["XIN"] + 4096, [1, 1024], F32)
        T["LOT"] = ar.t("LOT", L["XN"], [1, 2048], BF16)
        T["SQSB"] = T["LOT"]
        T["EPS1"] = ar.t("EPS1", L["EPS1"], [128, 2], F32)
        T["BOHL"] = ar.t("BOHL", L["BOHL"], [2, 1024], BF16)
        T["WOBHL"] = ar.t("WOBHL", L["WOBHL"], [2, 1024], BF16)
        T["ONES2"] = ar.t("ONES2", L["ONES2"], [2, 128], BF16)

        T["VTMP"] = ar.t("VTMP", L["VTMP"], [128, 1024], BF16)
        T["LNT"] = ar.t("LNT", L["LNT"], [128, 128], F32)
        T["XINA"] = ar.t("XINA", L["XINA"], [128, 4, 1024], BF16)
        T["ONAT"] = ar.t("ONAT", L["ONAT"], [128, 8, 128], BF16)
        T["PTM"] = ar.t("PTM", L["PTM"], [128, 4, 256], BF16)
        T["RCP"] = ar.t("RCP", L["RCP"], [128, 16], F32)
        T["A_W0"] = ar.t("A_W0", L["A_W0"], [128, NCH * 1024], BF16)
        T["A_W1"] = ar.t("A_W1", L["A_W1"], [128, NCH * 1024], BF16)
        T["A_W2"] = ar.t("A_W2", L["A_W2"], [128, NCH * 1024], BF16)
        T["XNT"] = ar.t("XNT", L["XNT"], [128, NCH * N], BF16)
        T["KG"] = ar.t("KG", L["KG"], [128, 4 * N], BF16)
        T["VG"] = ar.t("VG", L["VG"], [128, NT * 528], BF16)
        T["QB"] = ar.t("QB", L["QB"], [128, 2, 2048], BF16)
        T["PT"] = ar.t("PT", L["PT"], [128, 4, 1024], BF16)
        T["SQD"] = ar.t("SQD", L["ONAT"], [128, 1024], F32)
        T["WOUTT"] = ar.t("WOUTT", L["A_W0"], [128, NCH * 1024], BF16)
        T["WO_SB"] = ar.t("WO_SB", L["A_W2"], [128, NCH * 1024], BF16)
        T["RTRT"] = ar.t("RTRT", L["RTRT"], [128, NCH * 256], BF16)
        T["LEAF"] = ar.t("LEAF", L["LEAF"], [128, 2 * 1024], BF16)
        T["XN2T"] = ar.t("XN2T", L["XN2T"], [128, 2, NCH * 512], BF16)
        T["QT"] = ar.t("QT", L["QT"], [128, NCH * 512], BF16)
        T["SPE"] = ar.t("SPE", L["SPE"], [128, 2, 512], F32)
        T["SPL"] = ar.t("SPL", L["SPL"], [128, 4, 512], F32)
        T["LPT"] = ar.t("LPT", L["LPT"], [128, 2 * 512], BF16)
        T["RETT"] = ar.t("RETT", L["RETT"], [128, NCH * 512], BF16)
        T["OUT2"] = ar.t("OUT2", L["OUT2"], [128, 4, 1024], BF16)
        T["OUTF"] = ar.t("OUTF", L["OUTF"], [128, 2, 1024], F32)
        T["OATT"] = ar.t("OATT", L["OATT"], [128, 4, 1024], BF16)
        T["X1"] = ar.t("X1", L["X1"], [128, 8, 1024], BF16)

        pv = PsumViews(nc, ctx)
        T["pS"] = [pv.t(f"pS{i}", 4096 * i, [128, 2, 512], F32) for i in range(2)]
        T["pAV"] = [pv.t(f"pAV{i}", 8192 + 2048 * i, [128, 264], F32) for i in range(2)]
        T["pG"] = [pv.t(f"pG{i}", 12288 + 2048 * i, [128, 512], F32) for i in range(2)]
        T["pGb"] = [pv.t(f"pGb{i}", 12288 + 2048 * i, [128, 128], BF16) for i in range(2)]
        T["pTb"] = [pv.t(f"pTb{i}", 8192 + 2048 * i, [128, 128], BF16) for i in range(2)]
        T["pTa"] = [pv.t(f"pTa{i}", 2048 * i, [128, 128], BF16) for i in range(4)]
        T["pS2"] = [pv.t(f"pS2{i}", 4096 * i, [128, 2, 512], F32) for i in range(2)]
        T["pS4"] = pv.t("pS4", 0, [128, 4, 512], F32)
        T["pF"] = [pv.t(f"pF{i}", 2048 * i, [128, 512], F32) for i in range(4)]
        T["pBV"] = [pv.t(f"pBV{i}", 8192 + 2048 * i, [128, 512], F32) for i in range(2)]
        T["pF2"] = [pv.t(f"pF2{i}", 4096 * i, [128, 1024], F32) for i in range(2)]
        T["pAV2"] = [pv.t(f"pAV2{i}", 8192 + 4096 * i, [128, 2, 512], F32) for i in range(2)]
        T["pQB"] = [pv.t(f"pQB{i}", 4096 + 2048 * i, [128, 512], F32) for i in range(2)]
        T["pTo"] = [pv.t(f"pTo{i}", 8192 + 2048 * i, [128, 128], BF16) for i in range(4)]
        T["SQS"] = pv.t("SQSp", 8192, [128, 1024], F32)

        SM = {n: Sem(ctx.enter_context(nc.semaphore(n)), n) for n in SEM_NAMES}

        blk = ctx.enter_context(nc.Block())
        for role, starter in (("sync", blk.sync), ("gpsimd", blk.gpsimd),
                              ("vector", blk.vector), ("scalar", blk.scalar),
                              ("tensor", blk.tensor)):
            for s in SM.values():
                s.v = 0
            starter(lambda e, r=role: emit(r, e, T, dram, SM, nseq))

    return nc


def emit(role, E, T, dram, S, nseq):
    OT, XN, UC, MSK, IDN, SEL = (T["OT"], T["XN"], T["UC"],
                                 T["MSK"], T["IDN"], T["SEL"])
    BSBO, BSWOB, LOT = T["BSBO"], T["BSWOB"], T["LOT"]
    EPS1, BOHL, WOBHL, ONES2 = T["EPS1"], T["BOHL"], T["WOBHL"], T["ONES2"]
    SQD = T["SQD"]
    SQS, VTMP, LNT, ONAT, PTM, RCP = (T["SQS"], T["VTMP"], T["LNT"],
                                      T["ONAT"], T["PTM"], T["RCP"])
    XINA = T["XINA"]
    A_W0, A_W1, A_W2, XNT, KG, VG, QB, PT = (T["A_W0"], T["A_W1"], T["A_W2"],
                                             T["XNT"], T["KG"], T["VG"],
                                             T["QB"], T["PT"])
    WOUTT, RTRT, LEAF, XN2T, QT = T["WOUTT"], T["RTRT"], T["LEAF"], T["XN2T"], T["QT"]
    WO_SB = T["WO_SB"]
    SPE, SPL, LPT, RETT = T["SPE"], T["SPL"], T["LPT"], T["RETT"]
    OUT2, OUTF, OATT, X1 = T["OUT2"], T["OUTF"], T["OATT"], T["X1"]
    pS, pAV, pG, pGb, pTb, pTa = T["pS"], T["pAV"], T["pG"], T["pGb"], T["pTb"], T["pTa"]
    pS2, pAV2, pQB, pTo = T["pS2"], T["pAV2"], T["pQB"], T["pTo"]
    pS4, pF, pF2 = T["pS4"], T["pF"], T["pF2"]
    pBV = T["pBV"]
    xd, yd = dram["x"], dram["y"]

    def I(name, f):
        return f() if role == name else None

    def inc(ins, sem, k=1):
        if ins is not None:
            ins.then_inc(sem.h, k)
        sem.v += k

    _whw = {}

    def W(name, sem, val):
        if role == name and val > 0:
            if _whw.get(id(sem), -1) >= val:
                return
            _whw[id(sem)] = val
            E.wait_ge(sem.h, val)

    genk = [0]

    def gen_acquire():
        k = genk[0]
        genk[0] += 1
        bank = k % 2
        W("tensor", S["cGE0"] if bank == 0 else S["cGE1"], k // 2)
        return k, bank

    def gen_release(ins, bank):
        inc(ins, S["cGE0"] if bank == 0 else S["cGE1"])

    def ge_rec():
        return (S["cGE0"].v, S["cGE1"].v)

    def wait_ge_rec(name, rec):
        W(name, S["cGE0"], rec[0])
        W(name, S["cGE1"], rec[1])

    bvk = [0]
    BVS = ("cVG", "cTPg")

    def bv_acquire():
        k = bvk[0]
        bvk[0] += 1
        bank = k % 2
        W("tensor", S[BVS[bank]], k // 2)
        return bank

    def bv_release(ins, bank):
        inc(ins, S[BVS[bank]])

    f4k = [0]
    F4S = ("cTPf", "cTEf", "cTPo", "cTEo")
    f4_extra = [None] * 4

    def f4_acquire():
        k = f4k[0]
        f4k[0] += 1
        bank = k % 4
        W("tensor", S[F4S[bank]], k // 4)
        if f4_extra[bank] is not None:
            W("tensor", S[f4_extra[bank][0]], f4_extra[bank][1])
            f4_extra[bank] = None
        return bank

    def f4_release(ins, bank):
        inc(ins, S[F4S[bank]])

    def f4_rec():
        return tuple(S[nm].v for nm in F4S)

    def f4_wait(name, rec):
        for nm, vv in zip(F4S, rec):
            W(name, S[nm], vv)

    def f4_pairs(rec):
        return [(S[nm], vv) for nm, vv in zip(F4S, rec)]

    def lnt(col):
        return LNT[:, col:col + 1]

    epsap = EPS1[:, 0:1]
    oneap = EPS1[:, 1:2]

    def vchain(ins):
        inc(ins, S["cVC"])

    def vwait():
        W("vector", S["cVC"], S["cVC"].v)

    def schain(ins):
        inc(ins, S["cSC"])

    def swait():
        W("scalar", S["cSC"], S["cSC"].v)

    def ln_sums(src_ap, r, colbase, sems, extra_waits, extra_waits_s=None, dump=None):
        cR, cSq = sems[0], sems[1]
        for sm, val in extra_waits:
            W("vector", sm, val)
        W("vector", sems[3], sems[3].v - 3)
        i1 = I("vector", lambda: E.reduce_sum(out=lnt(colbase + 0 + r), in_=src_ap, axis=AXX))
        inc(i1, cR)
        for sm, val in (extra_waits_s if extra_waits_s is not None else extra_waits):
            W("scalar", sm, val)
        W("scalar", S["cE"], S["cE"].v)
        W("scalar", S["cS1"], S["cS1"].v)
        W("scalar", S["cS3"], S["cS3"].v)
        W("scalar", S["cS5"], S["cS5"].v)
        W("scalar", S["cS7"], S["cS7"].v)
        swait()
        dmp = SQD if dump is None else dump
        i1 = I("scalar", lambda: E.activation(out=dmp[:], in_=src_ap,
                                              func=AF.Square, accum_out=lnt(colbase + 4 + r)))
        inc(i1, cSq)

    def ln_sums_psum(pair_ap, r, colbase, sems, mm_val, vring):
        cR, cSq = sems[0], sems[1]
        W("vector", S["cGP"], mm_val)
        for sm, val in vring:
            W("vector", sm, val)
        W("vector", sems[3], sems[3].v - 3)
        i1 = I("vector", lambda: E.reduce_sum(out=lnt(colbase + 0 + r), in_=pair_ap, axis=AXX))
        inc(i1, cR)
        W("scalar", S["cGP"], mm_val)
        W("scalar", S["cE"], S["cE"].v)
        W("scalar", S["cS1"], S["cS1"].v)
        W("scalar", S["cS3"], S["cS3"].v)
        W("scalar", S["cS5"], S["cS5"].v)
        W("scalar", S["cS7"], S["cS7"].v)
        swait()
        i1 = I("scalar", lambda: E.activation(out=SQD[:], in_=pair_ap,
                                              func=AF.Square, accum_out=lnt(colbase + 4 + r)))
        inc(i1, cSq)

    def ln_finish(r, colbase, sems, negmu=False, war=None):
        cR, cSq, cV, cRs = sems
        W("vector", cSq, cSq.v)
        W("vector", cR, cR.v)
        if war is not None:
            W("vector", S[war[0]], war[1])
        vwait()
        i1 = I("vector", lambda: E.tensor_scalar(out=lnt(colbase + 8 + r), in0=lnt(colbase + 0 + r),
                                                 scalar1=(-1.0 if negmu else 1.0) / D,
                                                 scalar2=None, op0=ALU.mult))
        vchain(i1)
        i2 = I("vector", lambda: E.tensor_scalar(out=lnt(colbase + 20 + r), in0=lnt(colbase + 0 + r),
                                                 scalar1=-1.0 / (D * D), scalar2=lnt(colbase + 0 + r),
                                                 op0=ALU.mult, op1=ALU.mult))
        vchain(i2)
        vwait()
        i3 = I("vector", lambda: E.scalar_tensor_tensor(out=lnt(colbase + 12 + r), in0=lnt(colbase + 4 + r),
                                                        scalar=1.0 / D, in1=lnt(colbase + 20 + r),
                                                        op0=ALU.mult, op1=ALU.add))
        inc(i3, cV)
        W("scalar", cV, cV.v)
        i4 = I("scalar", lambda: E.activation(out=lnt(colbase + 16 + r), in_=lnt(colbase + 12 + r),
                                              func=AF.Ln, bias=epsap))
        schain(i4)
        swait()
        i5 = I("scalar", lambda: E.activation(out=lnt(colbase + 24 + r), in_=lnt(colbase + 16 + r),
                                              func=AF.Exp, scale=-0.5))
        inc(i5, cRs)
        if negmu:
            W("scalar", cRs, cRs.v)
            i6 = I("scalar", lambda: E.activation(out=lnt(colbase + 20 + r), in_=lnt(colbase + 8 + r),
                                                  func=AF.Identity, scale=lnt(colbase + 24 + r),
                                                  bias=0.0))
            schain(i6)

    for dst, src in ((UC, "u_c"), (MSK, "mask_c"), (IDN, "id_c"), (SEL, "sel_c"),
                     (ONES2, "ones2_c"), (EPS1, "epsone_c")):
        i1 = I("sync", lambda dst=dst, src=src: E.dma_start(out=dst[:], in_=dram[src]))
        inc(i1, S["dC"], 16)
    i1 = I("sync", lambda: E.dma_start(out=BSBO[:], in_=dram["bo"]))
    inc(i1, S["dC"], 16)
    i1 = I("sync", lambda: E.dma_start(out=BSWOB[:], in_=dram["wob"]))
    inc(i1, S["dC"], 16)
    W("scalar", S["dC"], 16 * 8)
    i1 = I("scalar", lambda: E.copy(out=BOHL[0:1, :], in_=BSBO[:]))
    i2 = I("scalar", lambda: E.copy(out=WOBHL[0:1, :], in_=BSWOB[:]))
    inc(i2, S["sInit"])
    W("vector", S["sInit"], 1)
    W("vector", S["dC"], 16 * 8)
    i1 = I("vector", lambda: E.tensor_sub(out=LOT[0:1, 0:1024], in0=BSBO[:], in1=BOHL[0:1, :]))
    i2 = I("vector", lambda: E.tensor_sub(out=LOT[0:1, 1024:2048], in0=BSWOB[:], in1=WOBHL[0:1, :]))
    inc(i2, S["sInit"])
    W("sync", S["sInit"], 2)
    i1 = I("sync", lambda: E.dma_start(out=BOHL[1:2, :], in_=LOT[0:1, 0:1024]))
    inc(i1, S["dC"], 16)
    i1 = I("sync", lambda: E.dma_start(out=WOBHL[1:2, :], in_=LOT[0:1, 1024:2048]))
    inc(i1, S["dC"], 16)
    W("gpsimd", S["dC"], 16 * 8)

    def emit_kv(grp, nb, tef_need, dw_need, f4=False, k_gen2=False,
                k_first=False, av_gate=0, cs_gates=None):
        def acq():
            if f4:
                return f4_acquire()
            return gen_acquire()[1]

        def rel(ins, bank):
            if f4:
                f4_release(ins, bank)
            else:
                gen_release(ins, bank)

        def pb(bank):
            return pF[bank] if f4 else pG[bank]
        def emit_v():
         if av_gate:
            W("vector", S["cAV"], av_gate)
         for ti, t in enumerate(range(4 * nb, 4 * nb + 4)):
            bank = acq()
            W("tensor", S[tef_need[ti][0]], tef_need[ti][1])
            W("tensor", S[dw_need[1][0]], dw_need[1][1])
            i1 = None
            for c in range(NCH):
                i1 = I("tensor", lambda c=c, t=t, grp=grp, bank=bank: E.matmul(
                    pb(bank)[:],
                    XNT[:, 4096 * (t // 4) + 512 * c + 128 * (t % 4):4096 * (t // 4) + 512 * c + 128 * (t % 4) + 128],
                    A_W2[:, 1024 * c + 512 * grp:1024 * c + 512 * grp + 512],
                    start=(c == 0), stop=(c == NCH - 1)))
            inc(i1, S["cGP"])
            W("vector", S["cGP"], S["cGP"].v)
            for hh in range(8):
                I("vector", lambda t=t, hh=hh, grp=grp, bank=bank: E.tensor_scalar(
                    out=VG[:, 528 * t + 66 * hh:528 * t + 66 * hh + 64],
                    in0=pb(bank)[:, 64 * hh:64 * hh + 64],
                    scalar1=UC[:, H * t + 8 * grp + hh:H * t + 8 * grp + hh + 1],
                    scalar2=None, op0=ALU.mult))
            i2 = I("vector", lambda t=t, grp=grp: E.tensor_copy(
                out=VG[:, 528 * t + 64:528 * t + 527:66],
                in_=UC[:, H * t + 8 * grp:H * t + 8 * grp + 8]))
            rel(i2, bank)
        def emit_k():
         for c2 in range(4):
            use_gen = k_gen2 and c2 < 2
            use_bv = k_gen2 and c2 >= 2
            if k_first and c2 == 0:
                for nm, vv in tef_need:
                    W("tensor", S[nm], vv)
            if k_first and cs_gates:
                W("vector", S["cS"], cs_gates[0] if c2 < 2 else cs_gates[1])
            if use_gen:
                k, bank = gen_acquire()
                dst = pG[bank]
            elif use_bv:
                bank = bv_acquire()
                dst = pBV[bank]
            else:
                bank = acq()
                dst = pb(bank)
            if c2 == 0:
                W("tensor", S[dw_need[0][0]], dw_need[0][1])
            i1 = None
            for c in range(NCH):
                i1 = I("tensor", lambda c=c, c2=c2, nb=nb, grp=grp, dst=dst: E.matmul(
                    dst[:],
                    A_W1[:, 1024 * c + 128 * (grp * 4 + c2):1024 * c + 128 * (grp * 4 + c2) + 128],
                    XNT[:, 4096 * nb + 512 * c:4096 * nb + 512 * c + 512],
                    start=(c == 0), stop=(c == NCH - 1)))
            inc(i1, S["cGP"])
            W("vector", S["cGP"], S["cGP"].v)
            i2 = I("vector", lambda c2=c2, nb=nb, dst=dst: E.tensor_copy(
                out=KG[:, N * c2 + 512 * nb:N * c2 + 512 * (nb + 1)], in_=dst[:]))
            if use_gen:
                gen_release(i2, bank)
            elif use_bv:
                bv_release(i2, bank)
            else:
                rel(i2, bank)
        if k_first:
            emit_k()
            emit_v()
        else:
            emit_v()
            emit_k()

    qbe_rec = {0: 0, 1: 0}

    def emit_qblk(grp, b, qb_rdy_blk, in_attention, tef_need=None, dw_need=0, f4=False):
        for c2 in range(4):
            if in_attention:
                bank = c2 % 2
                W("tensor", S["cE"], S["cE"].v)
                W("tensor", S["cQBE"], qbe_rec[bank])
                dst = pQB[bank]
            else:
                bank = f4_acquire() if f4 else gen_acquire()[1]
                if tef_need:
                    for nm, vv in tef_need:
                        W("tensor", S[nm], vv)
                if dw_need:
                    W("tensor", S[dw_need[0]], dw_need[1])
                dst = pF[bank] if f4 else pG[bank]
            i1 = None
            for c in range(NCH):
                i1 = I("tensor", lambda c=c, c2=c2, b=b, grp=grp, dst=dst: E.matmul(
                    dst[:],
                    A_W0[:, 1024 * c + 128 * (grp * 4 + c2):1024 * c + 128 * (grp * 4 + c2) + 128],
                    XNT[:, 4096 * b + 512 * c:4096 * b + 512 * c + 512],
                    start=(c == 0), stop=(c == NCH - 1)))
            inc(i1, S["cGP"])
            W("vector", S["cGP"], S["cGP"].v)
            i2 = I("vector", lambda c2=c2, b=b, dst=dst: E.tensor_copy(
                out=QB[:, b % 2, 512 * c2:512 * (c2 + 1)], in_=dst[:]))
            if in_attention:
                inc(i2, S["cQBE"])
                qbe_rec[bank] = S["cQBE"].v
                qb_rdy_blk[(grp, b, c2)] = [("cQBE", S["cQBE"].v)]
            else:
                if f4:
                    f4_release(i2, bank)
                    qb_rdy_blk[(grp, b, c2)] = [(nm, vv) for nm, vv in zip(F4S, f4_rec())]
                else:
                    gen_release(i2, bank)
                    qb_rdy_blk[(grp, b, c2)] = [("cGE0", S["cGE0"].v), ("cGE1", S["cGE1"].v)]

    ot_blk = [0]
    ot_prev = [None]
    xn2t_prev = [0]
    PREV = {}
    PRO = [None]

    def phi1a_prologue(s2, x1b_gate, dtf1_gate):
        base_LN, base_S1 = S["cLN"].v, S["cS1"].v
        base_V2, base_S2 = S["cV2"].v, S["cS2"].v
        base_G1 = S["cG1"].v
        base_dXA = (S["dXA0"].v, S["dXA1"].v, S["dXA2"].v, S["dXA3"].v)
        dta_tile = {}

        def p1_dma(g4):
            for tt in range(4):
                t = 4 * g4 + tt
                r = t % 4
                W("gpsimd", S["cLN"], base_LN + t - 3)
                W("gpsimd", S["cS1"], base_S1 + t - 3)
                W("gpsimd", S["cG1"], base_G1 + t - 3)
                if s2 == 0 and t == 0:
                    W("gpsimd", S["sInit"], 2)
                if x1b_gate is not None and t < 4:
                    W("gpsimd", S["cX1"], x1b_gate + 13 + r)
                i1 = I("gpsimd", lambda t=t, r=r: E.dma_start(
                    out=XINA[:, r, :], in_=xd[s2, 128 * t:128 * (t + 1), :]))
                inc(i1, S[f"dXA{r}"], 16)

        def p1_sums(g4):
            for tt in range(4):
                t = 4 * g4 + tt
                r = t % 4
                if t == 0:
                    W("scalar", S["cAV"], S["cAV"].v)
                    W("scalar", S["cM"], S["cM"].v)
                    W("scalar", S["cNRM"], S["cNRM"].v)
                    W("scalar", S["dTo0"], S["dTo0"].v)
                    W("scalar", S["dTo1"], S["dTo1"].v)
                ln_sums(XINA[:, r, :], r, 0,
                        (S["cG1"], S["cS1"], S["cV2"], S["cS2"]),
                        [(S[f"dXA{r}"], base_dXA[r] + 16 * (t // 4 + 1)),
                         (S["cV2"], base_V2 + t - 3)])

        def p1_fin(g4):
            for tt in range(4):
                t = 4 * g4 + tt
                r = t % 4
                ln_finish(r, 0, (S["cG1"], S["cS1"], S["cV2"], S["cS2"]),
                          negmu=True, war=("cLN", base_LN + t - 3))
                if s2 == 0 and t == 0:
                    W("scalar", S["dC"], 16 * 10)
                if dtf1_gate and t == 0:
                    W("scalar", S["dTf1"], dtf1_gate)
                if t >= 4:
                    W("scalar", S[dta_tile[t - 4][0]], dta_tile[t - 4][1])
                swait()
                i1 = I("scalar", lambda r=r: E.activation(
                    out=XN[:, r, :], in_=XINA[:, r, :], func=AF.Identity,
                    scale=lnt(24 + r), bias=lnt(20 + r)))
                inc(i1, S["cLN"])

        def p1_xnt(g4, ret3_gate):
            for tt in range(4):
                t = 4 * g4 + tt
                r = t % 4
                atd = f"dTa{t % 4}"
                W("sync", S["cLN"], base_LN + t + 1)
                if s2 == 0 and t == 0:
                    W("sync", S["dC"], 16 * 10)
                if ret3_gate and tt == 0:
                    W("sync", S["cGP"], ret3_gate)
                i1 = I("sync", lambda r=r, t=t: E.dma_start_transpose(
                    XNT[:].rearrange("p (nb c u) -> p nb c u", c=8, u=512)[
                        :, t // 4, :, 128 * (t % 4):128 * (t % 4) + 128],
                    XN[:, r, :].rearrange("p (c f) -> p c f", f=128)))
                inc(i1, S[atd], 16)
                dta_tile[t] = (atd, S[atd].v)

        return {"dma": p1_dma, "sums": p1_sums, "fin": p1_fin, "xnt": p1_xnt,
                "dta_tile": dta_tile}

    for s in range(nseq):
        for wdst, wsrc, gp, sem in ((A_W2, "wv", "wo3", "dWv"),
                                    (A_W1, "wk", "qt3", "dWk"),
                                    (A_W0, "wq", "o23", "dWq")):
            if s > 0:
                W("gpsimd", S["cGP"], PREV[gp])
            i1 = I("gpsimd", lambda wdst=wdst, wsrc=wsrc: E.dma_start(
                out=wdst[:].rearrange("p (c d) -> p c d", d=1024),
                in_=dram[wsrc].rearrange("(c p) d -> p c d", p=128)))
            inc(i1, S[sem], 16)
        dW_qkv = (("dWk", S["dWk"].v), ("dWv", S["dWv"].v))
        dW_q = ("dWq", S["dWq"].v)

        s_end_rec = [None, None]
        s_end_pp0 = [0]
        ot_rec = {}
        av_end_rec = [None, None]
        kv_rdy = [None, None]
        qblk_g1_gp = [0]
        qb_rdy_blk = {}
        otw_war = [0]

        kv_parts = {}
        if PRO[0] is not None:
            st = PRO[0]
            PRO[0] = None
        else:
            st = phi1a_prologue(s, None, 0)
            st["dma"](0)
            st["sums"](0)
            st["fin"](0)
            st["xnt"](0, None)
        st["dma"](1)
        st["sums"](1)
        dta_tile = st["dta_tile"]
        p1_dma, p1_sums, p1_fin, p1_xnt = st["dma"], st["sums"], st["fin"], st["xnt"]
        for g4 in range(4):
            dta_g4 = [dta_tile[4 * g4 + i] for i in range(4)]
            if g4 == 0:
                W("tensor", S["cNRM"], S["cNRM"].v)
                W("tensor", S["cS3"], S["cS3"].v)
                W("tensor", S["cS5"], S["cS5"].v)
            if g4 == 0 and s > 0:
                W("scalar", S["cGP"], PREV["o23"])
                W("scalar", S["cOF"], PREV["of"])
                W("vector", S["cGP"], PREV["o23"])
            if g4 == 3 and s > 0:
                W("vector", S["dY0"], 16 * (NT // 2) * s)
                W("vector", S["dY1"], 16 * (NT // 2) * s)
                W("vector", S["cS3"], PREV["s3"])
            nx = g4 + 1
            if nx < 4:
                p1_fin(nx)
                p1_xnt(nx, None)
            if g4 + 2 < 4:
                p1_dma(g4 + 2)
                p1_sums(g4 + 2)
            emit_kv(0, g4, dta_g4, dW_qkv, f4=True, k_gen2=True)
            kv_parts[g4] = [(nm, vv) for nm, vv in zip(F4S, f4_rec())]
            if g4 == 3:
                emit_qblk(0, 0, qb_rdy_blk, False, [dta_tile[i] for i in range(4)], dW_q, f4=True)

        xnt_tef = [dta_tile[12 + i] for i in range(4)]
        kv_rdy[0] = ([(nm, vv) for nm, vv in zip(F4S, f4_rec())]
                     + [("cGE0", S["cGE0"].v), ("cGE1", S["cGE1"].v)]
                     + [("cVG", S["cVG"].v), ("cTPg", S["cTPg"].v)])

        for grp in range(2):
            if grp == 1:
                W("scalar", S["cS"], s_end_rec[0])

                W("tensor", S["cNRM"], S["cNRM"].v)
                W("tensor", S["cE"], S["cE"].v)
                W("tensor", S["cQBE"], S["cQBE"].v)
                for nb in range(4):
                    emit_kv(1, nb, xnt_tef, dW_qkv, f4=True, k_gen2=True,
                            k_first=True, av_gate=av_end_rec[0],
                            cs_gates=(s_end_pp0[0], s_end_rec[0]))
                W("vector", S["cS"], s_end_rec[0])
                emit_qblk(1, 0, qb_rdy_blk, False, xnt_tef, dW_q, f4=True)
                kv_rdy[1] = ([(nm, vv) for nm, vv in zip(F4S, f4_rec())]
                             + [("cGE0", S["cGE0"].v), ("cGE1", S["cGE1"].v)]
                             + [("cVG", S["cVG"].v), ("cTPg", S["cTPg"].v)])

            if grp == 1:
                W("gpsimd", S["cGP"], S["cGP"].v)
                i1 = I("gpsimd", lambda: E.dma_start(
                    out=A_W1[:].rearrange("p (c d) -> p c d", d=1024),
                    in_=dram["wqfT"].rearrange("(c p) d -> p c d", p=128)))
                inc(i1, S["dW2"], 16)
                i1 = I("gpsimd", lambda: E.dma_start(
                    out=WO_SB[:].rearrange("p (c d) -> p c d", d=1024),
                    in_=dram["wo"].rearrange("(c p) d -> p c d", p=128)))
                inc(i1, S["dW2"], 16)
                dW_wo = S["dW2"].v

            NB2 = 4
            ptm_war = {0: 0, 1: 0, 2: 0, 3: 0}
            ptdw = {0: 0, 1: 0}
            mask_done = {}
            for b in range(NB2):
                for pp in range(2):
                    if grp == 0 and b == NB2 - 1 and pp == 1:
                        s_end_pp0[0] = S["cS"].v
                    pairs = (2 * pp, 2 * pp + 1)
                    nkj = 4 * b + 4
                    av_started = [[False, False], [False, False]]
                    exp_idx = [[-1] * nkj, [-1] * nkj]
                    av_idx = [[-1] * nkj, [-1] * nkj]
                    for kj in range(nkj):
                        qlo = max(0, kj - 4 * b)
                        for pi in range(2):
                            p = pairs[pi]
                            if kj == 0:
                                if pp == 0 and b == 0:
                                    for sm, val in kv_rdy[grp]:
                                        W("tensor", S[sm], val)
                                    W("tensor", S["cE"], S["cE"].v)
                                    if grp == 0:
                                        for nm in F4S:
                                            W("tensor", S[nm], S[nm].v)
                                for sm, val in qb_rdy_blk[(grp, b, p)]:
                                    W("tensor", S[sm], val)
                                if pi == 1 and (grp, b, 3) in qb_rdy_blk:
                                    for sm, val in qb_rdy_blk[(grp, b, 3)]:
                                        W("tensor", S[sm], val)
                            else:
                                W("tensor", S["cE"], exp_idx[pi][kj - 1] + 1)
                            i1 = None
                            for hh in range(2):
                                i1 = I("tensor", lambda hh=hh, p=p, kj=kj, pi=pi, qlo=qlo, b=b: E.matmul(
                                    pS4[:, 2 * pi + hh, 128 * qlo:512],
                                    KG[:, N * p + 128 * kj:N * p + 128 * (kj + 1)][64 * hh:64 * (hh + 1), :],
                                    QB[:, b % 2, 512 * p + 128 * qlo:512 * (p + 1)][64 * hh:64 * (hh + 1), :],
                                    start=True, stop=True))
                            inc(i1, S["cS"])
                        d = kj % 2
                        for pi in range(2):
                            gs = S["cE"].v
                            exp_idx[pi][kj] = gs
                            W("scalar", S["cS"], S["cS"].v - (1 - pi))
                            W("scalar", S["cAV"], ptdw[d])
                            i1 = I("scalar", lambda d=d, pi=pi, qlo=qlo: E.activation(
                                out=PT[:, 2 * d + pi, :].rearrange(
                                    "p (g c) -> p g c", g=2)[:, :, 128 * qlo:512],
                                in_=pS4[:, 2 * pi:2 * pi + 2, 128 * qlo:512],
                                func=AF.Exp, scale=SCALE))
                            inc(i1, S["cE"])
                        diag = (kj >= 4 * b)
                        if diag:
                            q = kj - 4 * b
                            for pi in range(2):
                                gs = exp_idx[pi][kj]
                                mslot = 2 * (kj % 2) + pi
                                W("gpsimd", S["cE"], gs + 1)
                                W("gpsimd", S["cAV"], ptm_war[mslot])
                                pslot = 2 * (kj % 2) + pi
                                i1 = None
                                for hh in range(2):
                                    i1 = I("gpsimd", lambda hh=hh, pslot=pslot, q=q, mslot=mslot: E.tensor_mul(
                                        out=PTM[:, mslot, 128 * hh:128 * (hh + 1)],
                                        in0=PT[:, pslot, 512 * hh + 128 * q:512 * hh + 128 * q + 128],
                                        in1=MSK[:]))
                                inc(i1, S["cM"])
                                mask_done[(pi, b, pp, kj)] = S["cM"].v
                        av_kjs = [kj - 1] if kj > 0 else []
                        if kj == nkj - 1:
                            av_kjs.append(kj)
                        for akj in av_kjs:
                            aqlo = max(0, akj - 4 * b)
                            adiag = (akj >= 4 * b)
                            for pi in range(2):
                                p = pairs[pi]
                                gs = exp_idx[pi][akj]
                                W("tensor", S["cE"], gs + 1)
                                if adiag:
                                    W("tensor", S["cM"], mask_done[(pi, b, pp, akj)])
                                av_last = None
                                for q in range(aqlo, 4):
                                    qh = q // 2
                                    if not av_started[pi][qh]:
                                        W("tensor", S["cNRM"], otw_war[0])
                                        av_started[pi][qh] = True
                                        st = True
                                    else:
                                        st = False
                                    for hh in range(2):
                                        is_diag_tile = adiag and (q == akj - 4 * b)
                                        if is_diag_tile:
                                            lhs = PTM[:, 2 * (akj % 2) + pi, 128 * hh:128 * (hh + 1)]
                                        else:
                                            lhs = PT[:, 2 * (akj % 2) + pi, 512 * hh + 128 * q:512 * hh + 128 * q + 128]
                                        av_last = I("tensor", lambda lhs=lhs, q=q, hh=hh, akj=akj, p=p, pi=pi, st=(st and hh == 0): E.matmul(
                                            pAV2[pi][:, q // 2, 130 * (q % 2) + 65 * hh:130 * (q % 2) + 65 * hh + 65],
                                            lhs,
                                            VG[:, 528 * akj + 66 * (2 * p + hh):528 * akj + 66 * (2 * p + hh) + 65],
                                            start=(st and hh == 0), stop=False, skip_group_check=True))
                                inc(av_last, S["cAV"])
                                av_idx[pi][akj] = S["cAV"].v
                                if adiag and akj == kj:
                                    pass
                                if adiag:
                                    ptm_war[2 * (akj % 2) + pi] = S["cAV"].v
                            ptdw[akj % 2] = S["cAV"].v
                    if pp == 1 and b + 1 < NB2:
                        emit_qblk(grp, b + 1, qb_rdy_blk, in_attention=True)
                    nrm_vals = [0, 0]
                    for pi in range(2):
                        p = pairs[pi]
                        W("vector", S["cAV"], av_idx[pi][nkj - 1])
                        W("vector", S["cNRM"], S["cNRM"].v)
                        if pi == 0 and ot_prev[0] is not None:
                            W("vector", S[ot_prev[0][0]], ot_prev[0][1])
                        iR = I("vector", lambda pi=pi: E.reciprocal(
                            out=RCP[:, 8 * pi:8 * pi + 8].rearrange("p (a b) -> p a b", a=2),
                            in_=pAV2[pi][:, :, 64:260:65]))
                        vchain(iR)
                        vwait()
                        last = None
                        for q in range(4):
                            for hh in range(2):
                                col = 8 * pi + 4 * (q // 2) + 2 * (q % 2) + hh
                                last = I("vector", lambda q=q, hh=hh, pi=pi, col=col: E.tensor_scalar(
                                    out=ONAT[:, 4 * pi + q, 64 * hh:64 * hh + 64],
                                    in0=pAV2[pi][:, q // 2, 130 * (q % 2) + 65 * hh:130 * (q % 2) + 65 * hh + 64],
                                    scalar1=RCP[:, col:col + 1],
                                    scalar2=None, op0=ALU.mult))
                        inc(last, S["cNRM"])
                        nrm_vals[pi] = S["cNRM"].v
                    otd = f"dTo{ot_blk[0] % 2}"
                    for pi in range(2):
                        p = pairs[pi]
                        gp = grp * 4 + p
                        W("sync", S["cNRM"], nrm_vals[pi])
                        i1 = I("sync", lambda pi=pi, b=b, gp=gp: E.dma_start_transpose(
                            OT[:, N * gp + 512 * b:N * gp + 512 * (b + 1)].rearrange(
                                "p (q f) -> p q f", f=128),
                            ONAT[:, 4 * pi:4 * pi + 4, :]))
                        inc(i1, S[otd], 16)
                    ot_prev[0] = (otd, S[otd].v)
                    ot_rec[(grp, b)] = (otd, S[otd].v)
                    ot_blk[0] += 1
                    otw_war[0] = S["cNRM"].v
            s_end_rec[grp] = S["cS"].v
            av_end_rec[grp] = S["cAV"].v
            if grp == 1:
                qblk_g1_gp[0] = S["cGP"].v


        ot_dto = (S["dTo0"].v, S["dTo1"].v)

        W("gpsimd", S["cGP"], qblk_g1_gp[0])
        for wdst, wsrc, dd in ((WOUTT, "woutT", 1024),
                               (RTRT, "rtrT", 256), (LEAF, "leaf", 1024)):
            i1 = I("gpsimd", lambda wdst=wdst, wsrc=wsrc, dd=dd: E.dma_start(
                out=wdst[:].rearrange("p (c d) -> p c d", d=dd),
                in_=dram[wsrc].rearrange("(c p) d -> p c d", p=128)))
            inc(i1, S["dW3"], 16)
        dW_ffn = S["dW3"].v

        base_dXB = (S["dXB0"].v, S["dXB1"].v, S["dXB2"].v, S["dXB3"].v)
        base_X1 = S["cX1"].v
        base_V4, base_S3, base_S4 = S["cV4"].v, S["cS3"].v, S["cS4"].v
        base_V6, base_S6, base_XN2 = S["cV6"].v, S["cS6"].v, S["cXN2"].v
        base_V8, base_S7, base_S8, base_OF = S["cV8"].v, S["cS7"].v, S["cS8"].v, S["cOF"].v
        base_dY = (S["dY0"].v, S["dY1"].v)
        base_VSt = S["cVSt"].v
        pipe = {"qt_gp": 0, "z_gp": 0, "sel_gp": 0, "ret_gp": 0, "o2_gp": 0,
                "xn2t": {}, "xn2t_tile": {}}

        def wo_stage(b4):
            gt0 = 4 * b4
            for tt in range(4):
                t = gt0 + tt
                r = t % 4
                W("gpsimd", S["cX1"], base_X1 + t - 3)
                if t < 4:
                    W("gpsimd", S["cG1"], S["cG1"].v)
                    W("gpsimd", S["cS1"], S["cS1"].v)
                    W("gpsimd", S["cLN"], S["cLN"].v)
                i1 = I("gpsimd", lambda t=t, r=r: E.dma_start(
                    out=XINA[:, r, :], in_=xd[s, 128 * t:128 * (t + 1), :]))
                inc(i1, S[f"dXB{r}"], 16)
            for tt in range(4):
                t = gt0 + tt
                ro = t % 4
                tile_banks = []
                for e in range(2):
                    bank = f4_acquire()
                    if tt == 0 and e == 0:
                        for g in range(2):
                            W("tensor", S[ot_rec[(g, b4)][0]], ot_rec[(g, b4)][1])
                        if b4 == 0:
                            W("tensor", S["dC"], 16 * 10)
                            W("tensor", S["cE"], S["cE"].v)
                            W("tensor", S["cQBE"], S["cQBE"].v)
                            W("tensor", S["cNRM"], S["cNRM"].v)
                    W("tensor", S["dW2"], dW_wo)
                    i1 = I("tensor", lambda e=e, bank=bank: E.matmul(
                        pF2[bank // 2][:, 512 * (bank % 2):512 * (bank % 2) + 512],
                        ONES2[:, :], BOHL[:, 512 * e:512 * (e + 1)],
                        start=True, stop=False))
                    for c in range(NCH):
                        i1 = I("tensor", lambda c=c, t=t, e=e, bank=bank: E.matmul(
                            pF2[bank // 2][:, 512 * (bank % 2):512 * (bank % 2) + 512],
                            OT[:, N * c + 128 * t:N * c + 128 * (t + 1)],
                            WO_SB[:, 1024 * c + 512 * e:1024 * c + 512 * (e + 1)],
                            start=False, stop=(c == NCH - 1)))
                    inc(i1, S["cGP"])
                    tile_banks.append(bank)
                assert tile_banks[0] % 2 == 0 and tile_banks[1] == tile_banks[0] + 1
                mmv = S["cGP"].v
                if b4 == 0 and tt == 0:
                    W("scalar", S["cNRM"], S["cNRM"].v)
                    W("scalar", S["cVG"], S["cVG"].v)
                    W("scalar", S["cTPg"], S["cTPg"].v)
                if b4 == 1 and tt == 0:
                    W("scalar", S["cAV"], S["cAV"].v)
                    W("scalar", S["cM"], S["cM"].v)
                    W("scalar", S["cNRM"], S["cNRM"].v)
                    W("scalar", S["dTo0"], ot_dto[0])
                    W("scalar", S["dTo1"], ot_dto[1])
                W("vector", S["cGP"], mmv)
                W("vector", S["cV4"], base_V4 + t - 3)
                W("vector", S["cS4"], S["cS4"].v - 3)
                iR = I("vector", lambda r4=t % 4, pr=tile_banks[0] // 2: E.reduce_sum(
                    out=lnt(28 + r4), in_=pF2[pr][:], axis=AXX))
                inc(iR, S["cG2"])
                for e in range(2):
                    bank = tile_banks[e]
                    W("vector", S["cGP"], mmv)
                    W("vector", S["cVSt"], base_VSt + t - 3)
                    i2 = I("vector", lambda e=e, ro=ro, bank=bank: E.tensor_copy(
                        out=OATT[:, ro, 512 * e:512 * (e + 1)],
                        in_=pF2[bank // 2][:, 512 * (bank % 2):512 * (bank % 2) + 512]))
                    f4_release(i2, bank)
                for sm, val in f4_pairs(f4_rec()):
                    W("scalar", sm, val)
                W("scalar", S["cE"], S["cE"].v)
                W("scalar", S["cS1"], S["cS1"].v)
                W("scalar", S["cS3"], S["cS3"].v)
                W("scalar", S["cS5"], S["cS5"].v)
                W("scalar", S["cS7"], S["cS7"].v)
                swait()
                sq_dmp = SQS if b4 == 0 else SQD
                iS = I("scalar", lambda ro=ro, r4=t % 4, sq_dmp=sq_dmp: E.activation(
                    out=sq_dmp[:], in_=OATT[:, ro, :], func=AF.Square,
                    accum_out=lnt(32 + r4)))
                inc(iS, S["cS3"])
            for tt in range(4):
                t = gt0 + tt
                ln_finish(t % 4, 28, (S["cG2"], S["cS3"], S["cV4"], S["cS4"]))
            for tt in range(4):
                t = gt0 + tt
                r = t % 4
                W("vector", S["cS4"], base_S4 + t + 1)
                W("vector", S[f"dXB{r}"], base_dXB[r] + 16 * (t // 4 + 1))
                W("vector", S["cX1"], S["cX1"].v)
                W("vector", S["cOF"], S["cOF"].v)
                i1 = I("vector", lambda r=r, r4=t % 4: E.tensor_scalar(
                    out=VTMP[:], in0=OATT[:, r, :], scalar1=lnt(36 + r4),
                    scalar2=lnt(52 + r4), op0=ALU.subtract, op1=ALU.mult))
                inc(i1, S["cVSt"])
                W("vector", S["cVSt"], S["cVSt"].v)
                i2 = I("vector", lambda t=t, r=r: E.tensor_add(
                    out=X1[:, t % 8, :], in0=XINA[:, r, :], in1=VTMP[:]))
                inc(i2, S["cX1"])
            for tt in range(4):
                t = gt0 + tt
                ln_sums(X1[:, t % 8, :], t % 4, 56,
                        (S["cG3"], S["cS5"], S["cV6"], S["cS6"]),
                        [(S["cX1"], base_X1 + t + 1), (S["cV6"], base_V6 + t - 3)],
                        dump=(SQS if b4 == 0 else None))
            xtd_prev = f"dTf{(b4 - 1) % 2}"
            for tt in range(4):
                t = gt0 + tt
                r = t % 4
                ln_finish(r, 56, (S["cG3"], S["cS5"], S["cV6"], S["cS6"]),
                          negmu=True, war=("cXN2", base_XN2 + t - 3))
                if t >= 4:
                    W("scalar", S[xtd_prev], pipe["xn2t"][b4 - 1][1])
                else:
                    W("scalar", S[dta_tile[12 + r][0]], dta_tile[12 + r][1])
                swait()
                i1 = I("scalar", lambda r=r, t=t, r4=t % 4: E.activation(
                    out=XN[:, r, :], in_=X1[:, t % 8, :], func=AF.Identity,
                    scale=lnt(80 + r4), bias=lnt(76 + r4)))
                inc(i1, S["cXN2"])
            xtd = f"dTf{b4 % 2}"
            for tt in range(4):
                t = gt0 + tt
                r = t % 4
                W("sync", S["cXN2"], base_XN2 + t + 1)
                i1 = I("sync", lambda r=r, tt=tt, b4=b4: E.dma_start_transpose(
                    XN2T[:, b4 % 2, :].rearrange("p (c q f) -> p c q f", q=4, f=128)[:, :, tt, :],
                    XN[:, r, :].rearrange("p (c f) -> p c f", f=128)))
                inc(i1, S[xtd], 16)
            pipe["xn2t"][b4] = (xtd, S[xtd].v)

        def ffn_stage(b4, hook=None):
            gt0 = 4 * b4
            xslot = b4 % 2
            for e in range(NCH):
                bank = f4_acquire()
                W("tensor", S[pipe["xn2t"][b4][0]], pipe["xn2t"][b4][1])
                W("tensor", S["dW2"], dW_wo)
                W("tensor", S["dW3"], dW_ffn)
                i1 = None
                for c in range(NCH):
                    i1 = I("tensor", lambda c=c, e=e, bank=bank, xslot=xslot: E.matmul(
                        pF[bank][:],
                        A_W1[:, 1024 * c + 128 * e:1024 * c + 128 * e + 128],
                        XN2T[:, xslot, 512 * c:512 * (c + 1)],
                        start=(c == 0), stop=(c == NCH - 1)))
                inc(i1, S["cGP"])
                W("vector", S["cGP"], S["cGP"].v)
                W("vector", S["cGP"], pipe["z_gp"])
                i2 = I("vector", lambda e=e, bank=bank: E.tensor_copy(
                    out=QT[:, 512 * e:512 * (e + 1)], in_=pF[bank][:]))
                f4_release(i2, bank)
            pipe["qt_gp"] = S["cGP"].v
            qt_rdy = f4_rec()

            for nch2 in range(2):
                bank = f4_acquire()
                f4_wait("tensor", qt_rdy)
                i1 = None
                for c in range(NCH):
                    i1 = I("tensor", lambda c=c, nch2=nch2, bank=bank: E.matmul(
                        pF[bank][:],
                        RTRT[:, 256 * c + 128 * nch2:256 * c + 128 * (nch2 + 1)],
                        QT[:, 512 * c:512 * (c + 1)],
                        start=(c == 0), stop=(c == NCH - 1)))
                inc(i1, S["cGP"])
                zv = S["cGP"].v
                W("scalar", S["cGP"], zv)
                W("scalar", S["cHL"], S["cHL"].v)
                if nch2 == 0:
                    W("scalar", S["cGP"], pipe["sel_gp"])
                i1 = I("scalar", lambda bank=bank: E.activation(
                    out=SPE[:, 0, :], in_=pF[bank][:], func=AF.Exp, scale=1.0))
                schain(i1)
                swait()
                i3 = I("scalar", lambda nch2=nch2: E.activation(
                    out=SPL[:, nch2, :], in_=SPE[:, 0, :], func=AF.Ln, bias=oneap))
                inc(i3, S["cHL"])
                W("vector", S["cGP"], zv)
                W("vector", S["cHL"], S["cHL"].v)
                if nch2 == 0:
                    W("vector", S["cGP"], pipe["sel_gp"])
                iv = I("vector", lambda nch2=nch2, bank=bank: E.tensor_sub(
                    out=SPL[:, nch2 + 2, :], in0=SPL[:, nch2, :], in1=pF[bank][:]))
                f4_release(iv, bank)
            pipe["z_gp"] = S["cGP"].v
            z_rdy = f4_rec()

            for mh in range(2):
                bank = f4_acquire()
                f4_wait("tensor", z_rdy)
                i1 = None
                for c in range(4):
                    i1 = I("tensor", lambda c=c, mh=mh, bank=bank: E.matmul(
                        pF[bank][:],
                        SEL[:, 256 * c + 128 * mh:256 * c + 128 * (mh + 1)],
                        SPL[:, c, :],
                        start=(c == 0), stop=(c == 3)))
                inc(i1, S["cGP"])
                W("scalar", S["cGP"], S["cGP"].v)
                if mh == 0:
                    W("scalar", S["cGP"], pipe["ret_gp"])
                i2 = I("scalar", lambda mh=mh, bank=bank: E.activation(
                    out=LPT[:, 512 * mh:512 * (mh + 1)], in_=pF[bank][:], func=AF.Exp))
                f4_release(i2, bank)
            pipe["sel_gp"] = S["cGP"].v
            lp_rdy = f4_rec()

            for dc in range(NCH):
                bank = f4_acquire()
                f4_wait("tensor", lp_rdy)
                i1 = None
                for mh in range(2):
                    i1 = I("tensor", lambda mh=mh, dc=dc, bank=bank: E.matmul(
                        pF[bank][:],
                        LEAF[:, 1024 * mh + 128 * dc:1024 * mh + 128 * dc + 128],
                        LPT[:, 512 * mh:512 * (mh + 1)],
                        start=(mh == 0), stop=(mh == 1)))
                inc(i1, S["cGP"])
                W("vector", S["cGP"], S["cGP"].v)
                if dc == 0:
                    W("vector", S["cGP"], pipe["o2_gp"])
                i2 = I("vector", lambda dc=dc, bank=bank: E.tensor_copy(
                    out=RETT[:, 512 * dc:512 * (dc + 1)], in_=pF[bank][:]))
                f4_release(i2, bank)
            pipe["ret_gp"] = S["cGP"].v
            ret_rdy = f4_rec()
            if hook is not None:
                hook()

            for tt in range(4):
                t = gt0 + tt
                ro = t % 4
                tile_banks = []
                for e in range(2):
                    bank = f4_acquire()
                    f4_wait("tensor", ret_rdy)
                    i1 = I("tensor", lambda e=e, bank=bank: E.matmul(
                        pF2[bank // 2][:, 512 * (bank % 2):512 * (bank % 2) + 512],
                        ONES2[:, :], WOBHL[:, 512 * e:512 * (e + 1)],
                        start=True, stop=False))
                    for c in range(NCH):
                        i1 = I("tensor", lambda c=c, tt=tt, e=e, bank=bank: E.matmul(
                            pF2[bank // 2][:, 512 * (bank % 2):512 * (bank % 2) + 512],
                            RETT[:, 512 * c + 128 * tt:512 * c + 128 * (tt + 1)],
                            WOUTT[:, 1024 * c + 512 * e:1024 * c + 512 * (e + 1)],
                            start=False, stop=(c == NCH - 1)))
                    inc(i1, S["cGP"])
                    tile_banks.append(bank)
                assert tile_banks[0] % 2 == 0 and tile_banks[1] == tile_banks[0] + 1
                mmv = S["cGP"].v
                for e in range(2):
                    bank = tile_banks[e]
                    W("vector", S["cGP"], mmv)
                    W("vector", S["cOF"], base_OF + t - 3)
                    W("vector", S["cS7"], base_S7 + t - 3)
                    i2 = I("vector", lambda e=e, ro=ro, bank=bank: E.tensor_copy(
                        out=OUT2[:, ro, 512 * e:512 * (e + 1)],
                        in_=pF2[bank // 2][:, 512 * (bank % 2):512 * (bank % 2) + 512]))
                    f4_release(i2, bank)
                ln_sums(OUT2[:, t % 4, :], t % 4, 84,
                        (S["cG4"], S["cS7"], S["cV8"], S["cS8"]),
                        f4_pairs(f4_rec()) + [(S["cV8"], base_V8 + t - 3)])
            for tt in range(4):
                t = gt0 + tt
                ln_finish(t % 4, 84, (S["cG4"], S["cS7"], S["cV8"], S["cS8"]))
            for tt in range(4):
                t = gt0 + tt
                r = t % 2
                W("vector", S["cS8"], base_S8 + t + 1)
                W("vector", S["dY0" if r == 0 else "dY1"], base_dY[r] + 16 * (t // 2))
                W("vector", S["cX1"], S["cX1"].v)
                W("vector", S["cOF"], S["cOF"].v)
                i1 = I("vector", lambda ro=t % 4, r4=t % 4: E.tensor_scalar(
                    out=VTMP[:], in0=OUT2[:, ro, :], scalar1=lnt(92 + r4),
                    scalar2=lnt(108 + r4), op0=ALU.subtract, op1=ALU.mult))
                vchain(i1)
                vwait()
                i2 = I("vector", lambda t=t, r=r: E.tensor_add(
                    out=OUTF[:, r, :], in0=X1[:, t % 8, :], in1=VTMP[:]))
                inc(i2, S["cOF"])
                W("sync", S["cOF"], S["cOF"].v)
                i3 = I("sync", lambda t=t, r=r: E.dma_start(
                    out=yd[s, 128 * t:128 * (t + 1), :], in_=OUTF[:, r, :]))
                inc(i3, S["dY0" if r == 0 else "dY1"], 16)
            pipe["o2_gp"] = S["cGP"].v

        wo_stage(0)
        wo_stage(1)
        ffn_stage(0)
        wo_stage(2)
        ffn_stage(1)
        wo_stage(3)
        wo3_gp = S["cGP"].v
        if s + 1 < nseq:
            PRO[0] = phi1a_prologue(s + 1, base_X1, S["dTf1"].v)
            PRO[0]["dma"](0)
        ffn_stage(2)
        if s + 1 < nseq:
            PRO[0]["sums"](0)

        def f3_hook():
            if s + 1 < nseq:
                PRO[0]["fin"](0)
                PRO[0]["xnt"](0, S["cGP"].v)
        ffn_stage(3, hook=f3_hook)
        xn2t_prev[0] = S["dTf1"].v
        PREV.update(wo3=wo3_gp, qt3=pipe["qt_gp"], ret3=pipe["ret_gp"],
                    o23=pipe["o2_gp"], x1b=base_X1, s3=S["cS3"].v, of=S["cOF"].v)

    W("sync", S["dY0"], 16 * (NT // 2) * nseq)
    W("sync", S["dY1"], 16 * (NT // 2) * nseq)


_NC_CACHE = {}


def _prep_weights(inputs):
    f = np.float32
    routers_pad = np.zeros((256, D), dtype=f)
    routers_pad[:255] = np.asarray(inputs["L0_ffn_routers"], f)
    return {
        "wq": np.ascontiguousarray(np.asarray(inputs["L0_attn_Wq"], f)),
        "wk": np.ascontiguousarray(np.asarray(inputs["L0_attn_Wk"], f)),
        "wv": np.ascontiguousarray(np.asarray(inputs["L0_attn_Wv"], f)),
        "wo": np.ascontiguousarray(np.asarray(inputs["L0_attn_Wo"], f)),
        "bo": np.ascontiguousarray(np.asarray(inputs["L0_attn_bo"], f).reshape(1, D)),
        "wqfT": np.ascontiguousarray(np.asarray(inputs["L0_ffn_Wq"], f).T),
        "rtrT": np.ascontiguousarray(routers_pad.T),
        "leaf": np.ascontiguousarray(np.asarray(inputs["L0_ffn_leaf_values"], f)),
        "woutT": np.ascontiguousarray(np.asarray(inputs["L0_ffn_Wout_w"], f).T),
        "wob": np.ascontiguousarray(np.asarray(inputs["L0_ffn_Wout_b"], f).reshape(1, D)),
    }


def kernel(**inputs):
    if "nc" not in _NC_CACHE:
        _NC_CACHE["nc"] = build_nc()
    nc = _NC_CACHE["nc"]
    x = np.asarray(inputs["L0_attn_x"], np.float32)
    w = _prep_weights(inputs)
    in_maps = []
    for c in range(NCORES):
        m = dict(w)
        m["x"] = np.ascontiguousarray(x[c * NSEQ:(c + 1) * NSEQ])
        in_maps.append(m)
    res = run_bass_kernel_spmd(nc, in_maps, core_ids=list(range(NCORES)))
    y = np.concatenate([r["y"] for r in res.results], axis=0)
    return y.reshape(B, N, D)


if __name__ == "__main__":
    nc = build_nc()
    print("built ok, arena bytes:", ARENA_SZ)
